# version 09
# baseline (speedup 1.0000x reference)
import numpy as np
import ml_dtypes
from contextlib import ExitStack

import concourse.bass as bass
import concourse.mybir as mybir
from concourse.bass_utils import run_bass_kernel_spmd

F32 = mybir.dt.float32
BF16 = mybir.dt.bfloat16
I32 = mybir.dt.int32
U32 = mybir.dt.uint32
ALU = mybir.AluOpType
AF = mybir.ActivationFunctionType
AXX = mybir.AxisListType.X

CFG_SMALL = dict(S=1024, D=1024, H=8, HD=128, INTER=1024, RANK=16,
                 P4=512, P6=256, P8=256, NC=8, debug=True)
CFG_FULL = dict(S=8192, D=4096, H=32, HD=128, INTER=14336, RANK=16,
                P4=2048, P6=1024, P8=1024, NC=8, debug=False)

LN2PI_C1 = 6.28125
LN2PI_C2 = float(np.float32(2 * np.pi - 6.28125))
LN2PI_C3 = float(2 * np.pi - 6.28125 - float(np.float32(2 * np.pi - 6.28125)))
SOFTMAX_SHIFT = 28.0


class Chain:
    def __init__(self, stack, nc, name):
        self.sem = stack.enter_context(nc.semaphore(name))
        self.n = 0

    def inc(self, inst, k=1):
        inst.then_inc(self.sem, k)
        self.n += k
        return self.n

    def dma(self, inst):
        return self.inc(inst, 16)


class DQ:

    def __init__(self, stack, nc, name, k=4):
        self.sems = [stack.enter_context(nc.semaphore(f"{name}{i}")) for i in range(k)]
        self.cnt = [0] * k
        self.i = 0
        self.k = k

    def issue(self, eng, inst_fn):
        s = self.i % self.k
        if self.cnt[s]:
            eng.wait_ge(self.sems[s], 16 * self.cnt[s])
        inst = inst_fn()
        inst.then_inc(self.sems[s], 16)
        self.cnt[s] += 1
        self.i += 1
        return (self.sems[s], 16 * self.cnt[s])

    def wait_all(self, eng):
        for s in range(self.k):
            if self.cnt[s]:
                eng.wait_ge(self.sems[s], 16 * self.cnt[s])


def regions(cfg, ncols):
    out = []
    for (st, w, maxq) in [(0, cfg["P4"], 7.0),
                          (cfg["P4"], cfg["P6"], 31.0),
                          (cfg["P4"] + cfg["P6"], cfg["P8"], 127.0)]:
        if st < ncols:
            out.append((st, min(w, ncols - st), maxq))
    return out


def emit_quant(v, x_f32, out_bf, gmax, qf, cfg, ncols):
    for (st, w, maxq) in regions(cfg, ncols):
        g0, gw = st // 32, w // 32
        v.tensor_reduce(gmax[:, g0:g0 + gw].unsqueeze(2),
                        x_f32[:, st:st + w].rearrange("p (g r) -> p g r", r=32),
                        AXX, ALU.max, apply_absolute_value=True)
        v.drain()
        v.tensor_scalar(gmax[:, g0:g0 + gw], gmax[:, g0:g0 + gw],
                        float(1.0 / maxq), 1e-12, ALU.mult, ALU.max)
        v.drain()
    GW = ncols // 32
    for (st, w, maxq) in regions(cfg, ncols):
        g0, gw = st // 32, w // 32
        v.reciprocal(gmax[:, GW + g0:GW + g0 + gw], gmax[:, g0:g0 + gw])
        v.drain()
        v.tensor_tensor(qf[:, st:st + w].rearrange("p (g r) -> p g r", r=32),
                        x_f32[:, st:st + w].rearrange("p (g r) -> p g r", r=32),
                        gmax[:, GW + g0:GW + g0 + gw].unsqueeze(2).broadcast_to([128, gw, 32]),
                        ALU.mult)
        v.drain()
    v.tensor_copy(qf[:, 0:ncols].bitcast(I32), qf[:, 0:ncols])
    v.drain()
    v.tensor_copy(qf[:, 0:ncols], qf[:, 0:ncols].bitcast(I32))
    v.drain()
    for (st, w, maxq) in regions(cfg, ncols):
        g0, gw = st // 32, w // 32
        v.tensor_tensor(out_bf[:, st:st + w].rearrange("p (g r) -> p g r", r=32),
                        qf[:, st:st + w].rearrange("p (g r) -> p g r", r=32),
                        gmax[:, g0:g0 + gw].unsqueeze(2).broadcast_to([128, gw, 32]),
                        ALU.mult)
        v.drain()


def build_nc(cfg):
    S, D, H, HD = cfg["S"], cfg["D"], cfg["H"], cfg["HD"]
    INTER, RANK, NC = cfg["INTER"], cfg["RANK"], cfg["NC"]
    TPC = S // NC
    HPC = H // NC
    NFC = D // 128
    NTT = TPC // 128
    NQT = S // 512
    RB = INTER // 128
    RB_PC = RB // NC
    NBLK = INTER // 256
    NOC = NFC
    OPC = D // NC
    NOCL = OPC // 128
    dbg = cfg["debug"]
    upto = cfg.get("upto", "all")

    nc = bass.Bass("TRN2", target_bir_lowering=False, debug=False)

    x_in = nc.dram_tensor("x", [TPC, D], F32, kind="ExternalInput").ap()
    wq_in = nc.dram_tensor("wq", [HPC * HD, D], F32, kind="ExternalInput").ap()
    wk_in = nc.dram_tensor("wk", [HPC * HD, D], F32, kind="ExternalInput").ap()
    wv_in = nc.dram_tensor("wv", [HPC * HD, D], F32, kind="ExternalInput").ap()
    wo_in = nc.dram_tensor("wo", [OPC, D], F32, kind="ExternalInput").ap()
    c1_in = nc.dram_tensor("c1", [RANK, D], F32, kind="ExternalInput").ap()
    c2_in = nc.dram_tensor("c2", [INTER, RANK], F32, kind="ExternalInput").ap()
    w1_in = nc.dram_tensor("w1", [RB_PC * 128, D], F32, kind="ExternalInput").ap()
    w2_in = nc.dram_tensor("w2", [RB_PC * 128, D], F32, kind="ExternalInput").ap()
    qp_in = nc.dram_tensor("qparam", [128, 2], F32, kind="ExternalInput").ap()
    y_out = nc.dram_tensor("y", [TPC, D], F32, kind="ExternalOutput").ap()

    d_hqT_in = nc.dram_tensor("d_hqT_in", [D, TPC], BF16)
    d_hqT = nc.dram_tensor("d_hqT", [NC * D, TPC], BF16)
    d_qwoT_in = nc.dram_tensor("d_qwoT_in", [NOCL * 128, NFC * 128], BF16)
    d_qwoT = nc.dram_tensor("d_qwoT", [NC * NOCL * 128, NFC * 128], BF16)
    d_qo_in = nc.dram_tensor("d_qo_in", [HPC * S, HD], BF16)
    d_qo = nc.dram_tensor("d_qo", [H * S, HD], BF16)
    d_w1t_in = nc.dram_tensor("d_w1t_in", [RB_PC * 128, NFC * 128], BF16)
    d_w1t = nc.dram_tensor("d_w1t", [RB * 128, NFC * 128], BF16)
    d_w2_in = nc.dram_tensor("d_w2_in", [RB_PC * 128, D], BF16)
    d_w2 = nc.dram_tensor("d_w2", [RB * 128, D], BF16)
    d_aproj = nc.dram_tensor("d_aproj", [TPC, D], BF16)
    d_qostage = nc.dram_tensor("d_qostage", [(D // 128) * (S // NC), 128], BF16)
    d_xfT = nc.dram_tensor("d_xfT", [D, TPC], BF16)

    dbg_outs = {}

    def dout(name, shape, dt=F32):
        if dbg:
            dbg_outs[name] = nc.dram_tensor(name, shape, dt, kind="ExternalOutput").ap()
            return dbg_outs[name]
        return None

    o_hq = dout("o_hq", [TPC, D], BF16)
    o_hqT = dout("o_hqT", [NC * D, TPC], BF16)
    dout("o_qTr", [128, S], BF16)
    dout("o_kTr", [128, S], BF16)
    dout("o_vtm", [128, S], BF16)
    dout("o_qo", [H * S, HD], BF16)
    dout("o_idx", [128, (TPC // 128) * (INTER // 256)], F32)
    dout("o_ffn", [TPC, D], BF16)
    dout("o_ht", [128, (INTER // 128) * min(512, TPC)], BF16)
    dout("o_aproj", [TPC, D], BF16)
    dout("o_xfT", [D, TPC], BF16)

    stack = ExitStack()
    with stack:
        sb = lambda name, shape, dt: stack.enter_context(nc.sbuf_tensor(name, shape, dt))
        c_id_bf = sb("c_id_bf", [128, 128], BF16)
        c_id_f = sb("c_id_f", [128, 128], F32)
        c_ones_bf = sb("c_ones_bf", [128, 1], BF16)
        c_ones_row = sb("c_ones_row", [1, 128], BF16)
        c_iota_lo = sb("c_iota_lo", [1, 128], BF16)
        c_iota_hi = sb("c_iota_hi", [1, 128], BF16)
        c_negC = sb("c_negC", [128, 1], F32)

        stage = sb("stage", [128, D], BF16)
        sc_a = sb("sc_a", [128, D], F32)
        sc_g = sb("sc_g", [128, 2 * (D // 32)], F32)
        sc_bf = sb("sc_bf", [128, D], BF16)
        sc_st = sb("sc_st", [128, 16], F32)
        qp_inv = sb("qp_inv", [128, 2], F32)

        stk2 = ExitStack()
        sb2 = lambda name, shape, dt: stk2.enter_context(nc.sbuf_tensor(name, shape, dt))
        c_masks = sb2("c_masks", [128, 4 * 512], BF16)
        c_cos = sb2("c_cos", [128, S], BF16)
        c_ssin = sb2("c_ssin", [128, S], BF16)
        sc_b = sb2("sc_b", [128, D], F32)
        sc_bf2 = sb2("sc_bf2", [128, D], BF16)

        prestk = ExitStack()
        ps_t0h = prestk.enter_context(nc.psum_tensor("ps_t0", [128, 128], BF16))
        ps_t1h = prestk.enter_context(nc.psum_tensor("ps_t1", [128, 128], BF16))
        ps_t0 = ps_t0h[:]
        ps_t1 = ps_t1h[:]

        ch = {n: Chain(stack, nc, n) for n in
              ["v", "a", "t", "g", "cc", "p1", "p2", "p3", "p4"]}
        dq_di = DQ(stack, nc, "di", 8)
        dq_dg = DQ(stack, nc, "dg", 8)

        def bar(blk, engines=("sync", "gpsimd", "vector", "scalar", "tensor")):
            emap = {"sync": (blk.sync, None), "gpsimd": (blk.gpsimd, ch["g"]),
                    "vector": (blk.vector, ch["v"]), "scalar": (blk.scalar, ch["a"]),
                    "tensor": (blk.tensor, ch["t"])}
            for e in engines:
                dec, c = emap[e]
                if c is not None:
                    @dec
                    def _(eng, c=c):
                        c.inc(eng.drain())
            counts = [(ch[k].sem, ch[k].n) for k in ["v", "a", "t", "g", "cc"]]
            counts += [(dq_di.sems[s], 16 * dq_di.cnt[s]) for s in range(dq_di.k)]
            counts += [(dq_dg.sems[s], 16 * dq_dg.cnt[s]) for s in range(dq_dg.k)]
            for e in engines:
                dec, _c = emap[e]
                @dec
                def _(eng, counts=counts):
                    for sem, n in counts:
                        if n:
                            eng.wait_ge(sem, n)

        with nc.Block() as blk:
            @blk.gpsimd
            def _(g):
                g.iota(sc_a[:, 0:128], pattern=[[1, 128]], base=0, channel_multiplier=0,
                       allow_small_or_imprecise_dtypes=True)
                g.iota(sc_a[:, 128:129], pattern=[[0, 1]], base=0, channel_multiplier=1,
                       allow_small_or_imprecise_dtypes=True)
                g.iota(sc_b[:, 0:512], pattern=[[1, 512]], base=0, channel_multiplier=-1,
                       allow_small_or_imprecise_dtypes=True)
                g.iota(sc_st[:, 0:1], pattern=[[0, 1]], base=0, channel_multiplier=1,
                       allow_small_or_imprecise_dtypes=True)
                dq_dg.issue(g, lambda: g.dma_start(out=qp_inv[:], in_=qp_in[:, :]))
                ch["g"].inc(g.drain())

            @blk.vector
            def _(v):
                v.wait_ge(ch["g"].sem, ch["g"].n)
                dq_dg.wait_all(v)
                v.tensor_scalar(c_id_f[:], sc_a[:, 0:128], sc_a[:, 128:129], None, ALU.is_equal)
                v.drain()
                v.tensor_copy(c_id_bf[:], c_id_f[:])
                v.memset(c_ones_bf[:], 1.0)
                v.memset(c_negC[:], -SOFTMAX_SHIFT)
                v.memset(c_ones_row[:], 1.0)
                for j in range(4):
                    v.tensor_scalar(c_masks[:, j * 512:(j + 1) * 512], sc_b[:, 0:512],
                                    float(j * 128), None, ALU.is_ge)
                v.drain()
                v.tensor_copy(c_iota_lo[:], sc_a[0:1, 0:128])
                v.drain()
                v.tensor_scalar(c_iota_hi[:], sc_a[0:1, 0:128], 128.0, None, ALU.add)
                v.drain()
                v.tensor_scalar(sc_st[:, 1:2], sc_st[:, 0:1], 64.0, -64.0, ALU.is_ge, ALU.mult)
                v.drain()
                v.tensor_tensor(sc_st[:, 2:3], sc_st[:, 0:1], sc_st[:, 1:2], ALU.add)
                v.drain()
                ch["v"].inc(v.drain())

            @blk.scalar
            def _(s):
                s.wait_ge(ch["v"].sem, ch["v"].n)
                s.activation(out=sc_st[:, 3:4], in_=sc_st[:, 2:3], func=AF.Exp,
                             bias=0.0, scale=float(-np.log(10000.0) / 64.0))
                ch["a"].inc(s.drain())
            bar(blk)

            w_rp = min(S, D)
            for rp in range(S // w_rp):
                lo, hi = rp * w_rp, (rp + 1) * w_rp

                @blk.gpsimd
                def _(g, lo=lo):
                    g.wait_ge(ch["v"].sem, ch["v"].n)
                    g.wait_ge(ch["a"].sem, ch["a"].n)
                    g.iota(sc_a[:, 0:w_rp], pattern=[[1, w_rp]], base=lo,
                           channel_multiplier=0, allow_small_or_imprecise_dtypes=True)
                    ch["g"].inc(g.drain())

                @blk.vector
                def _(v):
                    v.wait_ge(ch["g"].sem, ch["g"].n)
                    v.tensor_scalar(sc_b[:, 0:w_rp], sc_a[:, 0:w_rp], sc_st[:, 3:4], None, ALU.mult)
                    v.drain()
                    v.tensor_scalar(sc_a[:, 0:w_rp], sc_b[:, 0:w_rp],
                                    float(1.0 / (2 * np.pi)), None, ALU.mult)
                    v.drain()
                    v.tensor_copy(sc_a[:, 0:w_rp].bitcast(I32), sc_a[:, 0:w_rp])
                    v.drain()
                    v.tensor_copy(sc_a[:, 0:w_rp], sc_a[:, 0:w_rp].bitcast(I32))
                    v.drain()
                    for cc_ in (LN2PI_C1, LN2PI_C2, LN2PI_C3):
                        v.scalar_tensor_tensor(sc_b[:, 0:w_rp], sc_a[:, 0:w_rp],
                                               float(-cc_), sc_b[:, 0:w_rp],
                                               ALU.mult, ALU.add)
                        v.drain()
                    v.tensor_scalar(sc_a[:, 0:w_rp], sc_b[:, 0:w_rp], float(np.pi),
                                    float(-2 * np.pi), ALU.is_gt, ALU.mult)
                    v.drain()
                    v.tensor_tensor(sc_b[:, 0:w_rp], sc_b[:, 0:w_rp], sc_a[:, 0:w_rp], ALU.add)
                    v.drain()
                    v.tensor_scalar(sc_a[:, 0:w_rp], sc_b[:, 0:w_rp], float(-np.pi),
                                    float(2 * np.pi), ALU.is_lt, ALU.mult)
                    v.drain()
                    v.tensor_tensor(sc_b[:, 0:w_rp], sc_b[:, 0:w_rp], sc_a[:, 0:w_rp], ALU.add)
                    v.drain()
                    ch["v"].inc(v.drain())

                @blk.scalar
                def _(s):
                    s.wait_ge(ch["v"].sem, ch["v"].n)
                    s.activation(out=sc_a[:, 0:w_rp], in_=sc_b[:, 0:w_rp], func=AF.Sin)
                    ch["a"].inc(s.drain())

                @blk.vector
                def _(v, lo=lo, hi=hi):
                    v.wait_ge(ch["a"].sem, ch["a"].n)
                    v.tensor_scalar(c_ssin[0:64, lo:hi], sc_a[0:64, 0:w_rp], -1.0, None, ALU.mult)
                    v.drain()
                    v.tensor_copy(c_ssin[64:128, lo:hi], sc_a[64:128, 0:w_rp])
                    v.drain()
                    v.tensor_scalar(sc_b[:, 0:w_rp], sc_b[:, 0:w_rp], float(np.pi / 2),
                                    None, ALU.add)
                    v.drain()
                    v.tensor_scalar(sc_a[:, 0:w_rp], sc_b[:, 0:w_rp], float(np.pi),
                                    float(-2 * np.pi), ALU.is_gt, ALU.mult)
                    v.drain()
                    v.tensor_tensor(sc_b[:, 0:w_rp], sc_b[:, 0:w_rp], sc_a[:, 0:w_rp], ALU.add)
                    v.drain()
                    ch["v"].inc(v.drain())

                @blk.scalar
                def _(s):
                    s.wait_ge(ch["v"].sem, ch["v"].n)
                    s.activation(out=sc_a[:, 0:w_rp], in_=sc_b[:, 0:w_rp], func=AF.Sin)
                    ch["a"].inc(s.drain())

                @blk.vector
                def _(v, lo=lo, hi=hi):
                    v.wait_ge(ch["a"].sem, ch["a"].n)
                    v.tensor_copy(c_cos[:, lo:hi], sc_a[:, 0:w_rp])
                    v.drain()
                    ch["v"].inc(v.drain())
            bar(blk)

            for tt in range(NTT):
                @blk.sync
                def _(sy, tt=tt):
                    for c in ch.values():
                        if c.n:
                            sy.wait_ge(c.sem, c.n)
                    dq_di.wait_all(sy)
                    dq_dg.wait_all(sy)
                    dq_di.issue(sy, lambda: sy.dma_start(out=sc_a[:], in_=x_in[tt * 128:(tt + 1) * 128, :]))

                @blk.vector
                def _(v, tt=tt):
                    dq_di.wait_all(v)
                    v.reduce_sum(sc_st[:, 0:1], sc_a[:], AXX)
                    v.drain()
                    ch["v"].inc(v.drain())

                @blk.scalar
                def _(s, tt=tt):
                    dq_di.wait_all(s)
                    s.activation(out=sc_b[:], in_=sc_a[:], func=AF.Square,
                                 accum_out=sc_st[:, 1:2])
                    ch["a"].inc(s.drain())

                @blk.vector
                def _(v, tt=tt):
                    v.wait_ge(ch["a"].sem, ch["a"].n)
                    v.wait_ge(ch["v"].sem, ch["v"].n)
                    v.tensor_scalar(sc_st[:, 2:3], sc_st[:, 0:1], float(1.0 / D), None, ALU.mult)
                    v.drain()
                    v.tensor_tensor(sc_st[:, 3:4], sc_st[:, 2:3], sc_st[:, 2:3], ALU.mult)
                    v.drain()
                    v.tensor_scalar(sc_st[:, 4:5], sc_st[:, 1:2], float(1.0 / D), None, ALU.mult)
                    v.drain()
                    v.tensor_tensor(sc_st[:, 4:5], sc_st[:, 4:5], sc_st[:, 3:4], ALU.subtract)
                    v.drain()
                    v.tensor_scalar(sc_st[:, 4:5], sc_st[:, 4:5], 1e-5, None, ALU.add)
                    v.drain()
                    ch["v"].inc(v.drain())

                @blk.scalar
                def _(s, tt=tt):
                    s.wait_ge(ch["v"].sem, ch["v"].n)
                    s.activation(out=sc_st[:, 5:6], in_=sc_st[:, 4:5], func=AF.Sqrt)
                    ch["a"].inc(s.drain())

                @blk.vector
                def _(v, tt=tt):
                    v.wait_ge(ch["a"].sem, ch["a"].n)
                    v.reciprocal(sc_st[:, 6:7], sc_st[:, 5:6])
                    v.drain()
                    v.tensor_tensor(sc_st[:, 7:8], sc_st[:, 2:3], sc_st[:, 6:7], ALU.mult)
                    v.drain()
                    v.tensor_scalar(sc_st[:, 7:8], sc_st[:, 7:8], -1.0, None, ALU.mult)
                    v.drain()
                    ch["v"].inc(v.drain())

                @blk.scalar
                def _(s, tt=tt):
                    s.wait_ge(ch["v"].sem, ch["v"].n)
                    s.activation(out=sc_b[:], in_=sc_a[:], func=AF.Identity,
                                 bias=sc_st[:, 7:8], scale=sc_st[:, 6:7])
                    ch["a"].inc(s.drain())

                @blk.vector
                def _(v, tt=tt):
                    v.wait_ge(ch["a"].sem, ch["a"].n)
                    emit_quant(v, sc_b, sc_bf, sc_g, sc_a, cfg, D)
                    ch["v"].inc(v.drain())

                if dbg:
                    @blk.gpsimd
                    def _(g, tt=tt):
                        g.wait_ge(ch["v"].sem, ch["v"].n)
                        dq_dg.issue(g, lambda: g.dma_start(out=o_hq[tt * 128:(tt + 1) * 128, :],
                                                 in_=sc_bf[:]))

                for c in range(NFC):
                    @blk.tensor
                    def _(t, tt=tt, c=c):
                        if c == 0:
                            t.wait_ge(ch["v"].sem, ch["v"].n)
                        if c >= 2:
                            t.wait_ge(ch["p1"].sem, ch["p1"].n - 1)
                        t.transpose(ps_t0 if c % 2 == 0 else ps_t1,
                                    sc_bf[:, c * 128:(c + 1) * 128], c_id_bf[:])
                        ch["t"].inc(t.drain())

                    @blk.vector
                    def _(v, tt=tt, c=c):
                        v.wait_ge(ch["t"].sem, ch["t"].n)
                        v.tensor_copy(stage[:, c * 128:(c + 1) * 128],
                                      ps_t0 if c % 2 == 0 else ps_t1)
                        v.drain()
                        ch["p1"].inc(v.drain())

                @blk.gpsimd
                def _(g, tt=tt):
                    g.wait_ge(ch["p1"].sem, ch["p1"].n)
                    for cg in range(0, NFC, 8):
                        dq_dg.issue(g, lambda cg=cg: g.dma_start(
                            out=d_hqT_in.ap().rearrange("(c f) t -> c f t", f=128)[
                                cg:cg + 8, :, tt * 128:(tt + 1) * 128].transpose([1, 0, 2]),
                            in_=stage[:, cg * 128:(cg + 8) * 128].rearrange(
                                "f (c t) -> f c t", t=128)))

            @blk.gpsimd
            def _(g):
                dq_dg.wait_all(g)
                ch["cc"].inc(g.collective_compute(
                    "AllGather", ALU.bypass, replica_groups=[list(range(NC))],
                    ins=[d_hqT_in.ap().opt()], outs=[d_hqT.ap().opt()]))
                g.wait_ge(ch["cc"].sem, ch["cc"].n)
                if dbg:
                    dq_dg.issue(g, lambda: g.dma_start(out=o_hqT[:, :], in_=d_hqT[:, :]))
                dq_dg.wait_all(g)
            bar(blk)

            def emit_zero_y():
                @blk.vector
                def _(v):
                    v.memset(sc_a[:], 0.0)
                    ch["v"].inc(v.drain())

                @blk.gpsimd
                def _(g):
                    g.wait_ge(ch["v"].sem, ch["v"].n)
                    for tt in range(NTT):
                        dq_dg.issue(g, lambda: g.dma_start(out=y_out[tt * 128:(tt + 1) * 128, :], in_=sc_a[:]))
                    dq_dg.wait_all(g)

            if upto == "p0":
                emit_zero_y()
                stk2.close()
                return nc, dbg_outs

            def transpose_to_stage(n_chunks, src_bf, pst0, pst1, dst=None, dst_off=0):
                d = stage if dst is None else dst
                base = ch["p1"].n
                for c in range(n_chunks):
                    @blk.tensor
                    def _(t, c=c, base=base):
                        if c == 0:
                            t.wait_ge(ch["v"].sem, ch["v"].n)
                        t.wait_ge(ch["p1"].sem, max(base, base + c - 1))
                        t.transpose(pst0 if c % 2 == 0 else pst1,
                                    src_bf[:, c * 128:(c + 1) * 128], c_id_bf[:])
                        ch["t"].inc(t.drain())

                    @blk.vector
                    def _(v, c=c):
                        v.wait_ge(ch["t"].sem, ch["t"].n)
                        v.tensor_copy(d[:, dst_off + c * 128:dst_off + (c + 1) * 128],
                                      pst0 if c % 2 == 0 else pst1)
                        v.drain()
                        ch["p1"].inc(v.drain())

            for ocl in range(NOCL):
                @blk.sync
                def _(sy, ocl=ocl):
                    for c in ch.values():
                        if c.n:
                            sy.wait_ge(c.sem, c.n)
                    dq_di.wait_all(sy)
                    dq_dg.wait_all(sy)
                    dq_di.issue(sy, lambda: sy.dma_start(out=sc_a[:], in_=wo_in[ocl * 128:(ocl + 1) * 128, :]))

                @blk.vector
                def _(v, ocl=ocl):
                    dq_di.wait_all(v)
                    emit_quant(v, sc_a, sc_bf, sc_g, sc_b, cfg, D)
                    ch["v"].inc(v.drain())

                transpose_to_stage(NFC, sc_bf, ps_t0, ps_t1)

                @blk.gpsimd
                def _(g, ocl=ocl):
                    g.wait_ge(ch["p1"].sem, ch["p1"].n)
                    dq_dg.issue(g, lambda: g.dma_start(
                        out=d_qwoT_in.ap()[ocl * 128:(ocl + 1) * 128, :],
                        in_=stage[:]))

            @blk.gpsimd
            def _(g):
                dq_dg.wait_all(g)
                ch["cc"].inc(g.collective_compute(
                    "AllGather", ALU.bypass, replica_groups=[list(range(NC))],
                    ins=[d_qwoT_in.ap().opt()], outs=[d_qwoT.ap().opt()]))

            for rbl in range(RB_PC):
                @blk.gpsimd
                def _(g, rbl=rbl):
                    for c in ch.values():
                        if c.n:
                            g.wait_ge(c.sem, c.n)
                    dq_di.wait_all(g)
                    dq_dg.wait_all(g)
                    dq_dg.issue(g, lambda: g.dma_start(out=sc_bf[:], in_=w1_in[rbl * 128:(rbl + 1) * 128, :]))

                @blk.vector
                def _(v, rbl=rbl):
                    dq_dg.wait_all(v)
                    ch["v"].inc(v.drain())

                transpose_to_stage(NFC, sc_bf, ps_t0, ps_t1)

                @blk.gpsimd
                def _(g, rbl=rbl):
                    g.wait_ge(ch["p1"].sem, ch["p1"].n)
                    dq_dg.issue(g, lambda: g.dma_start(
                        out=d_w1t_in.ap()[rbl * 128:(rbl + 1) * 128, :],
                        in_=stage[:]))
                    dq_dg.issue(g, lambda: g.dma_start(out=sc_bf2[:],
                                             in_=w2_in[rbl * 128:(rbl + 1) * 128, :]))
                    dq_dg.wait_all(g)
                    dq_dg.issue(g, lambda: g.dma_start(
                        out=d_w2_in.ap()[rbl * 128:(rbl + 1) * 128, :],
                        in_=sc_bf2[:]))

            @blk.gpsimd
            def _(g):
                dq_dg.wait_all(g)
                ch["cc"].inc(g.collective_compute(
                    "AllGather", ALU.bypass, replica_groups=[list(range(NC))],
                    ins=[d_w1t_in.ap().opt()], outs=[d_w1t.ap().opt()]))
                ch["cc"].inc(g.collective_compute(
                    "AllGather", ALU.bypass, replica_groups=[list(range(NC))],
                    ins=[d_w2_in.ap().opt()], outs=[d_w2.ap().opt()]))
            bar(blk)

            if upto == "p1":
                emit_zero_y()
                stk2.close()
                return nc, dbg_outs

            prestk.close()
            TT1 = 256
            NP1 = S // TT1
            SCALE = float(1.0 / np.sqrt(HD))
            with ExitStack() as bstk:
                sbB = lambda name, shape, dt: bstk.enter_context(nc.sbuf_tensor(name, shape, dt))
                b_qw = sbB("b_qw", [128, 3 * D], BF16)
                b_qT = sbB("b_qT", [128, S], BF16)
                b_kT = sbB("b_kT", [128, S], BF16)
                b_v = sbB("b_v", [128, S], BF16)
                b_hq2 = sbB("b_hq2", [128, 2 * NFC * TT1], BF16)
                b_pt = sbB("b_pt", [128, 4 * 512], BF16)
                b_ob = sbB("b_ob", [128, 512], BF16)
                b_L = sbB("b_L", [1, 512], F32)
                b_rL = sbB("b_rL", [1, 512], F32)
                b_rLc = sbB("b_rLc", [128, 4], F32)
                p_qk = bstk.enter_context(nc.psum_tensor("p_qk", [128, 2 * TT1], F32))
                p_v = bstk.enter_context(nc.psum_tensor("p_v", [128, TT1], F32))
                p_sp = bstk.enter_context(nc.psum_tensor("p_sp", [128, 2048], F32))
                p_o = bstk.enter_context(nc.psum_tensor("p_o", [128, 512], F32))
                p_l = bstk.enter_context(nc.psum_tensor("p_l", [1, 512], F32))
                p_tb_ap = p_v[:, 128:192].bitcast(BF16)

                for h in range(HPC):
                    for pj, w_ext in enumerate([wq_in, wk_in, wv_in]):
                        @blk.sync
                        def _(sy, pj=pj, w_ext=w_ext, h=h):
                            for c in ch.values():
                                if c.n:
                                    sy.wait_ge(c.sem, c.n)
                            dq_di.wait_all(sy)
                            dq_dg.wait_all(sy)
                            dq_di.issue(sy, lambda: sy.dma_start(
                                out=sc_a[:], in_=w_ext[h * 128:(h + 1) * 128, :]))

                        @blk.vector
                        def _(v, pj=pj):
                            dq_di.wait_all(v)
                            emit_quant(v, sc_a, sc_bf, sc_g, sc_b, cfg, D)
                            ch["v"].inc(v.drain())

                        transpose_to_stage(NFC, sc_bf, p_sp[:, 0:64].bitcast(BF16),
                                           p_sp[:, 512:576].bitcast(BF16),
                                           dst=b_qw, dst_off=pj * D)

                    for tt in range(NP1):
                        glo = tt * TT1
                        buf = tt % 2
                        boff = buf * NFC * TT1
                        @blk.sync
                        def _(sy, glo=glo, boff=boff, tt=tt):
                            if tt < 2:
                                for c in ch.values():
                                    if c.n:
                                        sy.wait_ge(c.sem, c.n)
                                dq_di.wait_all(sy)
                                dq_dg.wait_all(sy)
                            else:
                                sy.wait_ge(ch["p2"].sem, ch["p2"].n - 1)
                            pieces = []
                            t0 = glo
                            while t0 < glo + TT1:
                                s_id = t0 // TPC
                                t1 = min(glo + TT1, (s_id + 1) * TPC)
                                pieces.append((t0, t1, s_id))
                                t0 = t1
                            for (a0, a1, s_id) in pieces:
                                for cg in range(0, NFC, 8):
                                    dq_di.issue(sy, lambda a0=a0, a1=a1, s_id=s_id, cg=cg: sy.dma_start(
                                        out=b_hq2[:, boff:boff + NFC * TT1].rearrange(
                                            "p (c t) -> p c t", t=TT1)[:, cg:cg + 8, a0 - glo:a1 - glo],
                                        in_=d_hqT.ap().rearrange("(s c f) t -> s c f t", c=NFC, f=128)[
                                            s_id, cg:cg + 8, :,
                                            a0 - s_id * TPC:a1 - s_id * TPC].transpose([1, 0, 2])))

                        @blk.tensor
                        def _(t, glo=glo, boff=boff, tt=tt):
                            dq_di.wait_all(t)
                            if tt >= 1:
                                t.wait_ge(ch["p3"].sem, ch["p3"].n)
                            for pj in range(2):
                                for c in range(NFC):
                                    t.matmul(p_qk[:, pj * TT1:(pj + 1) * TT1],
                                             b_qw[:, pj * D + c * 128:pj * D + (c + 1) * 128],
                                             b_hq2[:, boff + c * TT1:boff + (c + 1) * TT1],
                                             start=(c == 0), stop=(c == NFC - 1))
                            for sub in range(TT1 // 128):
                                for c in range(NFC):
                                    t.matmul(p_v[:, sub * 128:(sub + 1) * 128],
                                             b_hq2[:, boff + c * TT1 + sub * 128:boff + c * TT1 + (sub + 1) * 128],
                                             b_qw[:, 2 * D + c * 128:2 * D + (c + 1) * 128],
                                             start=(c == 0), stop=(c == NFC - 1))
                            ch["t"].inc(t.drain())
                            ch["p2"].inc(t.drain())

                        @blk.scalar
                        def _(s, glo=glo, tt=tt):
                            s.wait_ge(ch["t"].sem, ch["t"].n)
                            s.activation(out=b_qT[:, glo:glo + TT1], in_=p_qk[:, 0:TT1], func=AF.Identity)
                            s.activation(out=b_v[:, glo:glo + TT1], in_=p_v[:], func=AF.Identity)
                            ch["a"].inc(s.drain())

                        @blk.vector
                        def _(v, glo=glo, tt=tt):
                            v.wait_ge(ch["t"].sem, ch["t"].n)
                            v.tensor_copy(b_kT[:, glo:glo + TT1], p_qk[:, TT1:2 * TT1])
                            v.drain()
                            ch["v"].inc(v.drain())

                        @blk.gpsimd
                        def _(g, tt=tt):
                            g.wait_ge(ch["a"].sem, ch["a"].n)
                            g.wait_ge(ch["v"].sem, ch["v"].n)
                            ch["p3"].inc(g.drain())

                    t_sw = b_hq2[:, 0:S]
                    t_t1 = b_hq2[:, S:2 * S]
                    for tensbuf in (b_qT, b_kT):
                        @blk.gpsimd
                        def _(g, tensbuf=tensbuf):
                            for c in ch.values():
                                if c.n:
                                    g.wait_ge(c.sem, c.n)
                            dq_di.wait_all(g)
                            dq_dg.wait_all(g)
                            dq_dg.issue(g, lambda: g.dma_start(out=t_sw[0:64, :], in_=tensbuf[64:128, :]))
                            dq_dg.issue(g, lambda: g.dma_start(out=t_sw[64:128, :], in_=tensbuf[0:64, :]))

                        @blk.vector
                        def _(v, tensbuf=tensbuf):
                            dq_dg.wait_all(v)
                            v.tensor_tensor(t_t1[:, :], t_sw[:, :], c_ssin[:], ALU.mult)
                            v.drain()
                            v.tensor_tensor(tensbuf[:], tensbuf[:], c_cos[:], ALU.mult)
                            v.drain()
                            v.tensor_tensor(tensbuf[:], tensbuf[:], t_t1[:, :], ALU.add)
                            v.drain()
                            ch["v"].inc(v.drain())

                    if dbg and h == 0:
                        @blk.gpsimd
                        def _(g):
                            g.wait_ge(ch["v"].sem, ch["v"].n)
                            dq_dg.issue(g, lambda: g.dma_start(out=dbg_outs["o_qTr"][:, :], in_=b_qT[:]))
                            dq_dg.issue(g, lambda: g.dma_start(out=dbg_outs["o_kTr"][:, :], in_=b_kT[:]))
                            dq_dg.issue(g, lambda: g.dma_start(out=dbg_outs["o_vtm"][:, :], in_=b_v[:]))

                    for qt in range(NQT):
                        nkb = 4 * qt + 4
                        qsl = b_qT[:, qt * 512:(qt + 1) * 512]
                        t_base = ch["t"].n
                        a_base = ch["a"].n
                        v_base = ch["v"].n
                        n_diag_before = [0] * (nkb + 1)
                        for kb in range(nkb):
                            n_diag_before[kb + 1] = n_diag_before[kb] + (1 if kb >= 4 * qt else 0)

                        @blk.tensor
                        def _(t, qt=qt):
                            if qt == 0:
                                t.wait_ge(ch["v"].sem, ch["v"].n)
                            t.wait_ge(ch["p4"].sem, ch["p4"].n)
                        for p in range(nkb // 2):
                            for bb in range(2):
                                kb = 2 * p + bb

                                @blk.tensor
                                def _(t, kb=kb, p=p, bb=bb, a_base=a_base):
                                    if bb == 0 and p >= 2:
                                        t.wait_ge(ch["a"].sem, a_base + p - 1)
                                    ch["t"].inc(t.matmul(
                                        p_sp[:, (p % 2) * 1024 + bb * 512:(p % 2) * 1024 + (bb + 1) * 512],
                                        b_kT[:, kb * 128:(kb + 1) * 128], qsl,
                                        start=True, stop=True))

                            @blk.scalar
                            def _(s, p=p, t_base=t_base):
                                s.wait_ge(ch["t"].sem, t_base + 6 * p + 2)
                                s.activation(out=b_pt[:, (p % 2) * 1024:(p % 2 + 1) * 1024],
                                             in_=p_sp[:, (p % 2) * 1024:(p % 2 + 1) * 1024],
                                             func=AF.Exp, bias=c_negC[:, 0:1], scale=SCALE)
                                ch["a"].inc(s.drain())

                            for bb in range(2):
                                kb = 2 * p + bb
                                if kb >= 4 * qt:
                                    @blk.vector
                                    def _(v, kb=kb, p=p, bb=bb, qt=qt, a_base=a_base):
                                        v.wait_ge(ch["a"].sem, a_base + p + 1)
                                        d = kb - 4 * qt
                                        sl = b_pt[:, (p % 2) * 1024 + bb * 512:
                                                  (p % 2) * 1024 + (bb + 1) * 512]
                                        v.tensor_tensor(sl, sl,
                                                        c_masks[:, d * 512:(d + 1) * 512], ALU.mult)
                                        v.drain()
                                        ch["v"].inc(v.drain())

                            for bb in range(2):
                                kb = 2 * p + bb
                                diag = kb >= 4 * qt

                                @blk.tensor
                                def _(t, kb=kb, p=p, bb=bb, a_base=a_base, v_base=v_base,
                                      nd=n_diag_before, diag=diag, nkb=nkb):
                                    if diag:
                                        t.wait_ge(ch["v"].sem, v_base + nd[kb + 1])
                                    else:
                                        t.wait_ge(ch["a"].sem, a_base + p + 1)
                                    pt = b_pt[:, (p % 2) * 1024 + bb * 512:
                                              (p % 2) * 1024 + (bb + 1) * 512]
                                    ch["t"].inc(t.matmul(p_o[:], b_v[:, kb * 128:(kb + 1) * 128], pt,
                                                         start=(kb == 0), stop=(kb == nkb - 1)))
                                    ch["t"].inc(t.matmul(p_l[:], c_ones_bf[:], pt,
                                                         start=(kb == 0), stop=(kb == nkb - 1)))

                        @blk.scalar
                        def _(s, qt=qt):
                            s.wait_ge(ch["t"].sem, ch["t"].n)
                            s.activation(out=b_ob[:], in_=p_o[:], func=AF.Identity)
                            ch["a"].inc(s.drain())

                        @blk.vector
                        def _(v, qt=qt):
                            v.wait_ge(ch["t"].sem, ch["t"].n)
                            v.tensor_copy(b_L[:], p_l[:])
                            v.drain()
                            v.reciprocal(b_rL[:], b_L[:])
                            v.drain()
                            ch["v"].inc(v.drain())

                        @blk.tensor
                        def _(t, qt=qt):
                            t.wait_ge(ch["v"].sem, ch["v"].n)
                            for i in range(4):
                                t.transpose(p_qk[:, i:i + 1],
                                            b_rL[:, i * 128:(i + 1) * 128], c_id_f[0:1, 0:1])
                            ch["t"].inc(t.drain())

                        @blk.vector
                        def _(v, qt=qt):
                            v.wait_ge(ch["t"].sem, ch["t"].n)
                            v.tensor_copy(b_rLc[:], p_qk[:, 0:4])
                            v.drain()
                            ch["v"].inc(v.drain())

                        for i in range(4):
                            @blk.tensor
                            def _(t, i=i, qt=qt):
                                t.wait_ge(ch["a"].sem, ch["a"].n)
                                if i > 0:
                                    t.wait_ge(ch["p1"].sem, ch["p1"].n)
                                t.transpose(p_tb_ap, b_ob[:, i * 128:(i + 1) * 128], c_id_bf[:])
                                ch["t"].inc(t.drain())

                            @blk.vector
                            def _(v, i=i, qt=qt, h=h):
                                v.wait_ge(ch["t"].sem, ch["t"].n)
                                v.wait_ge(ch["v"].sem, ch["v"].n)
                                v.tensor_scalar(sc_a[:, 0:128], p_tb_ap, b_rLc[:, i:i + 1],
                                                None, ALU.mult)
                                v.drain()
                                v.tensor_reduce(sc_g[:, 0:4].unsqueeze(2),
                                                sc_a[:, 0:128].rearrange("p (g r) -> p g r", r=32),
                                                AXX, ALU.max, apply_absolute_value=True)
                                v.drain()
                                v.tensor_scalar(sc_g[:, 0:4], sc_g[:, 0:4], qp_inv[:, 0:1],
                                                1e-12, ALU.mult, ALU.max)
                                v.drain()
                                v.reciprocal(sc_g[:, 4:8], sc_g[:, 0:4])
                                v.drain()
                                v.tensor_tensor(sc_b[:, 0:128].rearrange("p (g r) -> p g r", r=32),
                                                sc_a[:, 0:128].rearrange("p (g r) -> p g r", r=32),
                                                sc_g[:, 4:8].unsqueeze(2).broadcast_to([128, 4, 32]),
                                                ALU.mult)
                                v.drain()
                                v.tensor_copy(sc_b[:, 0:128].bitcast(I32), sc_b[:, 0:128])
                                v.drain()
                                v.tensor_copy(sc_b[:, 0:128], sc_b[:, 0:128].bitcast(I32))
                                v.drain()
                                v.tensor_tensor(stage[:, i * 128:(i + 1) * 128].rearrange(
                                                    "p (g r) -> p g r", r=32),
                                                sc_b[:, 0:128].rearrange("p (g r) -> p g r", r=32),
                                                sc_g[:, 0:4].unsqueeze(2).broadcast_to([128, 4, 32]),
                                                ALU.mult)
                                v.drain()
                                ch["p1"].inc(v.drain())

                        @blk.gpsimd
                        def _(g, qt=qt, h=h):
                            g.wait_ge(ch["p1"].sem, ch["p1"].n)
                            dq_dg.issue(g, lambda: g.dma_start(
                                out=d_qo_in.ap()[h * S + qt * 512:h * S + (qt + 1) * 512, :].rearrange(
                                    "(i q) d -> i q d", q=128).transpose([1, 0, 2]),
                                in_=stage[:, 0:512].rearrange("q (i d) -> q i d", d=128)))
                            dq_dg.wait_all(g)
                            ch["p4"].inc(g.drain())

                @blk.gpsimd
                def _(g):
                    for c in ch.values():
                        if c.n:
                            g.wait_ge(c.sem, c.n)
                    dq_di.wait_all(g)
                    dq_dg.wait_all(g)
                    ch["cc"].inc(g.collective_compute(
                        "AllGather", ALU.bypass, replica_groups=[list(range(NC))],
                        ins=[d_qo_in.ap().opt()], outs=[d_qo.ap().opt()]))
                    g.wait_ge(ch["cc"].sem, ch["cc"].n)
                    if dbg:
                        dq_dg.issue(g, lambda: g.dma_start(out=dbg_outs["o_qo"][:, :], in_=d_qo[:, :]))
                bar(blk)

            stk2.close()
            if upto == "b":
                emit_zero_y()
                return nc, dbg_outs
            WG = min(TPC, 512)
            NTG = TPC // WG
            d_c2t = nc.dram_tensor("d_c2t", [2 * RANK, INTER], BF16)
            d_ffn = nc.dram_tensor("d_ffn", [TPC, D], BF16)
            with ExitStack() as cfstk:
                sbCF = lambda n_, s_, d_: cfstk.enter_context(nc.sbuf_tensor(n_, s_, d_))
                cnegT = sbCF("cnegT", [NBLK, TPC], BF16)
                with ExitStack() as cstk:
                    sbC = lambda n_, s_, d_: cstk.enter_context(nc.sbuf_tensor(n_, s_, d_))
                    c_c1t = sbC("c_c1t", [128, 2 * NFC * RANK], BF16)
                    cq_oT = sbC("cq_oT", [128, NFC * TPC], BF16)
                    cs_b = sbC("cs_b", [128, D], F32)
                    cwo = sbC("cwo", [128, 2 * NFC * 128], BF16)
                    cx2g = sbC("cx2g", [128, (WG // 128) * D], BF16)
                    cxT = sbC("cxT", [128, 2 * NFC * 128], BF16)
                    cr1 = sbC("cr1", [16, 3 * TPC], F32)
                    cc2 = sbC("cc2", [16, 2 * 1024], BF16)
                    cidxf = sbC("cidxf", [128, NTT * NBLK], F32)
                    cidxu = sbC("cidxu", [128, NBLK], U32)
                    cm8 = sbC("cm8", [128, 8], F32)
                    cmi = sbC("cmi", [128, 8], U32)
                    cps_tb = cstk.enter_context(nc.psum_tensor("cps_tb", [128, 128], BF16))
                    cps_t = cstk.enter_context(nc.psum_tensor("cps_t", [128, 512], F32))
                    cps_c = cstk.enter_context(nc.psum_tensor("cps_c", [128, 512], F32))
                    cps_r = cstk.enter_context(nc.psum_tensor("cps_r", [16, 512], F32))
                    cps_i = cstk.enter_context(nc.psum_tensor("cps_i", [128, 512], F32))

                    @blk.sync
                    def _(sy):
                        for c in ch.values():
                            if c.n:
                                sy.wait_ge(c.sem, c.n)
                        dq_di.wait_all(sy)
                        dq_dg.wait_all(sy)
                        dq_di.issue(sy, lambda: sy.dma_start(out=cs_b[0:RANK, :], in_=c1_in[:, :]))

                    for c in range(NFC):
                        @blk.tensor
                        def _(t, c=c):
                            dq_di.wait_all(t)
                            if c >= 1:
                                t.wait_ge(ch["p1"].sem, ch["p1"].n)
                            t.transpose(cps_t[:, 0:RANK], cs_b[0:RANK, c * 128:(c + 1) * 128],
                                        c_id_f[0:RANK, 0:RANK])
                            ch["t"].inc(t.drain())

                        @blk.vector
                        def _(v, c=c):
                            v.wait_ge(ch["t"].sem, ch["t"].n)
                            v.tensor_copy(sc_a[:, c * RANK:(c + 1) * RANK], cps_t[:, 0:RANK])
                            v.drain()
                            ch["p1"].inc(v.drain())

                    @blk.vector
                    def _(v):
                        v.wait_ge(ch["p1"].sem, ch["p1"].n)
                        v.tensor_copy(c_c1t[:, 0:NFC * RANK], sc_a[:, 0:NFC * RANK])
                        v.drain()
                        v.tensor_tensor(c_c1t[:, NFC * RANK:2 * NFC * RANK], sc_a[:, 0:NFC * RANK],
                                        c_c1t[:, 0:NFC * RANK], ALU.subtract)
                        v.drain()
                        ch["v"].inc(v.drain())

                    for rt in range(INTER // 128):
                        @blk.sync
                        def _(sy, rt=rt):
                            sy.wait_ge(ch["v"].sem, ch["v"].n)
                            if rt > 0:
                                sy.wait_ge(ch["p2"].sem, ch["p2"].n)
                            dq_di.issue(sy, lambda: sy.dma_start(out=cs_b[:, 0:RANK],
                                                      in_=c2_in[rt * 128:(rt + 1) * 128, :]))

                        @blk.tensor
                        def _(t, rt=rt):
                            dq_di.wait_all(t)
                            if rt > 0:
                                t.wait_ge(ch["p3"].sem, ch["p3"].n)
                            t.transpose(cps_t[0:RANK, 0:128], cs_b[:, 0:RANK], c_id_f[:])
                            ch["t"].inc(t.drain())

                        @blk.vector
                        def _(v, rt=rt):
                            v.wait_ge(ch["t"].sem, ch["t"].n)
                            v.tensor_copy(sc_a[0:RANK, 0:128], cps_t[0:RANK, 0:128])
                            v.drain()
                            ch["p3"].inc(v.drain())
                            v.tensor_copy(sc_bf[0:RANK, 0:128], sc_a[0:RANK, 0:128])
                            v.drain()
                            v.tensor_tensor(sc_bf[0:RANK, 128:256], sc_a[0:RANK, 0:128],
                                            sc_bf[0:RANK, 0:128], ALU.subtract)
                            v.drain()
                            ch["v"].inc(v.drain())

                        @blk.gpsimd
                        def _(g, rt=rt):
                            g.wait_ge(ch["v"].sem, ch["v"].n)
                            dq_dg.issue(g, lambda: g.dma_start(out=d_c2t.ap()[0:RANK, rt * 128:(rt + 1) * 128],
                                                     in_=sc_bf[0:RANK, 0:128]))
                            dq_dg.issue(g, lambda: g.dma_start(out=d_c2t.ap()[RANK:2 * RANK, rt * 128:(rt + 1) * 128],
                                                     in_=sc_bf[0:RANK, 128:256]))
                            dq_dg.wait_all(g)
                            ch["p2"].inc(g.drain())
                    bar(blk)

                    @blk.gpsimd
                    def _(g):
                        pid = g.partition_id()
                        dq_dg.issue(g, lambda: g.dma_start(
                            out=d_qostage.ap().rearrange("(ic t) d -> ic t d", t=TPC),
                            in_=d_qo.ap().rearrange("(ic s) d -> ic s d", s=S)[
                                :, bass.ds(pid * TPC, TPC), :]))

                    @blk.sync
                    def _(sy):
                        dq_dg.wait_all(sy)
                        for ic in range(NFC):
                            dq_di.issue(sy, lambda: sy.dma_start(
                                out=cq_oT[:, ic * TPC:(ic + 1) * TPC],
                                in_=d_qostage.ap()[ic * TPC:(ic + 1) * TPC, :],
                                transpose=True))

                    for tg in range(NTG):
                        for oc in range(NOC):
                            @blk.sync
                            def _(sy, tg=tg, oc=oc):
                                if tg == 0 and oc == 0:
                                    for c in ch.values():
                                        if c.n:
                                            sy.wait_ge(c.sem, c.n)
                                    dq_di.wait_all(sy)
                                    dq_dg.wait_all(sy)
                                if oc >= 2:
                                    sy.wait_ge(ch["p1"].sem, ch["p1"].n - 1)
                                s2, ocl = oc // NOCL, oc % NOCL
                                dq_di.issue(sy, lambda: sy.dma_start(
                                    out=cwo[:, (oc % 2) * NFC * 128:(oc % 2 + 1) * NFC * 128],
                                    in_=d_qwoT.ap()[(s2 * NOCL + ocl) * 128:
                                                    (s2 * NOCL + ocl + 1) * 128, :]))

                            @blk.tensor
                            def _(t, tg=tg, oc=oc):
                                dq_di.wait_all(t)
                                if oc >= 2:
                                    t.wait_ge(ch["a"].sem, ch["a"].n - 1)
                                for c in range(NFC):
                                    t.matmul(cps_c[:, 0:WG] if oc % 2 == 0 else cps_t[:, 0:WG],
                                             cwo[:, (oc % 2) * NFC * 128 + c * 128:
                                                 (oc % 2) * NFC * 128 + (c + 1) * 128],
                                             cq_oT[:, c * TPC + tg * WG:c * TPC + tg * WG + WG],
                                             start=(c == 0), stop=(c == NFC - 1))
                                ch["t"].inc(t.drain())
                                ch["p1"].inc(t.drain())

                            @blk.scalar
                            def _(s, tg=tg, oc=oc):
                                s.wait_ge(ch["t"].sem, ch["t"].n)
                                s.activation(out=stage[:, 0:WG] if oc % 2 == 0 else stage[:, WG:2 * WG],
                                             in_=cps_c[:, 0:WG] if oc % 2 == 0 else cps_t[:, 0:WG],
                                             func=AF.Identity)
                                ch["a"].inc(s.drain())

                            for i in range(WG // 128):
                                @blk.tensor
                                def _(t, oc=oc, i=i):
                                    t.wait_ge(ch["a"].sem, ch["a"].n)
                                    t.wait_ge(ch["p2"].sem, ch["p2"].n)
                                    t.wait_ge(ch["p4"].sem, ch["p4"].n)
                                    t.transpose(cps_tb[:],
                                                (stage[:, 0:WG] if oc % 2 == 0 else
                                                 stage[:, WG:2 * WG])[:, i * 128:(i + 1) * 128],
                                                c_id_bf[:])
                                    ch["t"].inc(t.drain())

                                @blk.vector
                                def _(v, oc=oc, i=i):
                                    v.wait_ge(ch["t"].sem, ch["t"].n)
                                    v.tensor_copy(cx2g[:, i * D + oc * 128:i * D + (oc + 1) * 128],
                                                  cps_tb[:])
                                    v.drain()
                                    ch["p2"].inc(v.drain())

                        @blk.gpsimd
                        def _(g, tg=tg):
                            g.wait_ge(ch["p2"].sem, ch["p2"].n)
                            dq_dg.issue(g, lambda: g.dma_start(
                                out=d_aproj.ap()[tg * WG:(tg + 1) * WG, :].rearrange(
                                    "(i p) d -> i p d", p=128).transpose([1, 0, 2]),
                                in_=cx2g[:].rearrange("p (i d) -> p i d", d=D)))
                            dq_dg.wait_all(g)
                            ch["p4"].inc(g.drain())

                        for j in range(WG // 128):
                            @blk.sync
                            def _(sy, tg=tg, j=j):
                                sy.wait_ge(ch["p4"].sem, ch["p4"].n)
                                sy.wait_ge(ch["v"].sem, ch["v"].n)
                                sy.wait_ge(ch["t"].sem, ch["t"].n)
                                dq_di.issue(sy, lambda: sy.dma_start(
                                    out=sc_a[:], in_=x_in[tg * WG + j * 128:tg * WG + (j + 1) * 128, :]))

                            @blk.vector
                            def _(v, tg=tg, j=j):
                                dq_di.wait_all(v)
                                v.tensor_tensor(cs_b[:], sc_a[:], cx2g[:, j * D:(j + 1) * D], ALU.add)
                                v.drain()
                                v.reduce_sum(sc_st[:, 0:1], cs_b[:], AXX)
                                v.drain()
                                ch["v"].inc(v.drain())

                            @blk.scalar
                            def _(s):
                                s.wait_ge(ch["v"].sem, ch["v"].n)
                                s.activation(out=sc_a[:], in_=cs_b[:], func=AF.Square,
                                             accum_out=sc_st[:, 1:2])
                                ch["a"].inc(s.drain())

                            @blk.vector
                            def _(v):
                                v.wait_ge(ch["a"].sem, ch["a"].n)
                                v.tensor_scalar(sc_st[:, 2:3], sc_st[:, 0:1], float(1.0 / D), None, ALU.mult)
                                v.drain()
                                v.tensor_tensor(sc_st[:, 3:4], sc_st[:, 2:3], sc_st[:, 2:3], ALU.mult)
                                v.drain()
                                v.tensor_scalar(sc_st[:, 4:5], sc_st[:, 1:2], float(1.0 / D), None, ALU.mult)
                                v.drain()
                                v.tensor_tensor(sc_st[:, 4:5], sc_st[:, 4:5], sc_st[:, 3:4], ALU.subtract)
                                v.drain()
                                v.tensor_scalar(sc_st[:, 4:5], sc_st[:, 4:5], 1e-5, None, ALU.add)
                                v.drain()
                                ch["v"].inc(v.drain())

                            @blk.scalar
                            def _(s):
                                s.wait_ge(ch["v"].sem, ch["v"].n)
                                s.activation(out=sc_st[:, 5:6], in_=sc_st[:, 4:5], func=AF.Sqrt)
                                ch["a"].inc(s.drain())

                            @blk.vector
                            def _(v):
                                v.wait_ge(ch["a"].sem, ch["a"].n)
                                v.reciprocal(sc_st[:, 6:7], sc_st[:, 5:6])
                                v.drain()
                                v.tensor_tensor(sc_st[:, 7:8], sc_st[:, 2:3], sc_st[:, 6:7], ALU.mult)
                                v.drain()
                                v.tensor_scalar(sc_st[:, 7:8], sc_st[:, 7:8], -1.0, None, ALU.mult)
                                v.drain()
                                ch["v"].inc(v.drain())

                            @blk.scalar
                            def _(s):
                                s.wait_ge(ch["v"].sem, ch["v"].n)
                                s.activation(out=sc_a[:], in_=cs_b[:], func=AF.Identity,
                                             bias=sc_st[:, 7:8], scale=sc_st[:, 6:7])
                                ch["a"].inc(s.drain())

                            @blk.vector
                            def _(v):
                                v.wait_ge(ch["a"].sem, ch["a"].n)
                                v.tensor_copy(sc_bf[:], sc_a[:])
                                v.drain()
                                v.tensor_tensor(stage[:, 0:D], sc_a[:], sc_bf[:], ALU.subtract)
                                v.drain()
                                ch["v"].inc(v.drain())

                            base_p1 = ch["p1"].n
                            for c in range(2 * NFC):
                                src = sc_bf if c < NFC else stage
                                cc0 = c if c < NFC else c - NFC

                                @blk.tensor
                                def _(t, c=c, src=src, cc0=cc0, base_p1=base_p1):
                                    if c == 0:
                                        t.wait_ge(ch["v"].sem, ch["v"].n)
                                        t.wait_ge(ch["p3"].sem, ch["p3"].n)
                                    t.wait_ge(ch["p1"].sem, base_p1 + c)
                                    t.transpose(cps_tb[:], src[:, cc0 * 128:(cc0 + 1) * 128], c_id_bf[:])
                                    ch["t"].inc(t.drain())

                                @blk.vector
                                def _(v, c=c):
                                    v.wait_ge(ch["t"].sem, ch["t"].n)
                                    v.tensor_copy(cxT[:, c * 128:(c + 1) * 128], cps_tb[:])
                                    v.drain()
                                    ch["p1"].inc(v.drain())

                            @blk.gpsimd
                            def _(g, tg=tg, j=j):
                                g.wait_ge(ch["p1"].sem, ch["p1"].n)
                                for cg in range(0, NFC, 8):
                                    dq_dg.issue(g, lambda cg=cg: g.dma_start(
                                        out=d_xfT.ap().rearrange("(c f) t -> f c t", f=128)[
                                            :, cg:cg + 8, tg * WG + j * 128:tg * WG + (j + 1) * 128],
                                        in_=cxT[:, cg * 128:(cg + 8) * 128].rearrange(
                                            "f (c t) -> f c t", t=128)))
                                dq_dg.wait_all(g)
                                ch["p3"].inc(g.drain())

                            @blk.tensor
                            def _(t, tg=tg, j=j):
                                t.wait_ge(ch["p1"].sem, ch["p1"].n)
                                first = True
                                for c in range(NFC):
                                    for (la, lb) in [(0, 0), (0, 1), (1, 0)]:
                                        t.matmul(cps_r[:, j * 128:(j + 1) * 128],
                                                 c_c1t[:, (la * NFC + c) * RANK:(la * NFC + c + 1) * RANK],
                                                 cxT[:, (lb * NFC + c) * 128:(lb * NFC + c + 1) * 128],
                                                 start=first,
                                                 stop=(c == NFC - 1 and (la, lb) == (1, 0)))
                                        first = False
                                ch["t"].inc(t.drain())

                        @blk.vector
                        def _(v, tg=tg):
                            v.wait_ge(ch["t"].sem, ch["t"].n)
                            v.tensor_copy(cr1[:, tg * WG:(tg + 1) * WG], cps_r[:, 0:WG])
                            v.drain()
                            ch["v"].inc(v.drain())
                    bar(blk)

                    @blk.vector
                    def _(v):
                        v.tensor_copy(cr1.bitcast(BF16)[:, 2 * TPC:3 * TPC], cr1[:, 0:TPC])
                        v.drain()
                        v.tensor_tensor(cr1.bitcast(BF16)[:, 3 * TPC:4 * TPC], cr1[:, 0:TPC],
                                        cr1.bitcast(BF16)[:, 2 * TPC:3 * TPC], ALU.subtract)
                        v.drain()
                        ch["v"].inc(v.drain())
                    bar(blk)

                    for tt in range(NTT):
                        for rt in range(INTER // 512):
                            @blk.sync
                            def _(sy, tt=tt, rt=rt):
                                it = tt * (INTER // 512) + rt
                                if it >= 2:
                                    sy.wait_ge(ch["p1"].sem, ch["p1"].n - 1)
                                bsel = it % 2
                                dq_di.issue(sy, lambda: sy.dma_start(
                                    out=cc2[:, bsel * 1024:bsel * 1024 + 512],
                                    in_=d_c2t.ap()[0:RANK, rt * 512:(rt + 1) * 512]))
                                dq_di.issue(sy, lambda: sy.dma_start(
                                    out=cc2[:, bsel * 1024 + 512:bsel * 1024 + 1024],
                                    in_=d_c2t.ap()[RANK:2 * RANK, rt * 512:(rt + 1) * 512]))

                            @blk.tensor
                            def _(t, tt=tt, rt=rt):
                                dq_di.wait_all(t)
                                t.wait_ge(ch["a"].sem, ch["a"].n)
                                bsel = (tt * (INTER // 512) + rt) % 2
                                combos = [(2, 0), (2, 512), (3, 0)]
                                first = True
                                for (hl, off) in combos:
                                    t.matmul(cps_i[:],
                                             cr1.bitcast(BF16)[:, hl * TPC + tt * 128:
                                                               hl * TPC + (tt + 1) * 128],
                                             cc2[:, bsel * 1024 + off:bsel * 1024 + off + 512],
                                             start=first, stop=(hl, off) == (3, 0))
                                    first = False
                                ch["t"].inc(t.drain())
                                ch["p1"].inc(t.drain())

                            @blk.scalar
                            def _(s, tt=tt, rt=rt):
                                s.wait_ge(ch["t"].sem, ch["t"].n)
                                s.wait_ge(ch["v"].sem, ch["v"].n)
                                s.activation(out=sc_a[:, 0:512], in_=cps_i[:], func=AF.Identity)
                                ch["a"].inc(s.drain())

                            @blk.vector
                            def _(v, tt=tt, rt=rt):
                                v.wait_ge(ch["a"].sem, ch["a"].n)
                                for half in range(2):
                                    b = rt * 2 + half
                                    v.max(cm8[:], sc_a[:, half * 256:(half + 1) * 256])
                                    v.drain()
                                    v.max_index(cmi[:], cm8[:], sc_a[:, half * 256:(half + 1) * 256])
                                    v.drain()
                                    v.tensor_copy(cidxu[:, b:b + 1], cmi[:, 0:1])
                                    v.drain()
                                ch["v"].inc(v.drain())

                        @blk.vector
                        def _(v, tt=tt):
                            v.wait_ge(ch["v"].sem, ch["v"].n)
                            v.tensor_copy(cidxf[:, tt * NBLK:(tt + 1) * NBLK], cidxu[:])
                            v.drain()
                            v.tensor_copy(stage[:, 0:NBLK], cidxf[:, tt * NBLK:(tt + 1) * NBLK])
                            v.drain()
                            ch["v"].inc(v.drain())

                        @blk.tensor
                        def _(t, tt=tt):
                            t.wait_ge(ch["v"].sem, ch["v"].n)
                            t.wait_ge(ch["p2"].sem, ch["p2"].n)
                            t.transpose(cps_tb[0:NBLK, 0:128], stage[:, 0:NBLK], c_id_bf[:])
                            ch["t"].inc(t.drain())

                        @blk.vector
                        def _(v, tt=tt):
                            v.wait_ge(ch["t"].sem, ch["t"].n)
                            v.tensor_scalar(cnegT[:, tt * 128:(tt + 1) * 128], cps_tb[0:NBLK, 0:128],
                                            -1.0, None, ALU.mult)
                            v.drain()
                            ch["p2"].inc(v.drain())
                    bar(blk)

                    if dbg:
                        @blk.gpsimd
                        def _(g):
                            dq_dg.issue(g, lambda: g.dma_start(out=dbg_outs["o_idx"][:, :], in_=cidxf[:, :]))
                            dq_dg.issue(g, lambda: g.dma_start(out=dbg_outs["o_aproj"][:, :], in_=d_aproj[:, :]))
                            dq_dg.issue(g, lambda: g.dma_start(out=dbg_outs["o_xfT"][:, :], in_=d_xfT[:, :]))
                            dq_dg.wait_all(g)
                        bar(blk)

                if upto == "c":
                    emit_zero_y()
                    return nc, dbg_outs

                GRP = min(512, TPC)
                NG = TPC // GRP
                with ExitStack() as fstk:
                    sbF = lambda n_, s_, d_: fstk.enter_context(nc.sbuf_tensor(n_, s_, d_))
                    f_ht = sbF("f_ht", [128, RB * GRP], BF16)
                    f_xT = sbF("f_xT", [128, NFC * GRP], BF16)
                    f_w1 = sbF("f_w1", [128, 2 * NFC * 128], BF16)
                    f_w2 = sbF("f_w2", [128, 2 * 2 * 512], BF16)
                    f_er = stage[0:1, 2 * GRP:6 * GRP]
                    f_ones = stage[0:1, 6 * GRP:7 * GRP]
                    fps_p = fstk.enter_context(nc.psum_tensor("fps_p", [128, GRP], F32))
                    fps_p2 = fstk.enter_context(nc.psum_tensor("fps_p2", [128, GRP], F32))
                    fps_e = fstk.enter_context(nc.psum_tensor("fps_e", [128, GRP], F32))
                    fps_e2 = fstk.enter_context(nc.psum_tensor("fps_e2", [128, GRP], F32))
                    fps_oacc = [fstk.enter_context(nc.psum_tensor(f"fps_oa{i}", [128, 512], F32))
                                for i in range(GRP // 128)]

                    @blk.vector
                    def _(v):
                        v.memset(f_ones, 1.0)
                        v.drain()
                        ch["v"].inc(v.drain())
                    bar(blk)

                    ffn_fences = []
                    for fg in range(NG):
                        glo = fg * GRP
                        @blk.sync
                        def _(sy, glo=glo):
                            for c in ch.values():
                                if c.n:
                                    sy.wait_ge(c.sem, c.n)
                            dq_di.wait_all(sy)
                            dq_dg.wait_all(sy)
                            for cg in range(0, NFC, 8):
                                dq_di.issue(sy, lambda cg=cg: sy.dma_start(
                                    out=f_xT[:].rearrange("p (c t) -> p c t", t=GRP)[:, cg:cg + 8, :],
                                    in_=d_xfT.ap().rearrange("(c f) t -> f c t", f=128)[
                                        :, cg:cg + 8, glo:glo + GRP]))

                        er_toks = {}

                        @blk.gpsimd
                        def _(g, glo=glo):
                            for j0 in range(min(4, NBLK)):
                                er_toks[j0] = dq_dg.issue(g, lambda j0=j0: g.dma_start(
                                    out=f_er[:, (j0 % 4) * GRP:(j0 % 4 + 1) * GRP],
                                    in_=cnegT[j0:j0 + 1, glo:glo + GRP]))
                            ch["g"].inc(g.drain())

                        for rb in range(RB):
                            j = rb // 2
                            hi_half = rb % 2 == 1

                            @blk.sync
                            def _(sy, rb=rb, j=j, glo=glo):
                                if rb < 2:
                                    dq_di.wait_all(sy)
                                    sy.wait_ge(ch["g"].sem, ch["g"].n)
                                else:
                                    sy.wait_ge(ch["p1"].sem, ch["p1"].n - 1)
                                dq_di.issue(sy, lambda: sy.dma_start(
                                    out=f_w1[:, (rb % 2) * NFC * 128:(rb % 2 + 1) * NFC * 128],
                                    in_=d_w1t.ap()[rb * 128:(rb + 1) * 128, :]))

                            @blk.tensor
                            def _(t, rb=rb, j=j, hi_half=hi_half):
                                dq_di.wait_all(t)
                                if rb >= 2:
                                    t.wait_ge(ch["a"].sem, ch["a"].n - 1)
                                    t.wait_ge(ch["v"].sem, ch["v"].n - 1)
                                pp = fps_p if rb % 2 == 0 else fps_p2
                                for c in range(NFC):
                                    t.matmul(pp[:],
                                             f_w1[:, (rb % 2) * NFC * 128 + c * 128:
                                                  (rb % 2) * NFC * 128 + (c + 1) * 128],
                                             f_xT[:, c * GRP:(c + 1) * GRP],
                                             start=(c == 0), stop=(c == NFC - 1))
                                if not hi_half:
                                    t.wait_ge(*er_toks[j])
                                    pe = fps_e if j % 2 == 0 else fps_e2
                                    t.matmul(pe[:], c_iota_lo[:], f_ones, start=True, stop=False)
                                    t.matmul(pe[:], c_ones_row[:],
                                             f_er[:, (j % 4) * GRP:(j % 4 + 1) * GRP],
                                             start=False, stop=True)
                                    ch["p3"].inc(t.drain())
                                ch["t"].inc(t.drain())

                            if hi_half and j + 4 < NBLK:
                                @blk.gpsimd
                                def _(g, j=j, glo=glo):
                                    g.wait_ge(ch["p3"].sem, ch["p3"].n)
                                    er_toks[j + 4] = dq_dg.issue(g, lambda: g.dma_start(
                                        out=f_er[:, ((j + 4) % 4) * GRP:((j + 4) % 4 + 1) * GRP],
                                        in_=cnegT[j + 4:j + 5, glo:glo + GRP]))

                            @blk.scalar
                            def _(s, rb=rb):
                                s.wait_ge(ch["t"].sem, ch["t"].n)
                                pp = fps_p if rb % 2 == 0 else fps_p2
                                s.activation(out=stage[:, 0:GRP] if rb % 2 == 0 else
                                             stage[:, GRP:2 * GRP],
                                             in_=pp[:], func=AF.Relu)
                                ch["a"].inc(s.drain())

                            @blk.vector
                            def _(v, rb=rb, j=j, hi_half=hi_half):
                                v.wait_ge(ch["a"].sem, ch["a"].n)
                                pe = fps_e if j % 2 == 0 else fps_e2
                                v.scalar_tensor_tensor(
                                    f_ht[:, rb * GRP:(rb + 1) * GRP], pe[:],
                                    -128.0 if hi_half else 0.0,
                                    stage[:, 0:GRP] if rb % 2 == 0 else stage[:, GRP:2 * GRP],
                                    ALU.is_equal, ALU.mult)
                                v.drain()
                                ch["v"].inc(v.drain())
                                ch["p1"].inc(v.drain())

                        for fgrp in range(D // 512):
                            for rbp in range(RB // 2):
                                @blk.sync
                                def _(sy, fgrp=fgrp, rbp=rbp):
                                    if fgrp == 0 and rbp == 0:
                                        sy.wait_ge(ch["v"].sem, ch["v"].n)
                                    it = fgrp * (RB // 2) + rbp
                                    if it >= 2:
                                        sy.wait_ge(ch["p2"].sem, ch["p2"].n - 1)
                                    dq_di.issue(sy, lambda: sy.dma_start(
                                        out=f_w2[:, (rbp % 2) * 1024:(rbp % 2 + 1) * 1024].rearrange(
                                            "p (b t) -> p b t", t=512),
                                        in_=d_w2.ap()[rbp * 256:(rbp + 1) * 256,
                                                      fgrp * 512:(fgrp + 1) * 512].rearrange(
                                            "(b f) t -> f b t", f=128)))

                                @blk.tensor
                                def _(t, fgrp=fgrp, rbp=rbp, glo=glo):
                                    dq_di.wait_all(t)
                                    if rbp == 0 and fgrp > 0:
                                        t.wait_ge(ch["a"].sem, ch["a"].n)
                                    for bb in range(2):
                                        rb = rbp * 2 + bb
                                        for i in range(GRP // 128):
                                            t.matmul(fps_oacc[i][:],
                                                     f_ht[:, rb * GRP + i * 128:rb * GRP + (i + 1) * 128],
                                                     f_w2[:, (rbp % 2) * 1024 + bb * 512:
                                                          (rbp % 2) * 1024 + (bb + 1) * 512],
                                                     start=(rb == 0), stop=(rb == RB - 1))
                                    ch["t"].inc(t.drain())
                                    ch["p2"].inc(t.drain())

                            for i in range(GRP // 128):
                                @blk.scalar
                                def _(s, fgrp=fgrp, i=i):
                                    if i == 0:
                                        s.wait_ge(ch["t"].sem, ch["t"].n)
                                    if len(ffn_fences) >= 4:
                                        s.wait_ge(*ffn_fences[-4])
                                    s.activation(out=sc_bf[:, (i % 4) * 512:(i % 4 + 1) * 512],
                                                 in_=fps_oacc[i][:],
                                                 func=AF.Identity)
                                    ch["a"].inc(s.drain())

                                @blk.gpsimd
                                def _(g, fgrp=fgrp, i=i, glo=glo):
                                    g.wait_ge(ch["a"].sem, ch["a"].n)
                                    tok = dq_dg.issue(g, lambda: g.dma_start(
                                        out=d_ffn.ap()[glo + i * 128:glo + (i + 1) * 128,
                                                       fgrp * 512:(fgrp + 1) * 512],
                                        in_=sc_bf[:, (i % 4) * 512:(i % 4 + 1) * 512]))
                                    ch["p4"].inc(g.drain())
                                    ffn_fences.append(tok)
                    bar(blk)

                    if dbg:
                        @blk.gpsimd
                        def _(g):
                            for c in ch.values():
                                if c.n:
                                    g.wait_ge(c.sem, c.n)
                            dq_di.wait_all(g)
                            dq_dg.wait_all(g)
                            dq_dg.issue(g, lambda: g.dma_start(
                                out=dbg_outs["o_ffn"][:, :], in_=d_ffn.ap()[:, :]))
                            dq_dg.wait_all(g)
                            ch["g"].inc(g.drain())

                    for tt in range(NTT):
                        @blk.sync
                        def _(sy, tt=tt):
                            sy.wait_ge(ch["v"].sem, ch["v"].n)
                            dq_dg.wait_all(sy)
                            dq_di.issue(sy, lambda: sy.dma_start(out=sc_a[:], in_=x_in[tt * 128:(tt + 1) * 128, :]))
                            dq_di.issue(sy, lambda: sy.dma_start(out=sc_bf[:], in_=d_aproj.ap()[
                                tt * 128:(tt + 1) * 128, :]))
                            dq_di.issue(sy, lambda: sy.dma_start(out=stage[:, 0:D], in_=d_ffn.ap()[
                                tt * 128:(tt + 1) * 128, :]))

                        @blk.vector
                        def _(v, tt=tt):
                            dq_di.wait_all(v)
                            v.tensor_tensor(sc_a[:], sc_a[:], sc_bf[:], ALU.add)
                            v.drain()
                            v.tensor_tensor(sc_a[:], sc_a[:], stage[:, 0:D], ALU.add)
                            v.drain()
                            ch["v"].inc(v.drain())

                        @blk.gpsimd
                        def _(g, tt=tt):
                            g.wait_ge(ch["v"].sem, ch["v"].n)
                            dq_dg.issue(g, lambda: g.dma_start(out=y_out[tt * 128:(tt + 1) * 128, :], in_=sc_a[:]))

                    @blk.gpsimd
                    def _(g):
                        dq_dg.wait_all(g)
                    bar(blk)
        return nc, dbg_outs


def shard_inputs(cfg, inputs):
    S, D, H, HD, INTER, NC = cfg["S"], cfg["D"], cfg["H"], cfg["HD"], cfg["INTER"], cfg["NC"]
    TPC, HPC, OPC = S // NC, H // NC, D // NC
    RPC = INTER // NC
    x = np.ascontiguousarray(inputs["L0_attn_hidden_states"].reshape(S, D).astype(np.float32))
    maps = []
    for c in range(NC):
        m = {
            "x": np.ascontiguousarray(x[c * TPC:(c + 1) * TPC]),
            "wq": np.ascontiguousarray(inputs["L0_attn_Wq"][c * HPC * HD:(c + 1) * HPC * HD]),
            "wk": np.ascontiguousarray(inputs["L0_attn_Wk"][c * HPC * HD:(c + 1) * HPC * HD]),
            "wv": np.ascontiguousarray(inputs["L0_attn_Wv"][c * HPC * HD:(c + 1) * HPC * HD]),
            "wo": np.ascontiguousarray(inputs["L0_attn_Wo"][c * OPC:(c + 1) * OPC]),
            "c1": np.ascontiguousarray(inputs["L0_ffn_c1_w"]),
            "c2": np.ascontiguousarray(inputs["L0_ffn_c2_w"]),
            "w1": np.ascontiguousarray(inputs["L0_ffn_w1"][c * RPC:(c + 1) * RPC]),
            "w2": np.ascontiguousarray(inputs["L0_ffn_w2t"][c * RPC:(c + 1) * RPC]),
        }
        f0 = c * HPC * HD
        if f0 < cfg["P4"]:
            mq = 7.0
        elif f0 < cfg["P4"] + cfg["P6"]:
            mq = 31.0
        else:
            mq = 127.0
        qp = np.zeros((128, 2), np.float32)
        qp[:, 0] = 1.0 / mq
        qp[:, 1] = mq
        m["qparam"] = qp
        maps.append(m)
    return maps


def assemble(cfg, results):
    S, D = cfg["S"], cfg["D"]
    return np.concatenate([r["y"] for r in results], axis=0).reshape(1, S, D)


_CACHE = {}


def kernel(L0_attn_hidden_states, L0_attn_Wq, L0_attn_Wk, L0_attn_Wv, L0_attn_Wo,
           L0_ffn_c1_w, L0_ffn_c2_w, L0_ffn_w1, L0_ffn_w2t):
    cfg = dict(CFG_FULL)
    cfg["upto"] = "all"
    inputs = {
        "L0_attn_hidden_states": np.asarray(L0_attn_hidden_states),
        "L0_attn_Wq": np.asarray(L0_attn_Wq),
        "L0_attn_Wk": np.asarray(L0_attn_Wk),
        "L0_attn_Wv": np.asarray(L0_attn_Wv),
        "L0_attn_Wo": np.asarray(L0_attn_Wo),
        "L0_ffn_c1_w": np.asarray(L0_ffn_c1_w),
        "L0_ffn_c2_w": np.asarray(L0_ffn_c2_w),
        "L0_ffn_w1": np.asarray(L0_ffn_w1),
        "L0_ffn_w2t": np.asarray(L0_ffn_w2t),
    }
    if "nc" not in _CACHE:
        _CACHE["nc"] = build_nc(cfg)
    nc, _dbg = _CACHE["nc"]
    maps = shard_inputs(cfg, inputs)
    res = run_bass_kernel_spmd(nc, maps, core_ids=list(range(cfg["NC"])))
    return assemble(cfg, res.results)


# version 10
# speedup vs baseline: 1.1719x; 1.1719x over previous
import numpy as np
import ml_dtypes
from contextlib import ExitStack

import concourse.bass as bass
import concourse.mybir as mybir
from concourse.bass_utils import run_bass_kernel_spmd

F32 = mybir.dt.float32
BF16 = mybir.dt.bfloat16
I32 = mybir.dt.int32
U32 = mybir.dt.uint32
ALU = mybir.AluOpType
AF = mybir.ActivationFunctionType
AXX = mybir.AxisListType.X

CFG_SMALL = dict(S=1024, D=1024, H=8, HD=128, INTER=1024, RANK=16,
                 P4=512, P6=256, P8=256, NC=8, debug=True)
CFG_FULL = dict(S=8192, D=4096, H=32, HD=128, INTER=14336, RANK=16,
                P4=2048, P6=1024, P8=1024, NC=8, debug=False)

LN2PI_C1 = 6.28125
LN2PI_C2 = float(np.float32(2 * np.pi - 6.28125))
LN2PI_C3 = float(2 * np.pi - 6.28125 - float(np.float32(2 * np.pi - 6.28125)))
SOFTMAX_SHIFT = 28.0


class Chain:
    def __init__(self, stack, nc, name):
        self.sem = stack.enter_context(nc.semaphore(name))
        self.n = 0

    def inc(self, inst, k=1):
        inst.then_inc(self.sem, k)
        self.n += k
        return self.n

    def dma(self, inst):
        return self.inc(inst, 16)


class DQ:

    def __init__(self, stack, nc, name, k=4):
        self.sems = [stack.enter_context(nc.semaphore(f"{name}{i}")) for i in range(k)]
        self.cnt = [0] * k
        self.i = 0
        self.k = k

    def issue(self, eng, inst_fn):
        s = self.i % self.k
        if self.cnt[s]:
            eng.wait_ge(self.sems[s], 16 * self.cnt[s])
        inst = inst_fn()
        inst.then_inc(self.sems[s], 16)
        self.cnt[s] += 1
        self.i += 1
        return (self.sems[s], 16 * self.cnt[s])

    def wait_all(self, eng):
        for s in range(self.k):
            if self.cnt[s]:
                eng.wait_ge(self.sems[s], 16 * self.cnt[s])


def regions(cfg, ncols):
    out = []
    for (st, w, maxq) in [(0, cfg["P4"], 7.0),
                          (cfg["P4"], cfg["P6"], 31.0),
                          (cfg["P4"] + cfg["P6"], cfg["P8"], 127.0)]:
        if st < ncols:
            out.append((st, min(w, ncols - st), maxq))
    return out


def emit_quant(v, x_f32, out_bf, gmax, qf, cfg, ncols):
    for (st, w, maxq) in regions(cfg, ncols):
        g0, gw = st // 32, w // 32
        v.tensor_reduce(gmax[:, g0:g0 + gw].unsqueeze(2),
                        x_f32[:, st:st + w].rearrange("p (g r) -> p g r", r=32),
                        AXX, ALU.max, apply_absolute_value=True)
        v.drain()
        v.tensor_scalar(gmax[:, g0:g0 + gw], gmax[:, g0:g0 + gw],
                        float(1.0 / maxq), 1e-12, ALU.mult, ALU.max)
        v.drain()
    GW = ncols // 32
    for (st, w, maxq) in regions(cfg, ncols):
        g0, gw = st // 32, w // 32
        v.reciprocal(gmax[:, GW + g0:GW + g0 + gw], gmax[:, g0:g0 + gw])
        v.drain()
        v.tensor_tensor(qf[:, st:st + w].rearrange("p (g r) -> p g r", r=32),
                        x_f32[:, st:st + w].rearrange("p (g r) -> p g r", r=32),
                        gmax[:, GW + g0:GW + g0 + gw].unsqueeze(2).broadcast_to([128, gw, 32]),
                        ALU.mult)
        v.drain()
    v.tensor_copy(qf[:, 0:ncols].bitcast(I32), qf[:, 0:ncols])
    v.drain()
    v.tensor_copy(qf[:, 0:ncols], qf[:, 0:ncols].bitcast(I32))
    v.drain()
    for (st, w, maxq) in regions(cfg, ncols):
        g0, gw = st // 32, w // 32
        v.tensor_tensor(out_bf[:, st:st + w].rearrange("p (g r) -> p g r", r=32),
                        qf[:, st:st + w].rearrange("p (g r) -> p g r", r=32),
                        gmax[:, g0:g0 + gw].unsqueeze(2).broadcast_to([128, gw, 32]),
                        ALU.mult)
        v.drain()


def build_nc(cfg):
    S, D, H, HD = cfg["S"], cfg["D"], cfg["H"], cfg["HD"]
    INTER, RANK, NC = cfg["INTER"], cfg["RANK"], cfg["NC"]
    TPC = S // NC
    HPC = H // NC
    NFC = D // 128
    NTT = TPC // 128
    NQT = S // 512
    RB = INTER // 128
    RB_PC = RB // NC
    NBLK = INTER // 256
    NOC = NFC
    OPC = D // NC
    NOCL = OPC // 128
    dbg = cfg["debug"]
    upto = cfg.get("upto", "all")

    nc = bass.Bass("TRN2", target_bir_lowering=False, debug=False)

    x_in = nc.dram_tensor("x", [TPC, D], F32, kind="ExternalInput").ap()
    wq_in = nc.dram_tensor("wq", [HPC * HD, D], F32, kind="ExternalInput").ap()
    wk_in = nc.dram_tensor("wk", [HPC * HD, D], F32, kind="ExternalInput").ap()
    wv_in = nc.dram_tensor("wv", [HPC * HD, D], F32, kind="ExternalInput").ap()
    wo_in = nc.dram_tensor("wo", [OPC, D], F32, kind="ExternalInput").ap()
    c1_in = nc.dram_tensor("c1", [RANK, D], F32, kind="ExternalInput").ap()
    c2_in = nc.dram_tensor("c2", [INTER, RANK], F32, kind="ExternalInput").ap()
    w1_in = nc.dram_tensor("w1", [RB_PC * 128, D], F32, kind="ExternalInput").ap()
    w2_in = nc.dram_tensor("w2", [RB_PC * 128, D], F32, kind="ExternalInput").ap()
    qp_in = nc.dram_tensor("qparam", [128, 2], F32, kind="ExternalInput").ap()
    y_out = nc.dram_tensor("y", [TPC, D], F32, kind="ExternalOutput").ap()

    d_hqT_in = nc.dram_tensor("d_hqT_in", [D, TPC], BF16)
    d_hqT = nc.dram_tensor("d_hqT", [NC * D, TPC], BF16)
    d_qwoT_in = nc.dram_tensor("d_qwoT_in", [NOCL * 128, NFC * 128], BF16)
    d_qwoT = nc.dram_tensor("d_qwoT", [NC * NOCL * 128, NFC * 128], BF16)
    d_qo_in = nc.dram_tensor("d_qo_in", [HPC * S, HD], BF16)
    d_qo = nc.dram_tensor("d_qo", [H * S, HD], BF16)
    d_w1t_in = nc.dram_tensor("d_w1t_in", [RB_PC * 128, NFC * 128], BF16)
    d_w1t = nc.dram_tensor("d_w1t", [RB * 128, NFC * 128], BF16)
    d_w2_in = nc.dram_tensor("d_w2_in", [RB_PC * 128, D], BF16)
    d_w2 = nc.dram_tensor("d_w2", [RB * 128, D], BF16)
    d_aproj = nc.dram_tensor("d_aproj", [TPC, D], BF16)
    d_qostage = nc.dram_tensor("d_qostage", [(D // 128) * (S // NC), 128], BF16)
    d_xfT = nc.dram_tensor("d_xfT", [D, TPC], BF16)

    dbg_outs = {}

    def dout(name, shape, dt=F32):
        if dbg:
            dbg_outs[name] = nc.dram_tensor(name, shape, dt, kind="ExternalOutput").ap()
            return dbg_outs[name]
        return None

    o_hq = dout("o_hq", [TPC, D], BF16)
    o_hqT = dout("o_hqT", [NC * D, TPC], BF16)
    dout("o_qTr", [128, S], BF16)
    dout("o_kTr", [128, S], BF16)
    dout("o_vtm", [128, S], BF16)
    dout("o_qo", [H * S, HD], BF16)
    dout("o_idx", [128, (TPC // 128) * (INTER // 256)], F32)
    dout("o_ffn", [TPC, D], BF16)
    dout("o_ht", [128, (INTER // 128) * min(512, TPC)], BF16)
    dout("o_aproj", [TPC, D], BF16)
    dout("o_xfT", [D, TPC], BF16)

    stack = ExitStack()
    with stack:
        sb = lambda name, shape, dt: stack.enter_context(nc.sbuf_tensor(name, shape, dt))
        c_id_bf = sb("c_id_bf", [128, 128], BF16)
        c_id_f = sb("c_id_f", [128, 128], F32)
        c_ones_bf = sb("c_ones_bf", [128, 1], BF16)
        c_ones_row = sb("c_ones_row", [1, 128], BF16)
        c_iota_lo = sb("c_iota_lo", [1, 128], BF16)
        c_iota_hi = sb("c_iota_hi", [1, 128], BF16)
        c_negC = sb("c_negC", [128, 1], F32)

        stage = sb("stage", [128, D], BF16)
        sc_a = sb("sc_a", [128, D], F32)
        sc_g = sb("sc_g", [128, 2 * (D // 32)], F32)
        sc_bf = sb("sc_bf", [128, D], BF16)
        sc_st = sb("sc_st", [128, 16], F32)
        qp_inv = sb("qp_inv", [128, 2], F32)

        stk2 = ExitStack()
        sb2 = lambda name, shape, dt: stk2.enter_context(nc.sbuf_tensor(name, shape, dt))
        c_masks = sb2("c_masks", [128, 4 * 512], BF16)
        c_cos = sb2("c_cos", [128, S], BF16)
        c_ssin = sb2("c_ssin", [128, S], BF16)
        sc_b = sb2("sc_b", [128, D], F32)
        sc_bf2 = sb2("sc_bf2", [128, D], BF16)

        prestk = ExitStack()
        ps_t0h = prestk.enter_context(nc.psum_tensor("ps_t0", [128, 128], BF16))
        ps_t1h = prestk.enter_context(nc.psum_tensor("ps_t1", [128, 128], BF16))
        ps_t0 = ps_t0h[:]
        ps_t1 = ps_t1h[:]

        ch = {n: Chain(stack, nc, n) for n in
              ["v", "a", "t", "g", "cc", "p1", "p2", "p3", "p4"]}
        dq_di = DQ(stack, nc, "di", 8)
        dq_dg = DQ(stack, nc, "dg", 8)

        def bar(blk, engines=("sync", "gpsimd", "vector", "scalar", "tensor"),
                skip_cc=None):
            emap = {"sync": (blk.sync, None), "gpsimd": (blk.gpsimd, ch["g"]),
                    "vector": (blk.vector, ch["v"]), "scalar": (blk.scalar, ch["a"]),
                    "tensor": (blk.tensor, ch["t"])}
            for e in engines:
                dec, c = emap[e]
                if c is not None:
                    @dec
                    def _(eng, c=c):
                        c.inc(eng.drain())
            counts = [(ch[k].sem, ch[k].n) for k in ["v", "a", "t", "g"]]
            counts.append((ch["cc"].sem, (skip_cc - 2) if skip_cc else ch["cc"].n))
            counts += [(dq_di.sems[s], 16 * dq_di.cnt[s]) for s in range(dq_di.k)]
            counts += [(dq_dg.sems[s], 16 * dq_dg.cnt[s]) for s in range(dq_dg.k)]
            for e in engines:
                dec, _c = emap[e]
                @dec
                def _(eng, counts=counts):
                    for sem, n in counts:
                        if n:
                            eng.wait_ge(sem, n)

        with nc.Block() as blk:
            @blk.gpsimd
            def _(g):
                g.iota(sc_a[:, 0:128], pattern=[[1, 128]], base=0, channel_multiplier=0,
                       allow_small_or_imprecise_dtypes=True)
                g.iota(sc_a[:, 128:129], pattern=[[0, 1]], base=0, channel_multiplier=1,
                       allow_small_or_imprecise_dtypes=True)
                g.iota(sc_b[:, 0:512], pattern=[[1, 512]], base=0, channel_multiplier=-1,
                       allow_small_or_imprecise_dtypes=True)
                g.iota(sc_st[:, 0:1], pattern=[[0, 1]], base=0, channel_multiplier=1,
                       allow_small_or_imprecise_dtypes=True)
                dq_dg.issue(g, lambda: g.dma_start(out=qp_inv[:], in_=qp_in[:, :]))
                ch["g"].inc(g.drain())

            @blk.vector
            def _(v):
                v.wait_ge(ch["g"].sem, ch["g"].n)
                dq_dg.wait_all(v)
                v.tensor_scalar(c_id_f[:], sc_a[:, 0:128], sc_a[:, 128:129], None, ALU.is_equal)
                v.drain()
                v.tensor_copy(c_id_bf[:], c_id_f[:])
                v.memset(c_ones_bf[:], 1.0)
                v.memset(c_negC[:], -SOFTMAX_SHIFT)
                v.memset(c_ones_row[:], 1.0)
                for j in range(4):
                    v.tensor_scalar(c_masks[:, j * 512:(j + 1) * 512], sc_b[:, 0:512],
                                    float(j * 128), None, ALU.is_ge)
                v.drain()
                v.tensor_copy(c_iota_lo[:], sc_a[0:1, 0:128])
                v.drain()
                v.tensor_scalar(c_iota_hi[:], sc_a[0:1, 0:128], 128.0, None, ALU.add)
                v.drain()
                v.tensor_scalar(sc_st[:, 1:2], sc_st[:, 0:1], 64.0, -64.0, ALU.is_ge, ALU.mult)
                v.drain()
                v.tensor_tensor(sc_st[:, 2:3], sc_st[:, 0:1], sc_st[:, 1:2], ALU.add)
                v.drain()
                ch["v"].inc(v.drain())

            @blk.scalar
            def _(s):
                s.wait_ge(ch["v"].sem, ch["v"].n)
                s.activation(out=sc_st[:, 3:4], in_=sc_st[:, 2:3], func=AF.Exp,
                             bias=0.0, scale=float(-np.log(10000.0) / 64.0))
                ch["a"].inc(s.drain())
            bar(blk)

            w_rp = min(S, D)
            for rp in range(S // w_rp):
                lo, hi = rp * w_rp, (rp + 1) * w_rp

                @blk.gpsimd
                def _(g, lo=lo):
                    g.wait_ge(ch["v"].sem, ch["v"].n)
                    g.wait_ge(ch["a"].sem, ch["a"].n)
                    g.iota(sc_a[:, 0:w_rp], pattern=[[1, w_rp]], base=lo,
                           channel_multiplier=0, allow_small_or_imprecise_dtypes=True)
                    ch["g"].inc(g.drain())

                @blk.vector
                def _(v):
                    v.wait_ge(ch["g"].sem, ch["g"].n)
                    v.tensor_scalar(sc_b[:, 0:w_rp], sc_a[:, 0:w_rp], sc_st[:, 3:4], None, ALU.mult)
                    v.drain()
                    v.tensor_scalar(sc_a[:, 0:w_rp], sc_b[:, 0:w_rp],
                                    float(1.0 / (2 * np.pi)), None, ALU.mult)
                    v.drain()
                    v.tensor_copy(sc_a[:, 0:w_rp].bitcast(I32), sc_a[:, 0:w_rp])
                    v.drain()
                    v.tensor_copy(sc_a[:, 0:w_rp], sc_a[:, 0:w_rp].bitcast(I32))
                    v.drain()
                    for cc_ in (LN2PI_C1, LN2PI_C2, LN2PI_C3):
                        v.scalar_tensor_tensor(sc_b[:, 0:w_rp], sc_a[:, 0:w_rp],
                                               float(-cc_), sc_b[:, 0:w_rp],
                                               ALU.mult, ALU.add)
                        v.drain()
                    v.tensor_scalar(sc_a[:, 0:w_rp], sc_b[:, 0:w_rp], float(np.pi),
                                    float(-2 * np.pi), ALU.is_gt, ALU.mult)
                    v.drain()
                    v.tensor_tensor(sc_b[:, 0:w_rp], sc_b[:, 0:w_rp], sc_a[:, 0:w_rp], ALU.add)
                    v.drain()
                    v.tensor_scalar(sc_a[:, 0:w_rp], sc_b[:, 0:w_rp], float(-np.pi),
                                    float(2 * np.pi), ALU.is_lt, ALU.mult)
                    v.drain()
                    v.tensor_tensor(sc_b[:, 0:w_rp], sc_b[:, 0:w_rp], sc_a[:, 0:w_rp], ALU.add)
                    v.drain()
                    ch["v"].inc(v.drain())

                @blk.scalar
                def _(s):
                    s.wait_ge(ch["v"].sem, ch["v"].n)
                    s.activation(out=sc_a[:, 0:w_rp], in_=sc_b[:, 0:w_rp], func=AF.Sin)
                    ch["a"].inc(s.drain())

                @blk.vector
                def _(v, lo=lo, hi=hi):
                    v.wait_ge(ch["a"].sem, ch["a"].n)
                    v.tensor_scalar(c_ssin[0:64, lo:hi], sc_a[0:64, 0:w_rp], -1.0, None, ALU.mult)
                    v.drain()
                    v.tensor_copy(c_ssin[64:128, lo:hi], sc_a[64:128, 0:w_rp])
                    v.drain()
                    v.tensor_scalar(sc_b[:, 0:w_rp], sc_b[:, 0:w_rp], float(np.pi / 2),
                                    None, ALU.add)
                    v.drain()
                    v.tensor_scalar(sc_a[:, 0:w_rp], sc_b[:, 0:w_rp], float(np.pi),
                                    float(-2 * np.pi), ALU.is_gt, ALU.mult)
                    v.drain()
                    v.tensor_tensor(sc_b[:, 0:w_rp], sc_b[:, 0:w_rp], sc_a[:, 0:w_rp], ALU.add)
                    v.drain()
                    ch["v"].inc(v.drain())

                @blk.scalar
                def _(s):
                    s.wait_ge(ch["v"].sem, ch["v"].n)
                    s.activation(out=sc_a[:, 0:w_rp], in_=sc_b[:, 0:w_rp], func=AF.Sin)
                    ch["a"].inc(s.drain())

                @blk.vector
                def _(v, lo=lo, hi=hi):
                    v.wait_ge(ch["a"].sem, ch["a"].n)
                    v.tensor_copy(c_cos[:, lo:hi], sc_a[:, 0:w_rp])
                    v.drain()
                    ch["v"].inc(v.drain())
            bar(blk)

            for tt in range(NTT):
                @blk.sync
                def _(sy, tt=tt):
                    for c in ch.values():
                        if c.n:
                            sy.wait_ge(c.sem, c.n)
                    dq_di.wait_all(sy)
                    dq_dg.wait_all(sy)
                    dq_di.issue(sy, lambda: sy.dma_start(out=sc_a[:], in_=x_in[tt * 128:(tt + 1) * 128, :]))

                @blk.vector
                def _(v, tt=tt):
                    dq_di.wait_all(v)
                    v.reduce_sum(sc_st[:, 0:1], sc_a[:], AXX)
                    v.drain()
                    ch["v"].inc(v.drain())

                @blk.scalar
                def _(s, tt=tt):
                    dq_di.wait_all(s)
                    s.activation(out=sc_b[:], in_=sc_a[:], func=AF.Square,
                                 accum_out=sc_st[:, 1:2])
                    ch["a"].inc(s.drain())

                @blk.vector
                def _(v, tt=tt):
                    v.wait_ge(ch["a"].sem, ch["a"].n)
                    v.wait_ge(ch["v"].sem, ch["v"].n)
                    v.tensor_scalar(sc_st[:, 2:3], sc_st[:, 0:1], float(1.0 / D), None, ALU.mult)
                    v.drain()
                    v.tensor_tensor(sc_st[:, 3:4], sc_st[:, 2:3], sc_st[:, 2:3], ALU.mult)
                    v.drain()
                    v.tensor_scalar(sc_st[:, 4:5], sc_st[:, 1:2], float(1.0 / D), None, ALU.mult)
                    v.drain()
                    v.tensor_tensor(sc_st[:, 4:5], sc_st[:, 4:5], sc_st[:, 3:4], ALU.subtract)
                    v.drain()
                    v.tensor_scalar(sc_st[:, 4:5], sc_st[:, 4:5], 1e-5, None, ALU.add)
                    v.drain()
                    ch["v"].inc(v.drain())

                @blk.scalar
                def _(s, tt=tt):
                    s.wait_ge(ch["v"].sem, ch["v"].n)
                    s.activation(out=sc_st[:, 5:6], in_=sc_st[:, 4:5], func=AF.Sqrt)
                    ch["a"].inc(s.drain())

                @blk.vector
                def _(v, tt=tt):
                    v.wait_ge(ch["a"].sem, ch["a"].n)
                    v.reciprocal(sc_st[:, 6:7], sc_st[:, 5:6])
                    v.drain()
                    v.tensor_tensor(sc_st[:, 7:8], sc_st[:, 2:3], sc_st[:, 6:7], ALU.mult)
                    v.drain()
                    v.tensor_scalar(sc_st[:, 7:8], sc_st[:, 7:8], -1.0, None, ALU.mult)
                    v.drain()
                    ch["v"].inc(v.drain())

                @blk.scalar
                def _(s, tt=tt):
                    s.wait_ge(ch["v"].sem, ch["v"].n)
                    s.activation(out=sc_b[:], in_=sc_a[:], func=AF.Identity,
                                 bias=sc_st[:, 7:8], scale=sc_st[:, 6:7])
                    ch["a"].inc(s.drain())

                @blk.vector
                def _(v, tt=tt):
                    v.wait_ge(ch["a"].sem, ch["a"].n)
                    emit_quant(v, sc_b, sc_bf, sc_g, sc_a, cfg, D)
                    ch["v"].inc(v.drain())

                if dbg:
                    @blk.gpsimd
                    def _(g, tt=tt):
                        g.wait_ge(ch["v"].sem, ch["v"].n)
                        dq_dg.issue(g, lambda: g.dma_start(out=o_hq[tt * 128:(tt + 1) * 128, :],
                                                 in_=sc_bf[:]))

                for c in range(NFC):
                    @blk.tensor
                    def _(t, tt=tt, c=c):
                        if c == 0:
                            t.wait_ge(ch["v"].sem, ch["v"].n)
                        if c >= 2:
                            t.wait_ge(ch["p1"].sem, ch["p1"].n - 1)
                        t.transpose(ps_t0 if c % 2 == 0 else ps_t1,
                                    sc_bf[:, c * 128:(c + 1) * 128], c_id_bf[:])
                        ch["t"].inc(t.drain())

                    @blk.vector
                    def _(v, tt=tt, c=c):
                        v.wait_ge(ch["t"].sem, ch["t"].n)
                        v.tensor_copy(stage[:, c * 128:(c + 1) * 128],
                                      ps_t0 if c % 2 == 0 else ps_t1)
                        v.drain()
                        ch["p1"].inc(v.drain())

                @blk.gpsimd
                def _(g, tt=tt):
                    g.wait_ge(ch["p1"].sem, ch["p1"].n)
                    for cg in range(0, NFC, 8):
                        dq_dg.issue(g, lambda cg=cg: g.dma_start(
                            out=d_hqT_in.ap().rearrange("(c f) t -> c f t", f=128)[
                                cg:cg + 8, :, tt * 128:(tt + 1) * 128].transpose([1, 0, 2]),
                            in_=stage[:, cg * 128:(cg + 8) * 128].rearrange(
                                "f (c t) -> f c t", t=128)))

            @blk.gpsimd
            def _(g):
                dq_dg.wait_all(g)
                ch["cc"].inc(g.collective_compute(
                    "AllGather", ALU.bypass, replica_groups=[list(range(NC))],
                    ins=[d_hqT_in.ap().opt()], outs=[d_hqT.ap().opt()]))
                g.wait_ge(ch["cc"].sem, ch["cc"].n)
                if dbg:
                    dq_dg.issue(g, lambda: g.dma_start(out=o_hqT[:, :], in_=d_hqT[:, :]))
                dq_dg.wait_all(g)
            bar(blk)

            def emit_zero_y():
                @blk.vector
                def _(v):
                    v.memset(sc_a[:], 0.0)
                    ch["v"].inc(v.drain())

                @blk.gpsimd
                def _(g):
                    g.wait_ge(ch["v"].sem, ch["v"].n)
                    for tt in range(NTT):
                        dq_dg.issue(g, lambda: g.dma_start(out=y_out[tt * 128:(tt + 1) * 128, :], in_=sc_a[:]))
                    dq_dg.wait_all(g)

            if upto == "p0":
                emit_zero_y()
                stk2.close()
                return nc, dbg_outs

            def transpose_to_stage(n_chunks, src_bf, pst0, pst1, dst=None, dst_off=0):
                d = stage if dst is None else dst
                base = ch["p1"].n
                for c in range(n_chunks):
                    @blk.tensor
                    def _(t, c=c, base=base):
                        if c == 0:
                            t.wait_ge(ch["v"].sem, ch["v"].n)
                        t.wait_ge(ch["p1"].sem, max(base, base + c - 1))
                        t.transpose(pst0 if c % 2 == 0 else pst1,
                                    src_bf[:, c * 128:(c + 1) * 128], c_id_bf[:])
                        ch["t"].inc(t.drain())

                    @blk.vector
                    def _(v, c=c):
                        v.wait_ge(ch["t"].sem, ch["t"].n)
                        v.tensor_copy(d[:, dst_off + c * 128:dst_off + (c + 1) * 128],
                                      pst0 if c % 2 == 0 else pst1)
                        v.drain()
                        ch["p1"].inc(v.drain())

            for ocl in range(NOCL):
                @blk.sync
                def _(sy, ocl=ocl):
                    for c in ch.values():
                        if c.n:
                            sy.wait_ge(c.sem, c.n)
                    dq_di.wait_all(sy)
                    dq_dg.wait_all(sy)
                    dq_di.issue(sy, lambda: sy.dma_start(out=sc_a[:], in_=wo_in[ocl * 128:(ocl + 1) * 128, :]))

                @blk.vector
                def _(v, ocl=ocl):
                    dq_di.wait_all(v)
                    emit_quant(v, sc_a, sc_bf, sc_g, sc_b, cfg, D)
                    ch["v"].inc(v.drain())

                transpose_to_stage(NFC, sc_bf, ps_t0, ps_t1)

                @blk.gpsimd
                def _(g, ocl=ocl):
                    g.wait_ge(ch["p1"].sem, ch["p1"].n)
                    dq_dg.issue(g, lambda: g.dma_start(
                        out=d_qwoT_in.ap()[ocl * 128:(ocl + 1) * 128, :],
                        in_=stage[:]))

            @blk.gpsimd
            def _(g):
                dq_dg.wait_all(g)
                ch["cc"].inc(g.collective_compute(
                    "AllGather", ALU.bypass, replica_groups=[list(range(NC))],
                    ins=[d_qwoT_in.ap().opt()], outs=[d_qwoT.ap().opt()]))

            for rbl in range(RB_PC):
                @blk.gpsimd
                def _(g, rbl=rbl):
                    for c in ch.values():
                        if c.n:
                            g.wait_ge(c.sem, c.n)
                    dq_di.wait_all(g)
                    dq_dg.wait_all(g)
                    dq_dg.issue(g, lambda: g.dma_start(out=sc_bf[:], in_=w1_in[rbl * 128:(rbl + 1) * 128, :]))

                @blk.vector
                def _(v, rbl=rbl):
                    dq_dg.wait_all(v)
                    ch["v"].inc(v.drain())

                transpose_to_stage(NFC, sc_bf, ps_t0, ps_t1)

                @blk.gpsimd
                def _(g, rbl=rbl):
                    g.wait_ge(ch["p1"].sem, ch["p1"].n)
                    dq_dg.issue(g, lambda: g.dma_start(
                        out=d_w1t_in.ap()[rbl * 128:(rbl + 1) * 128, :],
                        in_=stage[:]))
                    dq_dg.issue(g, lambda: g.dma_start(out=sc_bf2[:],
                                             in_=w2_in[rbl * 128:(rbl + 1) * 128, :]))
                    dq_dg.wait_all(g)
                    dq_dg.issue(g, lambda: g.dma_start(
                        out=d_w2_in.ap()[rbl * 128:(rbl + 1) * 128, :],
                        in_=sc_bf2[:]))

            @blk.gpsimd
            def _(g):
                dq_dg.wait_all(g)
                ch["cc"].inc(g.collective_compute(
                    "AllGather", ALU.bypass, replica_groups=[list(range(NC))],
                    ins=[d_w1t_in.ap().opt()], outs=[d_w1t.ap().opt()]))
                ch["cc"].inc(g.collective_compute(
                    "AllGather", ALU.bypass, replica_groups=[list(range(NC))],
                    ins=[d_w2_in.ap().opt()], outs=[d_w2.ap().opt()]))
            cc_w1w2 = ch["cc"].n
            bar(blk, engines=("sync", "gpsimd", "vector", "scalar", "tensor"),
                skip_cc=cc_w1w2)

            if upto == "p1":
                emit_zero_y()
                stk2.close()
                return nc, dbg_outs

            prestk.close()
            TT1 = 256
            NP1 = S // TT1
            SCALE = float(1.0 / np.sqrt(HD))
            with ExitStack() as bstk:
                sbB = lambda name, shape, dt: bstk.enter_context(nc.sbuf_tensor(name, shape, dt))
                b_qw = sbB("b_qw", [128, 3 * D], BF16)
                b_qT = sbB("b_qT", [128, S], BF16)
                b_kT = sbB("b_kT", [128, S], BF16)
                b_v = sbB("b_v", [128, S], BF16)
                b_hq2 = sbB("b_hq2", [128, 2 * NFC * TT1], BF16)
                b_pt = sbB("b_pt", [128, 4 * 512], BF16)
                b_ob = sbB("b_ob", [128, 512], BF16)
                b_L = sbB("b_L", [1, 512], F32)
                b_rL = sbB("b_rL", [1, 512], F32)
                b_rLc = sbB("b_rLc", [128, 4], F32)
                p_qk = bstk.enter_context(nc.psum_tensor("p_qk", [128, 2 * TT1], F32))
                p_v = bstk.enter_context(nc.psum_tensor("p_v", [128, TT1], F32))
                p_sp = bstk.enter_context(nc.psum_tensor("p_sp", [128, 2048], F32))
                p_o = bstk.enter_context(nc.psum_tensor("p_o", [128, 512], F32))
                p_l = bstk.enter_context(nc.psum_tensor("p_l", [1, 512], F32))
                p_tb_ap = p_v[:, 128:192].bitcast(BF16)

                for h in range(HPC):
                    for pj, w_ext in enumerate([wq_in, wk_in, wv_in]):
                        @blk.sync
                        def _(sy, pj=pj, w_ext=w_ext, h=h):
                            for c in ch.values():
                                if c.n:
                                    sy.wait_ge(c.sem, c.n)
                            dq_di.wait_all(sy)
                            dq_dg.wait_all(sy)
                            dq_di.issue(sy, lambda: sy.dma_start(
                                out=sc_a[:], in_=w_ext[h * 128:(h + 1) * 128, :]))

                        @blk.vector
                        def _(v, pj=pj):
                            dq_di.wait_all(v)
                            emit_quant(v, sc_a, sc_bf, sc_g, sc_b, cfg, D)
                            ch["v"].inc(v.drain())

                        transpose_to_stage(NFC, sc_bf, p_sp[:, 0:64].bitcast(BF16),
                                           p_sp[:, 512:576].bitcast(BF16),
                                           dst=b_qw, dst_off=pj * D)

                    for tt in range(NP1):
                        glo = tt * TT1
                        buf = tt % 2
                        boff = buf * NFC * TT1
                        @blk.sync
                        def _(sy, glo=glo, boff=boff, tt=tt):
                            if tt < 2:
                                for c in ch.values():
                                    if c.n:
                                        sy.wait_ge(c.sem, c.n)
                                dq_di.wait_all(sy)
                                dq_dg.wait_all(sy)
                            else:
                                sy.wait_ge(ch["p2"].sem, ch["p2"].n - 1)
                            pieces = []
                            t0 = glo
                            while t0 < glo + TT1:
                                s_id = t0 // TPC
                                t1 = min(glo + TT1, (s_id + 1) * TPC)
                                pieces.append((t0, t1, s_id))
                                t0 = t1
                            for (a0, a1, s_id) in pieces:
                                for cg in range(0, NFC, 8):
                                    dq_di.issue(sy, lambda a0=a0, a1=a1, s_id=s_id, cg=cg: sy.dma_start(
                                        out=b_hq2[:, boff:boff + NFC * TT1].rearrange(
                                            "p (c t) -> p c t", t=TT1)[:, cg:cg + 8, a0 - glo:a1 - glo],
                                        in_=d_hqT.ap().rearrange("(s c f) t -> s c f t", c=NFC, f=128)[
                                            s_id, cg:cg + 8, :,
                                            a0 - s_id * TPC:a1 - s_id * TPC].transpose([1, 0, 2])))

                        @blk.tensor
                        def _(t, glo=glo, boff=boff, tt=tt):
                            dq_di.wait_all(t)
                            if tt >= 1:
                                t.wait_ge(ch["p3"].sem, ch["p3"].n)
                            for pj in range(2):
                                for c in range(NFC):
                                    t.matmul(p_qk[:, pj * TT1:(pj + 1) * TT1],
                                             b_qw[:, pj * D + c * 128:pj * D + (c + 1) * 128],
                                             b_hq2[:, boff + c * TT1:boff + (c + 1) * TT1],
                                             start=(c == 0), stop=(c == NFC - 1))
                            for sub in range(TT1 // 128):
                                for c in range(NFC):
                                    t.matmul(p_v[:, sub * 128:(sub + 1) * 128],
                                             b_hq2[:, boff + c * TT1 + sub * 128:boff + c * TT1 + (sub + 1) * 128],
                                             b_qw[:, 2 * D + c * 128:2 * D + (c + 1) * 128],
                                             start=(c == 0), stop=(c == NFC - 1))
                            ch["t"].inc(t.drain())
                            ch["p2"].inc(t.drain())

                        @blk.scalar
                        def _(s, glo=glo, tt=tt):
                            s.wait_ge(ch["t"].sem, ch["t"].n)
                            s.activation(out=b_qT[:, glo:glo + TT1], in_=p_qk[:, 0:TT1], func=AF.Identity)
                            s.activation(out=b_v[:, glo:glo + TT1], in_=p_v[:], func=AF.Identity)
                            ch["a"].inc(s.drain())

                        @blk.vector
                        def _(v, glo=glo, tt=tt):
                            v.wait_ge(ch["t"].sem, ch["t"].n)
                            v.tensor_copy(b_kT[:, glo:glo + TT1], p_qk[:, TT1:2 * TT1])
                            v.drain()
                            ch["v"].inc(v.drain())

                        @blk.gpsimd
                        def _(g, tt=tt):
                            g.wait_ge(ch["a"].sem, ch["a"].n)
                            g.wait_ge(ch["v"].sem, ch["v"].n)
                            ch["p3"].inc(g.drain())

                    t_sw = b_hq2[:, 0:S]
                    t_t1 = b_hq2[:, S:2 * S]
                    for tensbuf in (b_qT, b_kT):
                        @blk.gpsimd
                        def _(g, tensbuf=tensbuf):
                            for c in ch.values():
                                if c.n:
                                    g.wait_ge(c.sem, c.n)
                            dq_di.wait_all(g)
                            dq_dg.wait_all(g)
                            dq_dg.issue(g, lambda: g.dma_start(out=t_sw[0:64, :], in_=tensbuf[64:128, :]))
                            dq_dg.issue(g, lambda: g.dma_start(out=t_sw[64:128, :], in_=tensbuf[0:64, :]))

                        @blk.vector
                        def _(v, tensbuf=tensbuf):
                            dq_dg.wait_all(v)
                            v.tensor_tensor(t_t1[:, :], t_sw[:, :], c_ssin[:], ALU.mult)
                            v.drain()
                            v.tensor_tensor(tensbuf[:], tensbuf[:], c_cos[:], ALU.mult)
                            v.drain()
                            v.tensor_tensor(tensbuf[:], tensbuf[:], t_t1[:, :], ALU.add)
                            v.drain()
                            ch["v"].inc(v.drain())

                    if dbg and h == 0:
                        @blk.gpsimd
                        def _(g):
                            g.wait_ge(ch["v"].sem, ch["v"].n)
                            dq_dg.issue(g, lambda: g.dma_start(out=dbg_outs["o_qTr"][:, :], in_=b_qT[:]))
                            dq_dg.issue(g, lambda: g.dma_start(out=dbg_outs["o_kTr"][:, :], in_=b_kT[:]))
                            dq_dg.issue(g, lambda: g.dma_start(out=dbg_outs["o_vtm"][:, :], in_=b_v[:]))

                    for qt in range(NQT):
                        nkb = 4 * qt + 4
                        qsl = b_qT[:, qt * 512:(qt + 1) * 512]
                        t_base = ch["t"].n
                        a_base = ch["a"].n
                        v_base = ch["v"].n
                        n_diag_before = [0] * (nkb + 1)
                        for kb in range(nkb):
                            n_diag_before[kb + 1] = n_diag_before[kb] + (1 if kb >= 4 * qt else 0)

                        @blk.tensor
                        def _(t, qt=qt):
                            if qt == 0:
                                t.wait_ge(ch["v"].sem, ch["v"].n)
                            t.wait_ge(ch["p4"].sem, ch["p4"].n)
                        def emit_pv_pair(p):
                            for bb in range(2):
                                kb = 2 * p + bb
                                diag = kb >= 4 * qt

                                @blk.tensor
                                def _(t, kb=kb, p=p, bb=bb, diag=diag):
                                    if diag:
                                        t.wait_ge(ch["v"].sem, v_base + n_diag_before[kb + 1])
                                    else:
                                        t.wait_ge(ch["a"].sem, a_base + p + 1)
                                    pt = b_pt[:, (p % 2) * 1024 + bb * 512:
                                              (p % 2) * 1024 + (bb + 1) * 512]
                                    ch["t"].inc(t.matmul(p_o[:], b_v[:, kb * 128:(kb + 1) * 128], pt,
                                                         start=(kb == 0), stop=(kb == nkb - 1)))
                                    ch["t"].inc(t.matmul(p_l[:], c_ones_bf[:], pt,
                                                         start=(kb == 0), stop=(kb == nkb - 1)))

                        for p in range(nkb // 2):
                            for bb in range(2):
                                kb = 2 * p + bb

                                @blk.tensor
                                def _(t, kb=kb, p=p, bb=bb, a_base=a_base):
                                    if bb == 0 and p >= 2:
                                        t.wait_ge(ch["a"].sem, a_base + p - 1)
                                    ch["t"].inc(t.matmul(
                                        p_sp[:, (p % 2) * 1024 + bb * 512:(p % 2) * 1024 + (bb + 1) * 512],
                                        b_kT[:, kb * 128:(kb + 1) * 128], qsl,
                                        start=True, stop=True))

                            @blk.scalar
                            def _(s, p=p, t_base=t_base):
                                s.wait_ge(ch["t"].sem,
                                          t_base + (2 if p == 0 else 6 * p - 2))
                                s.activation(out=b_pt[:, (p % 2) * 1024:(p % 2 + 1) * 1024],
                                             in_=p_sp[:, (p % 2) * 1024:(p % 2 + 1) * 1024],
                                             func=AF.Exp, bias=c_negC[:, 0:1], scale=SCALE)
                                ch["a"].inc(s.drain())

                            for bb in range(2):
                                kb = 2 * p + bb
                                if kb >= 4 * qt:
                                    @blk.vector
                                    def _(v, kb=kb, p=p, bb=bb, qt=qt, a_base=a_base):
                                        v.wait_ge(ch["a"].sem, a_base + p + 1)
                                        d = kb - 4 * qt
                                        sl = b_pt[:, (p % 2) * 1024 + bb * 512:
                                                  (p % 2) * 1024 + (bb + 1) * 512]
                                        v.tensor_tensor(sl, sl,
                                                        c_masks[:, d * 512:(d + 1) * 512], ALU.mult)
                                        v.drain()
                                        ch["v"].inc(v.drain())

                            if p >= 1:
                                emit_pv_pair(p - 1)
                        emit_pv_pair(nkb // 2 - 1)

                        @blk.scalar
                        def _(s, qt=qt):
                            s.wait_ge(ch["t"].sem, ch["t"].n)
                            s.activation(out=b_ob[:], in_=p_o[:], func=AF.Identity)
                            ch["a"].inc(s.drain())

                        @blk.vector
                        def _(v, qt=qt):
                            v.wait_ge(ch["t"].sem, ch["t"].n)
                            v.tensor_copy(b_L[:], p_l[:])
                            v.drain()
                            v.reciprocal(b_rL[:], b_L[:])
                            v.drain()
                            ch["v"].inc(v.drain())

                        @blk.tensor
                        def _(t, qt=qt):
                            t.wait_ge(ch["v"].sem, ch["v"].n)
                            for i in range(4):
                                t.transpose(p_qk[:, i:i + 1],
                                            b_rL[:, i * 128:(i + 1) * 128], c_id_f[0:1, 0:1])
                            ch["t"].inc(t.drain())

                        @blk.vector
                        def _(v, qt=qt):
                            v.wait_ge(ch["t"].sem, ch["t"].n)
                            v.tensor_copy(b_rLc[:], p_qk[:, 0:4])
                            v.drain()
                            ch["v"].inc(v.drain())

                        for i in range(4):
                            @blk.tensor
                            def _(t, i=i, qt=qt):
                                t.wait_ge(ch["a"].sem, ch["a"].n)
                                if i > 0:
                                    t.wait_ge(ch["p1"].sem, ch["p1"].n)
                                t.transpose(p_tb_ap, b_ob[:, i * 128:(i + 1) * 128], c_id_bf[:])
                                ch["t"].inc(t.drain())

                            @blk.vector
                            def _(v, i=i, qt=qt, h=h):
                                v.wait_ge(ch["t"].sem, ch["t"].n)
                                v.wait_ge(ch["v"].sem, ch["v"].n)
                                v.tensor_scalar(sc_a[:, 0:128], p_tb_ap, b_rLc[:, i:i + 1],
                                                None, ALU.mult)
                                v.drain()
                                v.tensor_reduce(sc_g[:, 0:4].unsqueeze(2),
                                                sc_a[:, 0:128].rearrange("p (g r) -> p g r", r=32),
                                                AXX, ALU.max, apply_absolute_value=True)
                                v.drain()
                                v.tensor_scalar(sc_g[:, 0:4], sc_g[:, 0:4], qp_inv[:, 0:1],
                                                1e-12, ALU.mult, ALU.max)
                                v.drain()
                                v.reciprocal(sc_g[:, 4:8], sc_g[:, 0:4])
                                v.drain()
                                v.tensor_tensor(sc_b[:, 0:128].rearrange("p (g r) -> p g r", r=32),
                                                sc_a[:, 0:128].rearrange("p (g r) -> p g r", r=32),
                                                sc_g[:, 4:8].unsqueeze(2).broadcast_to([128, 4, 32]),
                                                ALU.mult)
                                v.drain()
                                v.tensor_copy(sc_b[:, 0:128].bitcast(I32), sc_b[:, 0:128])
                                v.drain()
                                v.tensor_copy(sc_b[:, 0:128], sc_b[:, 0:128].bitcast(I32))
                                v.drain()
                                v.tensor_tensor(stage[:, i * 128:(i + 1) * 128].rearrange(
                                                    "p (g r) -> p g r", r=32),
                                                sc_b[:, 0:128].rearrange("p (g r) -> p g r", r=32),
                                                sc_g[:, 0:4].unsqueeze(2).broadcast_to([128, 4, 32]),
                                                ALU.mult)
                                v.drain()
                                ch["p1"].inc(v.drain())

                        @blk.gpsimd
                        def _(g, qt=qt, h=h):
                            g.wait_ge(ch["p1"].sem, ch["p1"].n)
                            dq_dg.issue(g, lambda: g.dma_start(
                                out=d_qo_in.ap()[h * S + qt * 512:h * S + (qt + 1) * 512, :].rearrange(
                                    "(i q) d -> i q d", q=128).transpose([1, 0, 2]),
                                in_=stage[:, 0:512].rearrange("q (i d) -> q i d", d=128)))
                            dq_dg.wait_all(g)
                            ch["p4"].inc(g.drain())

                @blk.gpsimd
                def _(g):
                    for c in ch.values():
                        if c.n:
                            g.wait_ge(c.sem, c.n)
                    dq_di.wait_all(g)
                    dq_dg.wait_all(g)
                    ch["cc"].inc(g.collective_compute(
                        "AllGather", ALU.bypass, replica_groups=[list(range(NC))],
                        ins=[d_qo_in.ap().opt()], outs=[d_qo.ap().opt()]))
                    g.wait_ge(ch["cc"].sem, ch["cc"].n)
                    if dbg:
                        dq_dg.issue(g, lambda: g.dma_start(out=dbg_outs["o_qo"][:, :], in_=d_qo[:, :]))
                bar(blk)

            stk2.close()
            if upto == "b":
                emit_zero_y()
                return nc, dbg_outs
            WG = min(TPC, 512)
            NTG = TPC // WG
            d_c2t = nc.dram_tensor("d_c2t", [2 * RANK, INTER], BF16)
            d_ffn = nc.dram_tensor("d_ffn", [TPC, D], BF16)
            with ExitStack() as cfstk:
                sbCF = lambda n_, s_, d_: cfstk.enter_context(nc.sbuf_tensor(n_, s_, d_))
                cnegT = sbCF("cnegT", [NBLK, TPC], BF16)
                with ExitStack() as cstk:
                    sbC = lambda n_, s_, d_: cstk.enter_context(nc.sbuf_tensor(n_, s_, d_))
                    c_c1t = sbC("c_c1t", [128, 2 * NFC * RANK], BF16)
                    cq_oT = sbC("cq_oT", [128, NFC * TPC], BF16)
                    cs_b = sbC("cs_b", [128, D], F32)
                    cwo = sbC("cwo", [128, 2 * NFC * 128], BF16)
                    cx2g = sbC("cx2g", [128, (WG // 128) * D], BF16)
                    cxT = sbC("cxT", [128, 2 * NFC * 128], BF16)
                    cr1 = sbC("cr1", [16, 3 * TPC], F32)
                    cc2 = sbC("cc2", [16, 2 * 1024], BF16)
                    cidxf = sbC("cidxf", [128, NTT * NBLK], F32)
                    cidxu = sbC("cidxu", [128, NBLK], U32)
                    cm8 = sbC("cm8", [128, 8], F32)
                    cmi = sbC("cmi", [128, 8], U32)
                    cps_tb = cstk.enter_context(nc.psum_tensor("cps_tb", [128, 128], BF16))
                    cps_t = cstk.enter_context(nc.psum_tensor("cps_t", [128, 512], F32))
                    cps_c = cstk.enter_context(nc.psum_tensor("cps_c", [128, 512], F32))
                    cps_r = cstk.enter_context(nc.psum_tensor("cps_r", [16, 512], F32))
                    cps_i = cstk.enter_context(nc.psum_tensor("cps_i", [128, 512], F32))

                    @blk.sync
                    def _(sy):
                        for c in ch.values():
                            if c.n:
                                sy.wait_ge(c.sem, c.n)
                        dq_di.wait_all(sy)
                        dq_dg.wait_all(sy)
                        dq_di.issue(sy, lambda: sy.dma_start(out=cs_b[0:RANK, :], in_=c1_in[:, :]))

                    for c in range(NFC):
                        @blk.tensor
                        def _(t, c=c):
                            dq_di.wait_all(t)
                            if c >= 1:
                                t.wait_ge(ch["p1"].sem, ch["p1"].n)
                            t.transpose(cps_t[:, 0:RANK], cs_b[0:RANK, c * 128:(c + 1) * 128],
                                        c_id_f[0:RANK, 0:RANK])
                            ch["t"].inc(t.drain())

                        @blk.vector
                        def _(v, c=c):
                            v.wait_ge(ch["t"].sem, ch["t"].n)
                            v.tensor_copy(sc_a[:, c * RANK:(c + 1) * RANK], cps_t[:, 0:RANK])
                            v.drain()
                            ch["p1"].inc(v.drain())

                    @blk.vector
                    def _(v):
                        v.wait_ge(ch["p1"].sem, ch["p1"].n)
                        v.tensor_copy(c_c1t[:, 0:NFC * RANK], sc_a[:, 0:NFC * RANK])
                        v.drain()
                        v.tensor_tensor(c_c1t[:, NFC * RANK:2 * NFC * RANK], sc_a[:, 0:NFC * RANK],
                                        c_c1t[:, 0:NFC * RANK], ALU.subtract)
                        v.drain()
                        ch["v"].inc(v.drain())

                    for rt in range(INTER // 128):
                        @blk.sync
                        def _(sy, rt=rt):
                            sy.wait_ge(ch["v"].sem, ch["v"].n)
                            if rt > 0:
                                sy.wait_ge(ch["p2"].sem, ch["p2"].n)
                            dq_di.issue(sy, lambda: sy.dma_start(out=cs_b[:, 0:RANK],
                                                      in_=c2_in[rt * 128:(rt + 1) * 128, :]))

                        @blk.tensor
                        def _(t, rt=rt):
                            dq_di.wait_all(t)
                            if rt > 0:
                                t.wait_ge(ch["p3"].sem, ch["p3"].n)
                            t.transpose(cps_t[0:RANK, 0:128], cs_b[:, 0:RANK], c_id_f[:])
                            ch["t"].inc(t.drain())

                        @blk.vector
                        def _(v, rt=rt):
                            v.wait_ge(ch["t"].sem, ch["t"].n)
                            v.tensor_copy(sc_a[0:RANK, 0:128], cps_t[0:RANK, 0:128])
                            v.drain()
                            ch["p3"].inc(v.drain())
                            v.tensor_copy(sc_bf[0:RANK, 0:128], sc_a[0:RANK, 0:128])
                            v.drain()
                            v.tensor_tensor(sc_bf[0:RANK, 128:256], sc_a[0:RANK, 0:128],
                                            sc_bf[0:RANK, 0:128], ALU.subtract)
                            v.drain()
                            ch["v"].inc(v.drain())

                        @blk.gpsimd
                        def _(g, rt=rt):
                            g.wait_ge(ch["v"].sem, ch["v"].n)
                            dq_dg.issue(g, lambda: g.dma_start(out=d_c2t.ap()[0:RANK, rt * 128:(rt + 1) * 128],
                                                     in_=sc_bf[0:RANK, 0:128]))
                            dq_dg.issue(g, lambda: g.dma_start(out=d_c2t.ap()[RANK:2 * RANK, rt * 128:(rt + 1) * 128],
                                                     in_=sc_bf[0:RANK, 128:256]))
                            dq_dg.wait_all(g)
                            ch["p2"].inc(g.drain())
                    bar(blk)

                    @blk.gpsimd
                    def _(g):
                        pid = g.partition_id()
                        dq_dg.issue(g, lambda: g.dma_start(
                            out=d_qostage.ap().rearrange("(ic t) d -> ic t d", t=TPC),
                            in_=d_qo.ap().rearrange("(ic s) d -> ic s d", s=S)[
                                :, bass.ds(pid * TPC, TPC), :]))

                    @blk.sync
                    def _(sy):
                        dq_dg.wait_all(sy)
                        for ic in range(NFC):
                            dq_di.issue(sy, lambda: sy.dma_start(
                                out=cq_oT[:, ic * TPC:(ic + 1) * TPC],
                                in_=d_qostage.ap()[ic * TPC:(ic + 1) * TPC, :],
                                transpose=True))

                    for tg in range(NTG):
                        for oc in range(NOC):
                            @blk.sync
                            def _(sy, tg=tg, oc=oc):
                                if tg == 0 and oc == 0:
                                    for c in ch.values():
                                        if c.n:
                                            sy.wait_ge(c.sem, c.n)
                                    dq_di.wait_all(sy)
                                    dq_dg.wait_all(sy)
                                if oc >= 2:
                                    sy.wait_ge(ch["p1"].sem, ch["p1"].n - 1)
                                s2, ocl = oc // NOCL, oc % NOCL
                                dq_di.issue(sy, lambda: sy.dma_start(
                                    out=cwo[:, (oc % 2) * NFC * 128:(oc % 2 + 1) * NFC * 128],
                                    in_=d_qwoT.ap()[(s2 * NOCL + ocl) * 128:
                                                    (s2 * NOCL + ocl + 1) * 128, :]))

                            @blk.tensor
                            def _(t, tg=tg, oc=oc):
                                dq_di.wait_all(t)
                                if oc >= 2:
                                    t.wait_ge(ch["a"].sem, ch["a"].n - 1)
                                for c in range(NFC):
                                    t.matmul(cps_c[:, 0:WG] if oc % 2 == 0 else cps_t[:, 0:WG],
                                             cwo[:, (oc % 2) * NFC * 128 + c * 128:
                                                 (oc % 2) * NFC * 128 + (c + 1) * 128],
                                             cq_oT[:, c * TPC + tg * WG:c * TPC + tg * WG + WG],
                                             start=(c == 0), stop=(c == NFC - 1))
                                ch["t"].inc(t.drain())
                                ch["p1"].inc(t.drain())

                            @blk.scalar
                            def _(s, tg=tg, oc=oc):
                                s.wait_ge(ch["t"].sem, ch["t"].n)
                                s.activation(out=stage[:, 0:WG] if oc % 2 == 0 else stage[:, WG:2 * WG],
                                             in_=cps_c[:, 0:WG] if oc % 2 == 0 else cps_t[:, 0:WG],
                                             func=AF.Identity)
                                ch["a"].inc(s.drain())

                            for i in range(WG // 128):
                                @blk.tensor
                                def _(t, oc=oc, i=i):
                                    t.wait_ge(ch["a"].sem, ch["a"].n)
                                    t.wait_ge(ch["p2"].sem, ch["p2"].n)
                                    t.wait_ge(ch["p4"].sem, ch["p4"].n)
                                    t.transpose(cps_tb[:],
                                                (stage[:, 0:WG] if oc % 2 == 0 else
                                                 stage[:, WG:2 * WG])[:, i * 128:(i + 1) * 128],
                                                c_id_bf[:])
                                    ch["t"].inc(t.drain())

                                @blk.vector
                                def _(v, oc=oc, i=i):
                                    v.wait_ge(ch["t"].sem, ch["t"].n)
                                    v.tensor_copy(cx2g[:, i * D + oc * 128:i * D + (oc + 1) * 128],
                                                  cps_tb[:])
                                    v.drain()
                                    ch["p2"].inc(v.drain())

                        @blk.gpsimd
                        def _(g, tg=tg):
                            g.wait_ge(ch["p2"].sem, ch["p2"].n)
                            dq_dg.issue(g, lambda: g.dma_start(
                                out=d_aproj.ap()[tg * WG:(tg + 1) * WG, :].rearrange(
                                    "(i p) d -> i p d", p=128).transpose([1, 0, 2]),
                                in_=cx2g[:].rearrange("p (i d) -> p i d", d=D)))
                            dq_dg.wait_all(g)
                            ch["p4"].inc(g.drain())

                        for j in range(WG // 128):
                            @blk.sync
                            def _(sy, tg=tg, j=j):
                                sy.wait_ge(ch["p4"].sem, ch["p4"].n)
                                sy.wait_ge(ch["v"].sem, ch["v"].n)
                                sy.wait_ge(ch["t"].sem, ch["t"].n)
                                dq_di.issue(sy, lambda: sy.dma_start(
                                    out=sc_a[:], in_=x_in[tg * WG + j * 128:tg * WG + (j + 1) * 128, :]))

                            @blk.vector
                            def _(v, tg=tg, j=j):
                                dq_di.wait_all(v)
                                v.tensor_tensor(cs_b[:], sc_a[:], cx2g[:, j * D:(j + 1) * D], ALU.add)
                                v.drain()
                                v.reduce_sum(sc_st[:, 0:1], cs_b[:], AXX)
                                v.drain()
                                ch["v"].inc(v.drain())

                            @blk.scalar
                            def _(s):
                                s.wait_ge(ch["v"].sem, ch["v"].n)
                                s.activation(out=sc_a[:], in_=cs_b[:], func=AF.Square,
                                             accum_out=sc_st[:, 1:2])
                                ch["a"].inc(s.drain())

                            @blk.vector
                            def _(v):
                                v.wait_ge(ch["a"].sem, ch["a"].n)
                                v.tensor_scalar(sc_st[:, 2:3], sc_st[:, 0:1], float(1.0 / D), None, ALU.mult)
                                v.drain()
                                v.tensor_tensor(sc_st[:, 3:4], sc_st[:, 2:3], sc_st[:, 2:3], ALU.mult)
                                v.drain()
                                v.tensor_scalar(sc_st[:, 4:5], sc_st[:, 1:2], float(1.0 / D), None, ALU.mult)
                                v.drain()
                                v.tensor_tensor(sc_st[:, 4:5], sc_st[:, 4:5], sc_st[:, 3:4], ALU.subtract)
                                v.drain()
                                v.tensor_scalar(sc_st[:, 4:5], sc_st[:, 4:5], 1e-5, None, ALU.add)
                                v.drain()
                                ch["v"].inc(v.drain())

                            @blk.scalar
                            def _(s):
                                s.wait_ge(ch["v"].sem, ch["v"].n)
                                s.activation(out=sc_st[:, 5:6], in_=sc_st[:, 4:5], func=AF.Sqrt)
                                ch["a"].inc(s.drain())

                            @blk.vector
                            def _(v):
                                v.wait_ge(ch["a"].sem, ch["a"].n)
                                v.reciprocal(sc_st[:, 6:7], sc_st[:, 5:6])
                                v.drain()
                                v.tensor_tensor(sc_st[:, 7:8], sc_st[:, 2:3], sc_st[:, 6:7], ALU.mult)
                                v.drain()
                                v.tensor_scalar(sc_st[:, 7:8], sc_st[:, 7:8], -1.0, None, ALU.mult)
                                v.drain()
                                ch["v"].inc(v.drain())

                            @blk.scalar
                            def _(s):
                                s.wait_ge(ch["v"].sem, ch["v"].n)
                                s.activation(out=sc_a[:], in_=cs_b[:], func=AF.Identity,
                                             bias=sc_st[:, 7:8], scale=sc_st[:, 6:7])
                                ch["a"].inc(s.drain())

                            @blk.vector
                            def _(v):
                                v.wait_ge(ch["a"].sem, ch["a"].n)
                                v.tensor_copy(sc_bf[:], sc_a[:])
                                v.drain()
                                v.tensor_tensor(stage[:, 0:D], sc_a[:], sc_bf[:], ALU.subtract)
                                v.drain()
                                ch["v"].inc(v.drain())

                            base_p1 = ch["p1"].n
                            for c in range(2 * NFC):
                                src = sc_bf if c < NFC else stage
                                cc0 = c if c < NFC else c - NFC

                                @blk.tensor
                                def _(t, c=c, src=src, cc0=cc0, base_p1=base_p1):
                                    if c == 0:
                                        t.wait_ge(ch["v"].sem, ch["v"].n)
                                        t.wait_ge(ch["p3"].sem, ch["p3"].n)
                                    t.wait_ge(ch["p1"].sem, base_p1 + c)
                                    t.transpose(cps_tb[:], src[:, cc0 * 128:(cc0 + 1) * 128], c_id_bf[:])
                                    ch["t"].inc(t.drain())

                                @blk.vector
                                def _(v, c=c):
                                    v.wait_ge(ch["t"].sem, ch["t"].n)
                                    v.tensor_copy(cxT[:, c * 128:(c + 1) * 128], cps_tb[:])
                                    v.drain()
                                    ch["p1"].inc(v.drain())

                            @blk.gpsimd
                            def _(g, tg=tg, j=j):
                                g.wait_ge(ch["p1"].sem, ch["p1"].n)
                                for cg in range(0, NFC, 8):
                                    dq_dg.issue(g, lambda cg=cg: g.dma_start(
                                        out=d_xfT.ap().rearrange("(c f) t -> f c t", f=128)[
                                            :, cg:cg + 8, tg * WG + j * 128:tg * WG + (j + 1) * 128],
                                        in_=cxT[:, cg * 128:(cg + 8) * 128].rearrange(
                                            "f (c t) -> f c t", t=128)))
                                dq_dg.wait_all(g)
                                ch["p3"].inc(g.drain())

                            @blk.tensor
                            def _(t, tg=tg, j=j):
                                t.wait_ge(ch["p1"].sem, ch["p1"].n)
                                first = True
                                for c in range(NFC):
                                    for (la, lb) in [(0, 0), (0, 1), (1, 0)]:
                                        t.matmul(cps_r[:, j * 128:(j + 1) * 128],
                                                 c_c1t[:, (la * NFC + c) * RANK:(la * NFC + c + 1) * RANK],
                                                 cxT[:, (lb * NFC + c) * 128:(lb * NFC + c + 1) * 128],
                                                 start=first,
                                                 stop=(c == NFC - 1 and (la, lb) == (1, 0)))
                                        first = False
                                ch["t"].inc(t.drain())

                        @blk.vector
                        def _(v, tg=tg):
                            v.wait_ge(ch["t"].sem, ch["t"].n)
                            v.tensor_copy(cr1[:, tg * WG:(tg + 1) * WG], cps_r[:, 0:WG])
                            v.drain()
                            ch["v"].inc(v.drain())
                    bar(blk)

                    @blk.vector
                    def _(v):
                        v.tensor_copy(cr1.bitcast(BF16)[:, 2 * TPC:3 * TPC], cr1[:, 0:TPC])
                        v.drain()
                        v.tensor_tensor(cr1.bitcast(BF16)[:, 3 * TPC:4 * TPC], cr1[:, 0:TPC],
                                        cr1.bitcast(BF16)[:, 2 * TPC:3 * TPC], ALU.subtract)
                        v.drain()
                        ch["v"].inc(v.drain())
                    bar(blk)

                    for tt in range(NTT):
                        for rt in range(INTER // 512):
                            @blk.sync
                            def _(sy, tt=tt, rt=rt):
                                it = tt * (INTER // 512) + rt
                                if it >= 2:
                                    sy.wait_ge(ch["p1"].sem, ch["p1"].n - 1)
                                bsel = it % 2
                                dq_di.issue(sy, lambda: sy.dma_start(
                                    out=cc2[:, bsel * 1024:bsel * 1024 + 512],
                                    in_=d_c2t.ap()[0:RANK, rt * 512:(rt + 1) * 512]))
                                dq_di.issue(sy, lambda: sy.dma_start(
                                    out=cc2[:, bsel * 1024 + 512:bsel * 1024 + 1024],
                                    in_=d_c2t.ap()[RANK:2 * RANK, rt * 512:(rt + 1) * 512]))

                            @blk.tensor
                            def _(t, tt=tt, rt=rt):
                                dq_di.wait_all(t)
                                t.wait_ge(ch["a"].sem, ch["a"].n)
                                bsel = (tt * (INTER // 512) + rt) % 2
                                combos = [(2, 0), (2, 512), (3, 0)]
                                first = True
                                for (hl, off) in combos:
                                    t.matmul(cps_i[:],
                                             cr1.bitcast(BF16)[:, hl * TPC + tt * 128:
                                                               hl * TPC + (tt + 1) * 128],
                                             cc2[:, bsel * 1024 + off:bsel * 1024 + off + 512],
                                             start=first, stop=(hl, off) == (3, 0))
                                    first = False
                                ch["t"].inc(t.drain())
                                ch["p1"].inc(t.drain())

                            @blk.scalar
                            def _(s, tt=tt, rt=rt):
                                s.wait_ge(ch["t"].sem, ch["t"].n)
                                s.wait_ge(ch["v"].sem, ch["v"].n)
                                s.activation(out=sc_a[:, 0:512], in_=cps_i[:], func=AF.Identity)
                                ch["a"].inc(s.drain())

                            @blk.vector
                            def _(v, tt=tt, rt=rt):
                                v.wait_ge(ch["a"].sem, ch["a"].n)
                                for half in range(2):
                                    b = rt * 2 + half
                                    v.max(cm8[:], sc_a[:, half * 256:(half + 1) * 256])
                                    v.drain()
                                    v.max_index(cmi[:], cm8[:], sc_a[:, half * 256:(half + 1) * 256])
                                    v.drain()
                                    v.tensor_copy(cidxu[:, b:b + 1], cmi[:, 0:1])
                                    v.drain()
                                ch["v"].inc(v.drain())

                        @blk.vector
                        def _(v, tt=tt):
                            v.wait_ge(ch["v"].sem, ch["v"].n)
                            v.tensor_copy(cidxf[:, tt * NBLK:(tt + 1) * NBLK], cidxu[:])
                            v.drain()
                            v.tensor_copy(stage[:, 0:NBLK], cidxf[:, tt * NBLK:(tt + 1) * NBLK])
                            v.drain()
                            ch["v"].inc(v.drain())

                        @blk.tensor
                        def _(t, tt=tt):
                            t.wait_ge(ch["v"].sem, ch["v"].n)
                            t.wait_ge(ch["p2"].sem, ch["p2"].n)
                            t.transpose(cps_tb[0:NBLK, 0:128], stage[:, 0:NBLK], c_id_bf[:])
                            ch["t"].inc(t.drain())

                        @blk.vector
                        def _(v, tt=tt):
                            v.wait_ge(ch["t"].sem, ch["t"].n)
                            v.tensor_scalar(cnegT[:, tt * 128:(tt + 1) * 128], cps_tb[0:NBLK, 0:128],
                                            -1.0, None, ALU.mult)
                            v.drain()
                            ch["p2"].inc(v.drain())
                    bar(blk)

                    if dbg:
                        @blk.gpsimd
                        def _(g):
                            dq_dg.issue(g, lambda: g.dma_start(out=dbg_outs["o_idx"][:, :], in_=cidxf[:, :]))
                            dq_dg.issue(g, lambda: g.dma_start(out=dbg_outs["o_aproj"][:, :], in_=d_aproj[:, :]))
                            dq_dg.issue(g, lambda: g.dma_start(out=dbg_outs["o_xfT"][:, :], in_=d_xfT[:, :]))
                            dq_dg.wait_all(g)
                        bar(blk)

                if upto == "c":
                    emit_zero_y()
                    return nc, dbg_outs

                GRP = min(512, TPC)
                NG = TPC // GRP
                with ExitStack() as fstk:
                    sbF = lambda n_, s_, d_: fstk.enter_context(nc.sbuf_tensor(n_, s_, d_))
                    f_ht = sbF("f_ht", [128, RB * GRP], BF16)
                    f_xT = sbF("f_xT", [128, NFC * GRP], BF16)
                    f_w1 = sbF("f_w1", [128, 2 * NFC * 128], BF16)
                    f_w2 = sbF("f_w2", [128, 2 * 4 * 512], BF16)
                    f_er = stage[0:1, 2 * GRP:6 * GRP]
                    f_ones = stage[0:1, 6 * GRP:7 * GRP]
                    fps_p = fstk.enter_context(nc.psum_tensor("fps_p", [128, GRP], F32))
                    fps_p2 = fstk.enter_context(nc.psum_tensor("fps_p2", [128, GRP], F32))
                    fps_e = fstk.enter_context(nc.psum_tensor("fps_e", [128, GRP], F32))
                    fps_e2 = fstk.enter_context(nc.psum_tensor("fps_e2", [128, GRP], F32))
                    fps_oacc = [fstk.enter_context(nc.psum_tensor(f"fps_oa{i}", [128, 512], F32))
                                for i in range(GRP // 128)]

                    @blk.vector
                    def _(v):
                        v.memset(f_ones, 1.0)
                        v.drain()
                        ch["v"].inc(v.drain())
                    bar(blk)

                    @blk.sync
                    def _(sy):
                        sy.wait_ge(ch["cc"].sem, cc_w1w2)
                    @blk.tensor
                    def _(t):
                        t.wait_ge(ch["cc"].sem, cc_w1w2)

                    ffn_fences = []
                    for fg in range(NG):
                        glo = fg * GRP
                        @blk.sync
                        def _(sy, glo=glo):
                            for c in ch.values():
                                if c.n:
                                    sy.wait_ge(c.sem, c.n)
                            dq_di.wait_all(sy)
                            dq_dg.wait_all(sy)
                            for cg in range(0, NFC, 8):
                                dq_di.issue(sy, lambda cg=cg: sy.dma_start(
                                    out=f_xT[:].rearrange("p (c t) -> p c t", t=GRP)[:, cg:cg + 8, :],
                                    in_=d_xfT.ap().rearrange("(c f) t -> f c t", f=128)[
                                        :, cg:cg + 8, glo:glo + GRP]))

                        er_toks = {}

                        @blk.gpsimd
                        def _(g, glo=glo):
                            for j0 in range(min(4, NBLK)):
                                er_toks[j0] = dq_dg.issue(g, lambda j0=j0: g.dma_start(
                                    out=f_er[:, (j0 % 4) * GRP:(j0 % 4 + 1) * GRP],
                                    in_=cnegT[j0:j0 + 1, glo:glo + GRP]))
                            ch["g"].inc(g.drain())

                        for rb in range(RB):
                            j = rb // 2
                            hi_half = rb % 2 == 1

                            @blk.sync
                            def _(sy, rb=rb, j=j, glo=glo):
                                if rb < 2:
                                    dq_di.wait_all(sy)
                                    sy.wait_ge(ch["g"].sem, ch["g"].n)
                                else:
                                    sy.wait_ge(ch["p1"].sem, ch["p1"].n - 1)
                                dq_di.issue(sy, lambda: sy.dma_start(
                                    out=f_w1[:, (rb % 2) * NFC * 128:(rb % 2 + 1) * NFC * 128],
                                    in_=d_w1t.ap()[rb * 128:(rb + 1) * 128, :]))

                            @blk.tensor
                            def _(t, rb=rb, j=j, hi_half=hi_half):
                                dq_di.wait_all(t)
                                if rb >= 2:
                                    t.wait_ge(ch["a"].sem, ch["a"].n - 1)
                                    t.wait_ge(ch["v"].sem, ch["v"].n - 1)
                                pp = fps_p if rb % 2 == 0 else fps_p2
                                for c in range(NFC):
                                    t.matmul(pp[:],
                                             f_w1[:, (rb % 2) * NFC * 128 + c * 128:
                                                  (rb % 2) * NFC * 128 + (c + 1) * 128],
                                             f_xT[:, c * GRP:(c + 1) * GRP],
                                             start=(c == 0), stop=(c == NFC - 1))
                                if not hi_half:
                                    t.wait_ge(*er_toks[j])
                                    pe = fps_e if j % 2 == 0 else fps_e2
                                    t.matmul(pe[:], c_iota_lo[:], f_ones, start=True, stop=False)
                                    t.matmul(pe[:], c_ones_row[:],
                                             f_er[:, (j % 4) * GRP:(j % 4 + 1) * GRP],
                                             start=False, stop=True)
                                    ch["p3"].inc(t.drain())
                                ch["t"].inc(t.drain())

                            if hi_half and j + 4 < NBLK:
                                @blk.gpsimd
                                def _(g, j=j, glo=glo):
                                    g.wait_ge(ch["p3"].sem, ch["p3"].n)
                                    er_toks[j + 4] = dq_dg.issue(g, lambda: g.dma_start(
                                        out=f_er[:, ((j + 4) % 4) * GRP:((j + 4) % 4 + 1) * GRP],
                                        in_=cnegT[j + 4:j + 5, glo:glo + GRP]))

                            @blk.scalar
                            def _(s, rb=rb):
                                s.wait_ge(ch["t"].sem, ch["t"].n)
                                pp = fps_p if rb % 2 == 0 else fps_p2
                                s.activation(out=stage[:, 0:GRP] if rb % 2 == 0 else
                                             stage[:, GRP:2 * GRP],
                                             in_=pp[:], func=AF.Relu)
                                ch["a"].inc(s.drain())

                            @blk.vector
                            def _(v, rb=rb, j=j, hi_half=hi_half):
                                v.wait_ge(ch["a"].sem, ch["a"].n)
                                pe = fps_e if j % 2 == 0 else fps_e2
                                v.scalar_tensor_tensor(
                                    f_ht[:, rb * GRP:(rb + 1) * GRP], pe[:],
                                    -128.0 if hi_half else 0.0,
                                    stage[:, 0:GRP] if rb % 2 == 0 else stage[:, GRP:2 * GRP],
                                    ALU.is_equal, ALU.mult)
                                v.drain()
                                ch["v"].inc(v.drain())
                                ch["p1"].inc(v.drain())

                        NWB = 4 if RB % 4 == 0 else 2
                        for fgrp in range(D // 512):
                            for rbp in range(RB // NWB):
                                @blk.sync
                                def _(sy, fgrp=fgrp, rbp=rbp):
                                    if fgrp == 0 and rbp == 0:
                                        sy.wait_ge(ch["v"].sem, ch["v"].n)
                                    it = fgrp * (RB // NWB) + rbp
                                    if it >= 2:
                                        sy.wait_ge(ch["p2"].sem, ch["p2"].n - 1)
                                    dq_di.issue(sy, lambda: sy.dma_start(
                                        out=f_w2[:, (rbp % 2) * NWB * 512:
                                                 (rbp % 2 + 1) * NWB * 512].rearrange(
                                            "p (b t) -> p b t", t=512),
                                        in_=d_w2.ap()[rbp * NWB * 128:(rbp + 1) * NWB * 128,
                                                      fgrp * 512:(fgrp + 1) * 512].rearrange(
                                            "(b f) t -> f b t", f=128)))

                                @blk.tensor
                                def _(t, fgrp=fgrp, rbp=rbp, glo=glo):
                                    dq_di.wait_all(t)
                                    if rbp == 0 and fgrp > 0:
                                        t.wait_ge(ch["a"].sem, ch["a"].n)
                                    for bb in range(NWB):
                                        rb = rbp * NWB + bb
                                        for i in range(GRP // 128):
                                            t.matmul(fps_oacc[i][:],
                                                     f_ht[:, rb * GRP + i * 128:rb * GRP + (i + 1) * 128],
                                                     f_w2[:, (rbp % 2) * NWB * 512 + bb * 512:
                                                          (rbp % 2) * NWB * 512 + (bb + 1) * 512],
                                                     start=(rb == 0), stop=(rb == RB - 1))
                                    ch["t"].inc(t.drain())
                                    ch["p2"].inc(t.drain())

                            for i in range(GRP // 128):
                                @blk.scalar
                                def _(s, fgrp=fgrp, i=i):
                                    if i == 0:
                                        s.wait_ge(ch["t"].sem, ch["t"].n)
                                    if len(ffn_fences) >= 4:
                                        s.wait_ge(*ffn_fences[-4])
                                    s.activation(out=sc_bf[:, (i % 4) * 512:(i % 4 + 1) * 512],
                                                 in_=fps_oacc[i][:],
                                                 func=AF.Identity)
                                    ch["a"].inc(s.drain())

                                @blk.gpsimd
                                def _(g, fgrp=fgrp, i=i, glo=glo):
                                    g.wait_ge(ch["a"].sem, ch["a"].n)
                                    tok = dq_dg.issue(g, lambda: g.dma_start(
                                        out=d_ffn.ap()[glo + i * 128:glo + (i + 1) * 128,
                                                       fgrp * 512:(fgrp + 1) * 512],
                                        in_=sc_bf[:, (i % 4) * 512:(i % 4 + 1) * 512]))
                                    ch["p4"].inc(g.drain())
                                    ffn_fences.append(tok)
                    bar(blk)

                    if dbg:
                        @blk.gpsimd
                        def _(g):
                            for c in ch.values():
                                if c.n:
                                    g.wait_ge(c.sem, c.n)
                            dq_di.wait_all(g)
                            dq_dg.wait_all(g)
                            dq_dg.issue(g, lambda: g.dma_start(
                                out=dbg_outs["o_ffn"][:, :], in_=d_ffn.ap()[:, :]))
                            dq_dg.wait_all(g)
                            ch["g"].inc(g.drain())

                    for tt in range(NTT):
                        @blk.sync
                        def _(sy, tt=tt):
                            sy.wait_ge(ch["v"].sem, ch["v"].n)
                            dq_dg.wait_all(sy)
                            dq_di.issue(sy, lambda: sy.dma_start(out=sc_a[:], in_=x_in[tt * 128:(tt + 1) * 128, :]))
                            dq_di.issue(sy, lambda: sy.dma_start(out=sc_bf[:], in_=d_aproj.ap()[
                                tt * 128:(tt + 1) * 128, :]))
                            dq_di.issue(sy, lambda: sy.dma_start(out=stage[:, 0:D], in_=d_ffn.ap()[
                                tt * 128:(tt + 1) * 128, :]))

                        @blk.vector
                        def _(v, tt=tt):
                            dq_di.wait_all(v)
                            v.tensor_tensor(sc_a[:], sc_a[:], sc_bf[:], ALU.add)
                            v.drain()
                            v.tensor_tensor(sc_a[:], sc_a[:], stage[:, 0:D], ALU.add)
                            v.drain()
                            ch["v"].inc(v.drain())

                        @blk.gpsimd
                        def _(g, tt=tt):
                            g.wait_ge(ch["v"].sem, ch["v"].n)
                            dq_dg.issue(g, lambda: g.dma_start(out=y_out[tt * 128:(tt + 1) * 128, :], in_=sc_a[:]))

                    @blk.gpsimd
                    def _(g):
                        dq_dg.wait_all(g)
                    bar(blk)
        return nc, dbg_outs


def shard_inputs(cfg, inputs):
    S, D, H, HD, INTER, NC = cfg["S"], cfg["D"], cfg["H"], cfg["HD"], cfg["INTER"], cfg["NC"]
    TPC, HPC, OPC = S // NC, H // NC, D // NC
    RPC = INTER // NC
    x = np.ascontiguousarray(inputs["L0_attn_hidden_states"].reshape(S, D).astype(np.float32))
    maps = []
    for c in range(NC):
        m = {
            "x": np.ascontiguousarray(x[c * TPC:(c + 1) * TPC]),
            "wq": np.ascontiguousarray(inputs["L0_attn_Wq"][c * HPC * HD:(c + 1) * HPC * HD]),
            "wk": np.ascontiguousarray(inputs["L0_attn_Wk"][c * HPC * HD:(c + 1) * HPC * HD]),
            "wv": np.ascontiguousarray(inputs["L0_attn_Wv"][c * HPC * HD:(c + 1) * HPC * HD]),
            "wo": np.ascontiguousarray(inputs["L0_attn_Wo"][c * OPC:(c + 1) * OPC]),
            "c1": np.ascontiguousarray(inputs["L0_ffn_c1_w"]),
            "c2": np.ascontiguousarray(inputs["L0_ffn_c2_w"]),
            "w1": np.ascontiguousarray(inputs["L0_ffn_w1"][c * RPC:(c + 1) * RPC]),
            "w2": np.ascontiguousarray(inputs["L0_ffn_w2t"][c * RPC:(c + 1) * RPC]),
        }
        f0 = c * HPC * HD
        if f0 < cfg["P4"]:
            mq = 7.0
        elif f0 < cfg["P4"] + cfg["P6"]:
            mq = 31.0
        else:
            mq = 127.0
        qp = np.zeros((128, 2), np.float32)
        qp[:, 0] = 1.0 / mq
        qp[:, 1] = mq
        m["qparam"] = qp
        maps.append(m)
    return maps


def assemble(cfg, results):
    S, D = cfg["S"], cfg["D"]
    return np.concatenate([r["y"] for r in results], axis=0).reshape(1, S, D)


_CACHE = {}


def kernel(L0_attn_hidden_states, L0_attn_Wq, L0_attn_Wk, L0_attn_Wv, L0_attn_Wo,
           L0_ffn_c1_w, L0_ffn_c2_w, L0_ffn_w1, L0_ffn_w2t):
    cfg = dict(CFG_FULL)
    cfg["upto"] = "all"
    inputs = {
        "L0_attn_hidden_states": np.asarray(L0_attn_hidden_states),
        "L0_attn_Wq": np.asarray(L0_attn_Wq),
        "L0_attn_Wk": np.asarray(L0_attn_Wk),
        "L0_attn_Wv": np.asarray(L0_attn_Wv),
        "L0_attn_Wo": np.asarray(L0_attn_Wo),
        "L0_ffn_c1_w": np.asarray(L0_ffn_c1_w),
        "L0_ffn_c2_w": np.asarray(L0_ffn_c2_w),
        "L0_ffn_w1": np.asarray(L0_ffn_w1),
        "L0_ffn_w2t": np.asarray(L0_ffn_w2t),
    }
    if "nc" not in _CACHE:
        _CACHE["nc"] = build_nc(cfg)
    nc, _dbg = _CACHE["nc"]
    maps = shard_inputs(cfg, inputs)
    res = run_bass_kernel_spmd(nc, maps, core_ids=list(range(cfg["NC"])))
    return assemble(cfg, res.results)


# version 11
# speedup vs baseline: 1.1985x; 1.0227x over previous
import numpy as np
import ml_dtypes
from contextlib import ExitStack

import concourse.bass as bass
import concourse.mybir as mybir
from concourse.bass_utils import run_bass_kernel_spmd

F32 = mybir.dt.float32
BF16 = mybir.dt.bfloat16
I32 = mybir.dt.int32
U32 = mybir.dt.uint32
ALU = mybir.AluOpType
AF = mybir.ActivationFunctionType
AXX = mybir.AxisListType.X

CFG_SMALL = dict(S=1024, D=1024, H=8, HD=128, INTER=1024, RANK=16,
                 P4=512, P6=256, P8=256, NC=8, debug=True)
CFG_FULL = dict(S=8192, D=4096, H=32, HD=128, INTER=14336, RANK=16,
                P4=2048, P6=1024, P8=1024, NC=8, debug=False)

LN2PI_C1 = 6.28125
LN2PI_C2 = float(np.float32(2 * np.pi - 6.28125))
LN2PI_C3 = float(2 * np.pi - 6.28125 - float(np.float32(2 * np.pi - 6.28125)))
SOFTMAX_SHIFT = 28.0


class Chain:
    def __init__(self, stack, nc, name):
        self.sem = stack.enter_context(nc.semaphore(name))
        self.n = 0

    def inc(self, inst, k=1):
        inst.then_inc(self.sem, k)
        self.n += k
        return self.n

    def dma(self, inst):
        return self.inc(inst, 16)


class DQ:

    def __init__(self, stack, nc, name, k=4):
        self.sems = [stack.enter_context(nc.semaphore(f"{name}{i}")) for i in range(k)]
        self.cnt = [0] * k
        self.i = 0
        self.k = k

    def issue(self, eng, inst_fn):
        s = self.i % self.k
        if self.cnt[s]:
            eng.wait_ge(self.sems[s], 16 * self.cnt[s])
        inst = inst_fn()
        inst.then_inc(self.sems[s], 16)
        self.cnt[s] += 1
        self.i += 1
        return (self.sems[s], 16 * self.cnt[s])

    def wait_all(self, eng):
        for s in range(self.k):
            if self.cnt[s]:
                eng.wait_ge(self.sems[s], 16 * self.cnt[s])


def regions(cfg, ncols):
    out = []
    for (st, w, maxq) in [(0, cfg["P4"], 7.0),
                          (cfg["P4"], cfg["P6"], 31.0),
                          (cfg["P4"] + cfg["P6"], cfg["P8"], 127.0)]:
        if st < ncols:
            out.append((st, min(w, ncols - st), maxq))
    return out


def emit_quant(v, x_f32, out_bf, gmax, qf, cfg, ncols):
    for (st, w, maxq) in regions(cfg, ncols):
        g0, gw = st // 32, w // 32
        v.tensor_reduce(gmax[:, g0:g0 + gw].unsqueeze(2),
                        x_f32[:, st:st + w].rearrange("p (g r) -> p g r", r=32),
                        AXX, ALU.max, apply_absolute_value=True)
        v.drain()
        v.tensor_scalar(gmax[:, g0:g0 + gw], gmax[:, g0:g0 + gw],
                        float(1.0 / maxq), 1e-12, ALU.mult, ALU.max)
        v.drain()
    GW = ncols // 32
    for (st, w, maxq) in regions(cfg, ncols):
        g0, gw = st // 32, w // 32
        v.reciprocal(gmax[:, GW + g0:GW + g0 + gw], gmax[:, g0:g0 + gw])
        v.drain()
        v.tensor_tensor(qf[:, st:st + w].rearrange("p (g r) -> p g r", r=32),
                        x_f32[:, st:st + w].rearrange("p (g r) -> p g r", r=32),
                        gmax[:, GW + g0:GW + g0 + gw].unsqueeze(2).broadcast_to([128, gw, 32]),
                        ALU.mult)
        v.drain()
    v.tensor_copy(qf[:, 0:ncols].bitcast(I32), qf[:, 0:ncols])
    v.drain()
    v.tensor_copy(qf[:, 0:ncols], qf[:, 0:ncols].bitcast(I32))
    v.drain()
    for (st, w, maxq) in regions(cfg, ncols):
        g0, gw = st // 32, w // 32
        v.tensor_tensor(out_bf[:, st:st + w].rearrange("p (g r) -> p g r", r=32),
                        qf[:, st:st + w].rearrange("p (g r) -> p g r", r=32),
                        gmax[:, g0:g0 + gw].unsqueeze(2).broadcast_to([128, gw, 32]),
                        ALU.mult)
        v.drain()


def build_nc(cfg):
    S, D, H, HD = cfg["S"], cfg["D"], cfg["H"], cfg["HD"]
    INTER, RANK, NC = cfg["INTER"], cfg["RANK"], cfg["NC"]
    TPC = S // NC
    HPC = H // NC
    NFC = D // 128
    NTT = TPC // 128
    NQT = S // 512
    RB = INTER // 128
    RB_PC = RB // NC
    NBLK = INTER // 256
    NOC = NFC
    OPC = D // NC
    NOCL = OPC // 128
    dbg = cfg["debug"]
    upto = cfg.get("upto", "all")

    nc = bass.Bass("TRN2", target_bir_lowering=False, debug=False)

    x_in = nc.dram_tensor("x", [TPC, D], F32, kind="ExternalInput").ap()
    wq_in = nc.dram_tensor("wq", [HPC * HD, D], F32, kind="ExternalInput").ap()
    wk_in = nc.dram_tensor("wk", [HPC * HD, D], F32, kind="ExternalInput").ap()
    wv_in = nc.dram_tensor("wv", [HPC * HD, D], F32, kind="ExternalInput").ap()
    wo_in = nc.dram_tensor("wo", [OPC, D], F32, kind="ExternalInput").ap()
    c1_in = nc.dram_tensor("c1", [RANK, D], F32, kind="ExternalInput").ap()
    c2_in = nc.dram_tensor("c2", [INTER, RANK], F32, kind="ExternalInput").ap()
    w1_in = nc.dram_tensor("w1", [RB_PC * 128, D], F32, kind="ExternalInput").ap()
    w2_in = nc.dram_tensor("w2", [RB_PC * 128, D], F32, kind="ExternalInput").ap()
    qp_in = nc.dram_tensor("qparam", [128, 2], F32, kind="ExternalInput").ap()
    y_out = nc.dram_tensor("y", [TPC, D], F32, kind="ExternalOutput").ap()

    d_hqT_in = nc.dram_tensor("d_hqT_in", [D, TPC], BF16)
    d_hqT = nc.dram_tensor("d_hqT", [NC * D, TPC], BF16)
    d_qwoT_in = nc.dram_tensor("d_qwoT_in", [NOCL * 128, NFC * 128], BF16)
    d_qwoT = nc.dram_tensor("d_qwoT", [NC * NOCL * 128, NFC * 128], BF16)
    d_qo_in = nc.dram_tensor("d_qo_in", [HPC * S, HD], BF16)
    d_qo = nc.dram_tensor("d_qo", [H * S, HD], BF16)
    d_w1t_in = nc.dram_tensor("d_w1t_in", [RB_PC * 128, NFC * 128], BF16)
    d_w1t = nc.dram_tensor("d_w1t", [RB * 128, NFC * 128], BF16)
    d_w2_in = nc.dram_tensor("d_w2_in", [RB_PC * 128, D], BF16)
    d_w2 = nc.dram_tensor("d_w2", [RB * 128, D], BF16)
    d_aproj = nc.dram_tensor("d_aproj", [TPC, D], BF16)
    d_qostage = nc.dram_tensor("d_qostage", [(D // 128) * (S // NC), 128], BF16)
    d_xfT = nc.dram_tensor("d_xfT", [D, TPC], BF16)

    dbg_outs = {}

    def dout(name, shape, dt=F32):
        if dbg:
            dbg_outs[name] = nc.dram_tensor(name, shape, dt, kind="ExternalOutput").ap()
            return dbg_outs[name]
        return None

    o_hq = dout("o_hq", [TPC, D], BF16)
    o_hqT = dout("o_hqT", [NC * D, TPC], BF16)
    dout("o_qTr", [128, S], BF16)
    dout("o_kTr", [128, S], BF16)
    dout("o_vtm", [128, S], BF16)
    dout("o_qo", [H * S, HD], BF16)
    dout("o_idx", [128, (TPC // 128) * (INTER // 256)], F32)
    dout("o_ffn", [TPC, D], BF16)
    dout("o_ht", [128, (INTER // 128) * min(512, TPC)], BF16)
    dout("o_aproj", [TPC, D], BF16)
    dout("o_xfT", [D, TPC], BF16)

    stack = ExitStack()
    with stack:
        sb = lambda name, shape, dt: stack.enter_context(nc.sbuf_tensor(name, shape, dt))
        c_id_bf = sb("c_id_bf", [128, 128], BF16)
        c_id_f = sb("c_id_f", [128, 128], F32)
        c_ones_bf = sb("c_ones_bf", [128, 1], BF16)
        c_ones_row = sb("c_ones_row", [1, 128], BF16)
        c_iota_lo = sb("c_iota_lo", [1, 128], BF16)
        c_iota_hi = sb("c_iota_hi", [1, 128], BF16)
        c_negC = sb("c_negC", [128, 1], F32)

        stage = sb("stage", [128, D], BF16)
        sc_a = sb("sc_a", [128, D], F32)
        sc_g = sb("sc_g", [128, 2 * (D // 32)], F32)
        sc_bf = sb("sc_bf", [128, D], BF16)
        sc_st = sb("sc_st", [128, 16], F32)
        qp_inv = sb("qp_inv", [128, 2], F32)

        stk2 = ExitStack()
        sb2 = lambda name, shape, dt: stk2.enter_context(nc.sbuf_tensor(name, shape, dt))
        c_masks = sb2("c_masks", [128, 4 * 512], BF16)
        c_cos = sb2("c_cos", [128, S], BF16)
        c_ssin = sb2("c_ssin", [128, S], BF16)
        sc_b = sb2("sc_b", [128, D], F32)
        sc_bf2 = sb2("sc_bf2", [128, D], BF16)

        prestk = ExitStack()
        ps_t0h = prestk.enter_context(nc.psum_tensor("ps_t0", [128, 128], BF16))
        ps_t1h = prestk.enter_context(nc.psum_tensor("ps_t1", [128, 128], BF16))
        ps_t0 = ps_t0h[:]
        ps_t1 = ps_t1h[:]

        ch = {n: Chain(stack, nc, n) for n in
              ["v", "a", "t", "g", "cc", "p1", "p2", "p3", "p4"]}
        dq_di = DQ(stack, nc, "di", 8)
        dq_dg = DQ(stack, nc, "dg", 8)

        def bar(blk, engines=("sync", "gpsimd", "vector", "scalar", "tensor"),
                cc_upto=None):
            emap = {"sync": (blk.sync, None), "gpsimd": (blk.gpsimd, ch["g"]),
                    "vector": (blk.vector, ch["v"]), "scalar": (blk.scalar, ch["a"]),
                    "tensor": (blk.tensor, ch["t"])}
            for e in engines:
                dec, c = emap[e]
                if c is not None:
                    @dec
                    def _(eng, c=c):
                        c.inc(eng.drain())
            counts = [(ch[k].sem, ch[k].n) for k in ["v", "a", "t", "g"]]
            counts.append((ch["cc"].sem, cc_upto if cc_upto is not None else ch["cc"].n))
            counts += [(dq_di.sems[s], 16 * dq_di.cnt[s]) for s in range(dq_di.k)]
            counts += [(dq_dg.sems[s], 16 * dq_dg.cnt[s]) for s in range(dq_dg.k)]
            for e in engines:
                dec, _c = emap[e]
                @dec
                def _(eng, counts=counts):
                    for sem, n in counts:
                        if n:
                            eng.wait_ge(sem, n)

        with nc.Block() as blk:
            @blk.gpsimd
            def _(g):
                g.iota(sc_a[:, 0:128], pattern=[[1, 128]], base=0, channel_multiplier=0,
                       allow_small_or_imprecise_dtypes=True)
                g.iota(sc_a[:, 128:129], pattern=[[0, 1]], base=0, channel_multiplier=1,
                       allow_small_or_imprecise_dtypes=True)
                g.iota(sc_b[:, 0:512], pattern=[[1, 512]], base=0, channel_multiplier=-1,
                       allow_small_or_imprecise_dtypes=True)
                g.iota(sc_st[:, 0:1], pattern=[[0, 1]], base=0, channel_multiplier=1,
                       allow_small_or_imprecise_dtypes=True)
                dq_dg.issue(g, lambda: g.dma_start(out=qp_inv[:], in_=qp_in[:, :]))
                ch["g"].inc(g.drain())

            @blk.vector
            def _(v):
                v.wait_ge(ch["g"].sem, ch["g"].n)
                dq_dg.wait_all(v)
                v.tensor_scalar(c_id_f[:], sc_a[:, 0:128], sc_a[:, 128:129], None, ALU.is_equal)
                v.drain()
                v.tensor_copy(c_id_bf[:], c_id_f[:])
                v.memset(c_ones_bf[:], 1.0)
                v.memset(c_negC[:], -SOFTMAX_SHIFT)
                v.memset(c_ones_row[:], 1.0)
                for j in range(4):
                    v.tensor_scalar(c_masks[:, j * 512:(j + 1) * 512], sc_b[:, 0:512],
                                    float(j * 128), None, ALU.is_ge)
                v.drain()
                v.tensor_copy(c_iota_lo[:], sc_a[0:1, 0:128])
                v.drain()
                v.tensor_scalar(c_iota_hi[:], sc_a[0:1, 0:128], 128.0, None, ALU.add)
                v.drain()
                v.tensor_scalar(sc_st[:, 1:2], sc_st[:, 0:1], 64.0, -64.0, ALU.is_ge, ALU.mult)
                v.drain()
                v.tensor_tensor(sc_st[:, 2:3], sc_st[:, 0:1], sc_st[:, 1:2], ALU.add)
                v.drain()
                ch["v"].inc(v.drain())

            @blk.scalar
            def _(s):
                s.wait_ge(ch["v"].sem, ch["v"].n)
                s.activation(out=sc_st[:, 3:4], in_=sc_st[:, 2:3], func=AF.Exp,
                             bias=0.0, scale=float(-np.log(10000.0) / 64.0))
                ch["a"].inc(s.drain())
            bar(blk)

            w_rp = min(S, D)
            for rp in range(S // w_rp):
                lo, hi = rp * w_rp, (rp + 1) * w_rp

                @blk.gpsimd
                def _(g, lo=lo):
                    g.wait_ge(ch["v"].sem, ch["v"].n)
                    g.wait_ge(ch["a"].sem, ch["a"].n)
                    g.iota(sc_a[:, 0:w_rp], pattern=[[1, w_rp]], base=lo,
                           channel_multiplier=0, allow_small_or_imprecise_dtypes=True)
                    ch["g"].inc(g.drain())

                @blk.vector
                def _(v):
                    v.wait_ge(ch["g"].sem, ch["g"].n)
                    v.tensor_scalar(sc_b[:, 0:w_rp], sc_a[:, 0:w_rp], sc_st[:, 3:4], None, ALU.mult)
                    v.drain()
                    v.tensor_scalar(sc_a[:, 0:w_rp], sc_b[:, 0:w_rp],
                                    float(1.0 / (2 * np.pi)), None, ALU.mult)
                    v.drain()
                    v.tensor_copy(sc_a[:, 0:w_rp].bitcast(I32), sc_a[:, 0:w_rp])
                    v.drain()
                    v.tensor_copy(sc_a[:, 0:w_rp], sc_a[:, 0:w_rp].bitcast(I32))
                    v.drain()
                    for cc_ in (LN2PI_C1, LN2PI_C2, LN2PI_C3):
                        v.scalar_tensor_tensor(sc_b[:, 0:w_rp], sc_a[:, 0:w_rp],
                                               float(-cc_), sc_b[:, 0:w_rp],
                                               ALU.mult, ALU.add)
                        v.drain()
                    v.tensor_scalar(sc_a[:, 0:w_rp], sc_b[:, 0:w_rp], float(np.pi),
                                    float(-2 * np.pi), ALU.is_gt, ALU.mult)
                    v.drain()
                    v.tensor_tensor(sc_b[:, 0:w_rp], sc_b[:, 0:w_rp], sc_a[:, 0:w_rp], ALU.add)
                    v.drain()
                    v.tensor_scalar(sc_a[:, 0:w_rp], sc_b[:, 0:w_rp], float(-np.pi),
                                    float(2 * np.pi), ALU.is_lt, ALU.mult)
                    v.drain()
                    v.tensor_tensor(sc_b[:, 0:w_rp], sc_b[:, 0:w_rp], sc_a[:, 0:w_rp], ALU.add)
                    v.drain()
                    ch["v"].inc(v.drain())

                @blk.scalar
                def _(s):
                    s.wait_ge(ch["v"].sem, ch["v"].n)
                    s.activation(out=sc_a[:, 0:w_rp], in_=sc_b[:, 0:w_rp], func=AF.Sin)
                    ch["a"].inc(s.drain())

                @blk.vector
                def _(v, lo=lo, hi=hi):
                    v.wait_ge(ch["a"].sem, ch["a"].n)
                    v.tensor_scalar(c_ssin[0:64, lo:hi], sc_a[0:64, 0:w_rp], -1.0, None, ALU.mult)
                    v.drain()
                    v.tensor_copy(c_ssin[64:128, lo:hi], sc_a[64:128, 0:w_rp])
                    v.drain()
                    v.tensor_scalar(sc_b[:, 0:w_rp], sc_b[:, 0:w_rp], float(np.pi / 2),
                                    None, ALU.add)
                    v.drain()
                    v.tensor_scalar(sc_a[:, 0:w_rp], sc_b[:, 0:w_rp], float(np.pi),
                                    float(-2 * np.pi), ALU.is_gt, ALU.mult)
                    v.drain()
                    v.tensor_tensor(sc_b[:, 0:w_rp], sc_b[:, 0:w_rp], sc_a[:, 0:w_rp], ALU.add)
                    v.drain()
                    ch["v"].inc(v.drain())

                @blk.scalar
                def _(s):
                    s.wait_ge(ch["v"].sem, ch["v"].n)
                    s.activation(out=sc_a[:, 0:w_rp], in_=sc_b[:, 0:w_rp], func=AF.Sin)
                    ch["a"].inc(s.drain())

                @blk.vector
                def _(v, lo=lo, hi=hi):
                    v.wait_ge(ch["a"].sem, ch["a"].n)
                    v.tensor_copy(c_cos[:, lo:hi], sc_a[:, 0:w_rp])
                    v.drain()
                    ch["v"].inc(v.drain())
            bar(blk)

            for tt in range(NTT):
                @blk.sync
                def _(sy, tt=tt):
                    for cname_, c in ch.items():
                        if c.n and cname_ != "cc":
                            sy.wait_ge(c.sem, c.n)
                    dq_di.wait_all(sy)
                    dq_dg.wait_all(sy)
                    dq_di.issue(sy, lambda: sy.dma_start(out=sc_a[:], in_=x_in[tt * 128:(tt + 1) * 128, :]))

                @blk.vector
                def _(v, tt=tt):
                    dq_di.wait_all(v)
                    v.reduce_sum(sc_st[:, 0:1], sc_a[:], AXX)
                    v.drain()
                    ch["v"].inc(v.drain())

                @blk.scalar
                def _(s, tt=tt):
                    dq_di.wait_all(s)
                    s.activation(out=sc_b[:], in_=sc_a[:], func=AF.Square,
                                 accum_out=sc_st[:, 1:2])
                    ch["a"].inc(s.drain())

                @blk.vector
                def _(v, tt=tt):
                    v.wait_ge(ch["a"].sem, ch["a"].n)
                    v.wait_ge(ch["v"].sem, ch["v"].n)
                    v.tensor_scalar(sc_st[:, 2:3], sc_st[:, 0:1], float(1.0 / D), None, ALU.mult)
                    v.drain()
                    v.tensor_tensor(sc_st[:, 3:4], sc_st[:, 2:3], sc_st[:, 2:3], ALU.mult)
                    v.drain()
                    v.tensor_scalar(sc_st[:, 4:5], sc_st[:, 1:2], float(1.0 / D), None, ALU.mult)
                    v.drain()
                    v.tensor_tensor(sc_st[:, 4:5], sc_st[:, 4:5], sc_st[:, 3:4], ALU.subtract)
                    v.drain()
                    v.tensor_scalar(sc_st[:, 4:5], sc_st[:, 4:5], 1e-5, None, ALU.add)
                    v.drain()
                    ch["v"].inc(v.drain())

                @blk.scalar
                def _(s, tt=tt):
                    s.wait_ge(ch["v"].sem, ch["v"].n)
                    s.activation(out=sc_st[:, 5:6], in_=sc_st[:, 4:5], func=AF.Sqrt)
                    ch["a"].inc(s.drain())

                @blk.vector
                def _(v, tt=tt):
                    v.wait_ge(ch["a"].sem, ch["a"].n)
                    v.reciprocal(sc_st[:, 6:7], sc_st[:, 5:6])
                    v.drain()
                    v.tensor_tensor(sc_st[:, 7:8], sc_st[:, 2:3], sc_st[:, 6:7], ALU.mult)
                    v.drain()
                    v.tensor_scalar(sc_st[:, 7:8], sc_st[:, 7:8], -1.0, None, ALU.mult)
                    v.drain()
                    ch["v"].inc(v.drain())

                @blk.scalar
                def _(s, tt=tt):
                    s.wait_ge(ch["v"].sem, ch["v"].n)
                    s.activation(out=sc_b[:], in_=sc_a[:], func=AF.Identity,
                                 bias=sc_st[:, 7:8], scale=sc_st[:, 6:7])
                    ch["a"].inc(s.drain())

                @blk.vector
                def _(v, tt=tt):
                    v.wait_ge(ch["a"].sem, ch["a"].n)
                    emit_quant(v, sc_b, sc_bf, sc_g, sc_a, cfg, D)
                    ch["v"].inc(v.drain())

                if dbg:
                    @blk.gpsimd
                    def _(g, tt=tt):
                        g.wait_ge(ch["v"].sem, ch["v"].n)
                        dq_dg.issue(g, lambda: g.dma_start(out=o_hq[tt * 128:(tt + 1) * 128, :],
                                                 in_=sc_bf[:]))

                for c in range(NFC):
                    @blk.tensor
                    def _(t, tt=tt, c=c):
                        if c == 0:
                            t.wait_ge(ch["v"].sem, ch["v"].n)
                        if c >= 2:
                            t.wait_ge(ch["p1"].sem, ch["p1"].n - 1)
                        t.transpose(ps_t0 if c % 2 == 0 else ps_t1,
                                    sc_bf[:, c * 128:(c + 1) * 128], c_id_bf[:])
                        ch["t"].inc(t.drain())

                    @blk.vector
                    def _(v, tt=tt, c=c):
                        v.wait_ge(ch["t"].sem, ch["t"].n)
                        v.tensor_copy(stage[:, c * 128:(c + 1) * 128],
                                      ps_t0 if c % 2 == 0 else ps_t1)
                        v.drain()
                        ch["p1"].inc(v.drain())

                @blk.gpsimd
                def _(g, tt=tt):
                    g.wait_ge(ch["p1"].sem, ch["p1"].n)
                    for cg in range(0, NFC, 8):
                        dq_dg.issue(g, lambda cg=cg: g.dma_start(
                            out=d_hqT_in.ap().rearrange("(c f) t -> c f t", f=128)[
                                cg:cg + 8, :, tt * 128:(tt + 1) * 128].transpose([1, 0, 2]),
                            in_=stage[:, cg * 128:(cg + 8) * 128].rearrange(
                                "f (c t) -> f c t", t=128)))

            @blk.gpsimd
            def _(g):
                dq_dg.wait_all(g)
                ch["cc"].inc(g.collective_compute(
                    "AllGather", ALU.bypass, replica_groups=[list(range(NC))],
                    ins=[d_hqT_in.ap().opt()], outs=[d_hqT.ap().opt()]))
                g.wait_ge(ch["cc"].sem, ch["cc"].n)
                if dbg:
                    dq_dg.issue(g, lambda: g.dma_start(out=o_hqT[:, :], in_=d_hqT[:, :]))
                dq_dg.wait_all(g)
            bar(blk)

            def emit_zero_y():
                @blk.vector
                def _(v):
                    v.memset(sc_a[:], 0.0)
                    ch["v"].inc(v.drain())

                @blk.gpsimd
                def _(g):
                    g.wait_ge(ch["v"].sem, ch["v"].n)
                    for tt in range(NTT):
                        dq_dg.issue(g, lambda: g.dma_start(out=y_out[tt * 128:(tt + 1) * 128, :], in_=sc_a[:]))
                    dq_dg.wait_all(g)

            if upto == "p0":
                emit_zero_y()
                stk2.close()
                return nc, dbg_outs

            def transpose_to_stage(n_chunks, src_bf, pst0, pst1, dst=None, dst_off=0):
                d = stage if dst is None else dst
                base = ch["p1"].n
                for c in range(n_chunks):
                    @blk.tensor
                    def _(t, c=c, base=base):
                        if c == 0:
                            t.wait_ge(ch["v"].sem, ch["v"].n)
                        t.wait_ge(ch["p1"].sem, max(base, base + c - 1))
                        t.transpose(pst0 if c % 2 == 0 else pst1,
                                    src_bf[:, c * 128:(c + 1) * 128], c_id_bf[:])
                        ch["t"].inc(t.drain())

                    @blk.vector
                    def _(v, c=c):
                        v.wait_ge(ch["t"].sem, ch["t"].n)
                        v.tensor_copy(d[:, dst_off + c * 128:dst_off + (c + 1) * 128],
                                      pst0 if c % 2 == 0 else pst1)
                        v.drain()
                        ch["p1"].inc(v.drain())

            for ocl in range(NOCL):
                @blk.sync
                def _(sy, ocl=ocl):
                    for cname_, c in ch.items():
                        if c.n and cname_ != "cc":
                            sy.wait_ge(c.sem, c.n)
                    dq_di.wait_all(sy)
                    dq_dg.wait_all(sy)
                    dq_di.issue(sy, lambda: sy.dma_start(out=sc_a[:], in_=wo_in[ocl * 128:(ocl + 1) * 128, :]))

                @blk.vector
                def _(v, ocl=ocl):
                    dq_di.wait_all(v)
                    emit_quant(v, sc_a, sc_bf, sc_g, sc_b, cfg, D)
                    ch["v"].inc(v.drain())

                transpose_to_stage(NFC, sc_bf, ps_t0, ps_t1)

                @blk.gpsimd
                def _(g, ocl=ocl):
                    g.wait_ge(ch["p1"].sem, ch["p1"].n)
                    dq_dg.issue(g, lambda: g.dma_start(
                        out=d_qwoT_in.ap()[ocl * 128:(ocl + 1) * 128, :],
                        in_=stage[:]))

            @blk.gpsimd
            def _(g):
                dq_dg.wait_all(g)
                ch["cc"].inc(g.collective_compute(
                    "AllGather", ALU.bypass, replica_groups=[list(range(NC))],
                    ins=[d_qwoT_in.ap().opt()], outs=[d_qwoT.ap().opt()]))

            for rbl in range(RB_PC):
                @blk.gpsimd
                def _(g, rbl=rbl):
                    for cname_, c in ch.items():
                        if c.n and cname_ != "cc":
                            g.wait_ge(c.sem, c.n)
                    dq_di.wait_all(g)
                    dq_dg.wait_all(g)
                    dq_dg.issue(g, lambda: g.dma_start(out=sc_bf[:], in_=w1_in[rbl * 128:(rbl + 1) * 128, :]))

                @blk.vector
                def _(v, rbl=rbl):
                    dq_dg.wait_all(v)
                    ch["v"].inc(v.drain())

                transpose_to_stage(NFC, sc_bf, ps_t0, ps_t1)

                @blk.gpsimd
                def _(g, rbl=rbl):
                    g.wait_ge(ch["p1"].sem, ch["p1"].n)
                    dq_dg.issue(g, lambda: g.dma_start(
                        out=d_w1t_in.ap()[rbl * 128:(rbl + 1) * 128, :],
                        in_=stage[:]))
                    dq_dg.issue(g, lambda: g.dma_start(out=sc_bf2[:],
                                             in_=w2_in[rbl * 128:(rbl + 1) * 128, :]))
                    dq_dg.wait_all(g)
                    dq_dg.issue(g, lambda: g.dma_start(
                        out=d_w2_in.ap()[rbl * 128:(rbl + 1) * 128, :],
                        in_=sc_bf2[:]))

            @blk.gpsimd
            def _(g):
                dq_dg.wait_all(g)
                ch["cc"].inc(g.collective_compute(
                    "AllGather", ALU.bypass, replica_groups=[list(range(NC))],
                    ins=[d_w1t_in.ap().opt()], outs=[d_w1t.ap().opt()]))
                ch["cc"].inc(g.collective_compute(
                    "AllGather", ALU.bypass, replica_groups=[list(range(NC))],
                    ins=[d_w2_in.ap().opt()], outs=[d_w2.ap().opt()]))
            cc_w1w2 = ch["cc"].n
            bar(blk, cc_upto=cc_w1w2 - 3)

            if upto == "p1":
                emit_zero_y()
                stk2.close()
                return nc, dbg_outs

            prestk.close()
            TT1 = 256
            NP1 = S // TT1
            SCALE = float(1.0 / np.sqrt(HD))
            with ExitStack() as bstk:
                sbB = lambda name, shape, dt: bstk.enter_context(nc.sbuf_tensor(name, shape, dt))
                b_qw = sbB("b_qw", [128, 3 * D], BF16)
                b_qT = sbB("b_qT", [128, S], BF16)
                b_kT = sbB("b_kT", [128, S], BF16)
                b_v = sbB("b_v", [128, S], BF16)
                b_hq2 = sbB("b_hq2", [128, 2 * NFC * TT1], BF16)
                b_pt = sbB("b_pt", [128, 4 * 512], BF16)
                b_ob = sbB("b_ob", [128, 512], BF16)
                b_L = sbB("b_L", [1, 512], F32)
                b_rL = sbB("b_rL", [1, 512], F32)
                b_rLc = sbB("b_rLc", [128, 4], F32)
                p_qk = bstk.enter_context(nc.psum_tensor("p_qk", [128, 2 * TT1], F32))
                p_v = bstk.enter_context(nc.psum_tensor("p_v", [128, TT1], F32))
                p_sp = bstk.enter_context(nc.psum_tensor("p_sp", [128, 2048], F32))
                p_o = bstk.enter_context(nc.psum_tensor("p_o", [128, 512], F32))
                p_l = bstk.enter_context(nc.psum_tensor("p_l", [1, 512], F32))
                p_tb_ap = p_v[:, 128:192].bitcast(BF16)

                for h in range(HPC):
                    for pj, w_ext in enumerate([wq_in, wk_in, wv_in]):
                        @blk.sync
                        def _(sy, pj=pj, w_ext=w_ext, h=h):
                            for cname_, c in ch.items():
                                if c.n and cname_ != "cc":
                                    sy.wait_ge(c.sem, c.n)
                            dq_di.wait_all(sy)
                            dq_dg.wait_all(sy)
                            dq_di.issue(sy, lambda: sy.dma_start(
                                out=sc_a[:], in_=w_ext[h * 128:(h + 1) * 128, :]))

                        @blk.vector
                        def _(v, pj=pj):
                            dq_di.wait_all(v)
                            emit_quant(v, sc_a, sc_bf, sc_g, sc_b, cfg, D)
                            ch["v"].inc(v.drain())

                        transpose_to_stage(NFC, sc_bf, p_sp[:, 0:64].bitcast(BF16),
                                           p_sp[:, 512:576].bitcast(BF16),
                                           dst=b_qw, dst_off=pj * D)

                    for tt in range(NP1):
                        glo = tt * TT1
                        buf = tt % 2
                        boff = buf * NFC * TT1
                        @blk.sync
                        def _(sy, glo=glo, boff=boff, tt=tt):
                            if tt < 2:
                                for cname_, c in ch.items():
                                    if c.n and cname_ != "cc":
                                        sy.wait_ge(c.sem, c.n)
                                dq_di.wait_all(sy)
                                dq_dg.wait_all(sy)
                            else:
                                sy.wait_ge(ch["p2"].sem, ch["p2"].n - 1)
                            pieces = []
                            t0 = glo
                            while t0 < glo + TT1:
                                s_id = t0 // TPC
                                t1 = min(glo + TT1, (s_id + 1) * TPC)
                                pieces.append((t0, t1, s_id))
                                t0 = t1
                            for (a0, a1, s_id) in pieces:
                                for cg in range(0, NFC, 8):
                                    dq_di.issue(sy, lambda a0=a0, a1=a1, s_id=s_id, cg=cg: sy.dma_start(
                                        out=b_hq2[:, boff:boff + NFC * TT1].rearrange(
                                            "p (c t) -> p c t", t=TT1)[:, cg:cg + 8, a0 - glo:a1 - glo],
                                        in_=d_hqT.ap().rearrange("(s c f) t -> s c f t", c=NFC, f=128)[
                                            s_id, cg:cg + 8, :,
                                            a0 - s_id * TPC:a1 - s_id * TPC].transpose([1, 0, 2])))

                        @blk.tensor
                        def _(t, glo=glo, boff=boff, tt=tt):
                            dq_di.wait_all(t)
                            if tt >= 1:
                                t.wait_ge(ch["p3"].sem, ch["p3"].n)
                            for pj in range(2):
                                for c in range(NFC):
                                    t.matmul(p_qk[:, pj * TT1:(pj + 1) * TT1],
                                             b_qw[:, pj * D + c * 128:pj * D + (c + 1) * 128],
                                             b_hq2[:, boff + c * TT1:boff + (c + 1) * TT1],
                                             start=(c == 0), stop=(c == NFC - 1))
                            for sub in range(TT1 // 128):
                                for c in range(NFC):
                                    t.matmul(p_v[:, sub * 128:(sub + 1) * 128],
                                             b_hq2[:, boff + c * TT1 + sub * 128:boff + c * TT1 + (sub + 1) * 128],
                                             b_qw[:, 2 * D + c * 128:2 * D + (c + 1) * 128],
                                             start=(c == 0), stop=(c == NFC - 1))
                            ch["t"].inc(t.drain())
                            ch["p2"].inc(t.drain())

                        @blk.scalar
                        def _(s, glo=glo, tt=tt):
                            s.wait_ge(ch["t"].sem, ch["t"].n)
                            s.activation(out=b_qT[:, glo:glo + TT1], in_=p_qk[:, 0:TT1], func=AF.Identity)
                            s.activation(out=b_v[:, glo:glo + TT1], in_=p_v[:], func=AF.Identity)
                            ch["a"].inc(s.drain())

                        @blk.vector
                        def _(v, glo=glo, tt=tt):
                            v.wait_ge(ch["t"].sem, ch["t"].n)
                            v.tensor_copy(b_kT[:, glo:glo + TT1], p_qk[:, TT1:2 * TT1])
                            v.drain()
                            ch["v"].inc(v.drain())

                        @blk.gpsimd
                        def _(g, tt=tt):
                            g.wait_ge(ch["a"].sem, ch["a"].n)
                            g.wait_ge(ch["v"].sem, ch["v"].n)
                            ch["p3"].inc(g.drain())

                    t_sw = b_hq2[:, 0:S]
                    t_t1 = b_hq2[:, S:2 * S]
                    for tensbuf in (b_qT, b_kT):
                        @blk.gpsimd
                        def _(g, tensbuf=tensbuf):
                            for cname_, c in ch.items():
                                if c.n and cname_ != "cc":
                                    g.wait_ge(c.sem, c.n)
                            dq_di.wait_all(g)
                            dq_dg.wait_all(g)
                            dq_dg.issue(g, lambda: g.dma_start(out=t_sw[0:64, :], in_=tensbuf[64:128, :]))
                            dq_dg.issue(g, lambda: g.dma_start(out=t_sw[64:128, :], in_=tensbuf[0:64, :]))

                        @blk.vector
                        def _(v, tensbuf=tensbuf):
                            dq_dg.wait_all(v)
                            v.tensor_tensor(t_t1[:, :], t_sw[:, :], c_ssin[:], ALU.mult)
                            v.drain()
                            v.tensor_tensor(tensbuf[:], tensbuf[:], c_cos[:], ALU.mult)
                            v.drain()
                            v.tensor_tensor(tensbuf[:], tensbuf[:], t_t1[:, :], ALU.add)
                            v.drain()
                            ch["v"].inc(v.drain())

                    if dbg and h == 0:
                        @blk.gpsimd
                        def _(g):
                            g.wait_ge(ch["v"].sem, ch["v"].n)
                            dq_dg.issue(g, lambda: g.dma_start(out=dbg_outs["o_qTr"][:, :], in_=b_qT[:]))
                            dq_dg.issue(g, lambda: g.dma_start(out=dbg_outs["o_kTr"][:, :], in_=b_kT[:]))
                            dq_dg.issue(g, lambda: g.dma_start(out=dbg_outs["o_vtm"][:, :], in_=b_v[:]))

                    for qt in range(NQT):
                        nkb = 4 * qt + 4
                        qsl = b_qT[:, qt * 512:(qt + 1) * 512]
                        t_base = ch["t"].n
                        a_base = ch["a"].n
                        v_base = ch["v"].n
                        n_diag_before = [0] * (nkb + 1)
                        for kb in range(nkb):
                            n_diag_before[kb + 1] = n_diag_before[kb] + (1 if kb >= 4 * qt else 0)

                        @blk.tensor
                        def _(t, qt=qt):
                            if qt == 0:
                                t.wait_ge(ch["v"].sem, ch["v"].n)
                            t.wait_ge(ch["p4"].sem, ch["p4"].n)
                        def emit_pv_pair(p):
                            for bb in range(2):
                                kb = 2 * p + bb
                                diag = kb >= 4 * qt

                                @blk.tensor
                                def _(t, kb=kb, p=p, bb=bb, diag=diag):
                                    if diag:
                                        t.wait_ge(ch["v"].sem, v_base + n_diag_before[kb + 1])
                                    else:
                                        t.wait_ge(ch["a"].sem, a_base + p + 1)
                                    pt = b_pt[:, (p % 2) * 1024 + bb * 512:
                                              (p % 2) * 1024 + (bb + 1) * 512]
                                    ch["t"].inc(t.matmul(p_o[:], b_v[:, kb * 128:(kb + 1) * 128], pt,
                                                         start=(kb == 0), stop=(kb == nkb - 1)))
                                    ch["t"].inc(t.matmul(p_l[:], c_ones_bf[:], pt,
                                                         start=(kb == 0), stop=(kb == nkb - 1)))

                        for p in range(nkb // 2):
                            for bb in range(2):
                                kb = 2 * p + bb

                                @blk.tensor
                                def _(t, kb=kb, p=p, bb=bb, a_base=a_base):
                                    if bb == 0 and p >= 2:
                                        t.wait_ge(ch["a"].sem, a_base + p - 1)
                                    ch["t"].inc(t.matmul(
                                        p_sp[:, (p % 2) * 1024 + bb * 512:(p % 2) * 1024 + (bb + 1) * 512],
                                        b_kT[:, kb * 128:(kb + 1) * 128], qsl,
                                        start=True, stop=True))

                            @blk.scalar
                            def _(s, p=p, t_base=t_base):
                                s.wait_ge(ch["t"].sem,
                                          t_base + (2 if p == 0 else 6 * p - 2))
                                s.activation(out=b_pt[:, (p % 2) * 1024:(p % 2 + 1) * 1024],
                                             in_=p_sp[:, (p % 2) * 1024:(p % 2 + 1) * 1024],
                                             func=AF.Exp, bias=c_negC[:, 0:1], scale=SCALE)
                                ch["a"].inc(s.drain())

                            for bb in range(2):
                                kb = 2 * p + bb
                                if kb >= 4 * qt:
                                    @blk.vector
                                    def _(v, kb=kb, p=p, bb=bb, qt=qt, a_base=a_base):
                                        v.wait_ge(ch["a"].sem, a_base + p + 1)
                                        d = kb - 4 * qt
                                        sl = b_pt[:, (p % 2) * 1024 + bb * 512:
                                                  (p % 2) * 1024 + (bb + 1) * 512]
                                        v.tensor_tensor(sl, sl,
                                                        c_masks[:, d * 512:(d + 1) * 512], ALU.mult)
                                        v.drain()
                                        ch["v"].inc(v.drain())

                            if p >= 1:
                                emit_pv_pair(p - 1)
                        emit_pv_pair(nkb // 2 - 1)

                        @blk.scalar
                        def _(s, qt=qt):
                            s.wait_ge(ch["t"].sem, ch["t"].n)
                            s.activation(out=b_ob[:], in_=p_o[:], func=AF.Identity)
                            ch["a"].inc(s.drain())

                        @blk.vector
                        def _(v, qt=qt):
                            v.wait_ge(ch["t"].sem, ch["t"].n)
                            v.tensor_copy(b_L[:], p_l[:])
                            v.drain()
                            v.reciprocal(b_rL[:], b_L[:])
                            v.drain()
                            ch["v"].inc(v.drain())

                        @blk.tensor
                        def _(t, qt=qt):
                            t.wait_ge(ch["v"].sem, ch["v"].n)
                            for i in range(4):
                                t.transpose(p_qk[:, i:i + 1],
                                            b_rL[:, i * 128:(i + 1) * 128], c_id_f[0:1, 0:1])
                            ch["t"].inc(t.drain())

                        @blk.vector
                        def _(v, qt=qt):
                            v.wait_ge(ch["t"].sem, ch["t"].n)
                            v.tensor_copy(b_rLc[:], p_qk[:, 0:4])
                            v.drain()
                            ch["v"].inc(v.drain())

                        for i in range(4):
                            @blk.tensor
                            def _(t, i=i, qt=qt):
                                t.wait_ge(ch["a"].sem, ch["a"].n)
                                if i > 0:
                                    t.wait_ge(ch["p1"].sem, ch["p1"].n)
                                t.transpose(p_tb_ap, b_ob[:, i * 128:(i + 1) * 128], c_id_bf[:])
                                ch["t"].inc(t.drain())

                            @blk.vector
                            def _(v, i=i, qt=qt, h=h):
                                v.wait_ge(ch["t"].sem, ch["t"].n)
                                v.wait_ge(ch["v"].sem, ch["v"].n)
                                v.tensor_scalar(sc_a[:, 0:128], p_tb_ap, b_rLc[:, i:i + 1],
                                                None, ALU.mult)
                                v.drain()
                                v.tensor_reduce(sc_g[:, 0:4].unsqueeze(2),
                                                sc_a[:, 0:128].rearrange("p (g r) -> p g r", r=32),
                                                AXX, ALU.max, apply_absolute_value=True)
                                v.drain()
                                v.tensor_scalar(sc_g[:, 0:4], sc_g[:, 0:4], qp_inv[:, 0:1],
                                                1e-12, ALU.mult, ALU.max)
                                v.drain()
                                v.reciprocal(sc_g[:, 4:8], sc_g[:, 0:4])
                                v.drain()
                                v.tensor_tensor(sc_b[:, 0:128].rearrange("p (g r) -> p g r", r=32),
                                                sc_a[:, 0:128].rearrange("p (g r) -> p g r", r=32),
                                                sc_g[:, 4:8].unsqueeze(2).broadcast_to([128, 4, 32]),
                                                ALU.mult)
                                v.drain()
                                v.tensor_copy(sc_b[:, 0:128].bitcast(I32), sc_b[:, 0:128])
                                v.drain()
                                v.tensor_copy(sc_b[:, 0:128], sc_b[:, 0:128].bitcast(I32))
                                v.drain()
                                v.tensor_tensor(stage[:, i * 128:(i + 1) * 128].rearrange(
                                                    "p (g r) -> p g r", r=32),
                                                sc_b[:, 0:128].rearrange("p (g r) -> p g r", r=32),
                                                sc_g[:, 0:4].unsqueeze(2).broadcast_to([128, 4, 32]),
                                                ALU.mult)
                                v.drain()
                                ch["p1"].inc(v.drain())

                        @blk.gpsimd
                        def _(g, qt=qt, h=h):
                            g.wait_ge(ch["p1"].sem, ch["p1"].n)
                            dq_dg.issue(g, lambda: g.dma_start(
                                out=d_qo_in.ap()[h * S + qt * 512:h * S + (qt + 1) * 512, :].rearrange(
                                    "(i q) d -> i q d", q=128).transpose([1, 0, 2]),
                                in_=stage[:, 0:512].rearrange("q (i d) -> q i d", d=128)))
                            dq_dg.wait_all(g)
                            ch["p4"].inc(g.drain())

                @blk.gpsimd
                def _(g):
                    for cname_, c in ch.items():
                        if c.n and cname_ != "cc":
                            g.wait_ge(c.sem, c.n)
                    dq_di.wait_all(g)
                    dq_dg.wait_all(g)
                    ch["cc"].inc(g.collective_compute(
                        "AllGather", ALU.bypass, replica_groups=[list(range(NC))],
                        ins=[d_qo_in.ap().opt()], outs=[d_qo.ap().opt()]))
                    if dbg:
                        g.wait_ge(ch["cc"].sem, ch["cc"].n)
                        dq_dg.issue(g, lambda: g.dma_start(out=dbg_outs["o_qo"][:, :], in_=d_qo[:, :]))
                cc_ag4 = ch["cc"].n
                bar(blk, cc_upto=cc_ag4 - 1)

            stk2.close()
            if upto == "b":
                emit_zero_y()
                return nc, dbg_outs
            WG = min(TPC, 512)
            NTG = TPC // WG
            d_c2t = nc.dram_tensor("d_c2t", [2 * RANK, INTER], BF16)
            d_ffn = nc.dram_tensor("d_ffn", [TPC, D], BF16)
            with ExitStack() as cfstk:
                sbCF = lambda n_, s_, d_: cfstk.enter_context(nc.sbuf_tensor(n_, s_, d_))
                cnegT = sbCF("cnegT", [NBLK, TPC], BF16)
                with ExitStack() as cstk:
                    sbC = lambda n_, s_, d_: cstk.enter_context(nc.sbuf_tensor(n_, s_, d_))
                    c_c1t = sbC("c_c1t", [128, 2 * NFC * RANK], BF16)
                    cq_oT = sbC("cq_oT", [128, NFC * TPC], BF16)
                    cs_b = sbC("cs_b", [128, D], F32)
                    cwo = sbC("cwo", [128, 2 * NFC * 128], BF16)
                    cx2g = sbC("cx2g", [128, (WG // 128) * D], BF16)
                    cxT = sbC("cxT", [128, 2 * NFC * 128], BF16)
                    cr1 = sbC("cr1", [16, 3 * TPC], F32)
                    cc2 = sbC("cc2", [16, 2 * 1024], BF16)
                    cidxf = sbC("cidxf", [128, NTT * NBLK], F32)
                    cidxu = sbC("cidxu", [128, NBLK], U32)
                    cm8 = sbC("cm8", [128, 8], F32)
                    cmi = sbC("cmi", [128, 8], U32)
                    cps_tb = cstk.enter_context(nc.psum_tensor("cps_tb", [128, 128], BF16))
                    cps_t = cstk.enter_context(nc.psum_tensor("cps_t", [128, 512], F32))
                    cps_c = cstk.enter_context(nc.psum_tensor("cps_c", [128, 512], F32))
                    cps_r = cstk.enter_context(nc.psum_tensor("cps_r", [16, 512], F32))
                    cps_i = cstk.enter_context(nc.psum_tensor("cps_i", [128, 512], F32))

                    @blk.sync
                    def _(sy):
                        for cname_, c in ch.items():
                            if c.n and cname_ != "cc":
                                sy.wait_ge(c.sem, c.n)
                        dq_di.wait_all(sy)
                        dq_dg.wait_all(sy)
                        dq_di.issue(sy, lambda: sy.dma_start(out=cs_b[0:RANK, :], in_=c1_in[:, :]))

                    for c in range(NFC):
                        @blk.tensor
                        def _(t, c=c):
                            dq_di.wait_all(t)
                            if c >= 1:
                                t.wait_ge(ch["p1"].sem, ch["p1"].n)
                            t.transpose(cps_t[:, 0:RANK], cs_b[0:RANK, c * 128:(c + 1) * 128],
                                        c_id_f[0:RANK, 0:RANK])
                            ch["t"].inc(t.drain())

                        @blk.vector
                        def _(v, c=c):
                            v.wait_ge(ch["t"].sem, ch["t"].n)
                            v.tensor_copy(sc_a[:, c * RANK:(c + 1) * RANK], cps_t[:, 0:RANK])
                            v.drain()
                            ch["p1"].inc(v.drain())

                    @blk.vector
                    def _(v):
                        v.wait_ge(ch["p1"].sem, ch["p1"].n)
                        v.tensor_copy(c_c1t[:, 0:NFC * RANK], sc_a[:, 0:NFC * RANK])
                        v.drain()
                        v.tensor_tensor(c_c1t[:, NFC * RANK:2 * NFC * RANK], sc_a[:, 0:NFC * RANK],
                                        c_c1t[:, 0:NFC * RANK], ALU.subtract)
                        v.drain()
                        ch["v"].inc(v.drain())

                    for rt in range(INTER // 128):
                        @blk.sync
                        def _(sy, rt=rt):
                            sy.wait_ge(ch["v"].sem, ch["v"].n)
                            if rt > 0:
                                sy.wait_ge(ch["p2"].sem, ch["p2"].n)
                            dq_di.issue(sy, lambda: sy.dma_start(out=cs_b[:, 0:RANK],
                                                      in_=c2_in[rt * 128:(rt + 1) * 128, :]))

                        @blk.tensor
                        def _(t, rt=rt):
                            dq_di.wait_all(t)
                            if rt > 0:
                                t.wait_ge(ch["p3"].sem, ch["p3"].n)
                            t.transpose(cps_t[0:RANK, 0:128], cs_b[:, 0:RANK], c_id_f[:])
                            ch["t"].inc(t.drain())

                        @blk.vector
                        def _(v, rt=rt):
                            v.wait_ge(ch["t"].sem, ch["t"].n)
                            v.tensor_copy(sc_a[0:RANK, 0:128], cps_t[0:RANK, 0:128])
                            v.drain()
                            ch["p3"].inc(v.drain())
                            v.tensor_copy(sc_bf[0:RANK, 0:128], sc_a[0:RANK, 0:128])
                            v.drain()
                            v.tensor_tensor(sc_bf[0:RANK, 128:256], sc_a[0:RANK, 0:128],
                                            sc_bf[0:RANK, 0:128], ALU.subtract)
                            v.drain()
                            ch["v"].inc(v.drain())

                        @blk.gpsimd
                        def _(g, rt=rt):
                            g.wait_ge(ch["v"].sem, ch["v"].n)
                            dq_dg.issue(g, lambda: g.dma_start(out=d_c2t.ap()[0:RANK, rt * 128:(rt + 1) * 128],
                                                     in_=sc_bf[0:RANK, 0:128]))
                            dq_dg.issue(g, lambda: g.dma_start(out=d_c2t.ap()[RANK:2 * RANK, rt * 128:(rt + 1) * 128],
                                                     in_=sc_bf[0:RANK, 128:256]))
                            dq_dg.wait_all(g)
                            ch["p2"].inc(g.drain())
                    bar(blk)

                    @blk.gpsimd
                    def _(g):
                        g.wait_ge(ch["cc"].sem, cc_ag4)
                        pid = g.partition_id()
                        dq_dg.issue(g, lambda: g.dma_start(
                            out=d_qostage.ap().rearrange("(ic t) d -> ic t d", t=TPC),
                            in_=d_qo.ap().rearrange("(ic s) d -> ic s d", s=S)[
                                :, bass.ds(pid * TPC, TPC), :]))

                    @blk.sync
                    def _(sy):
                        dq_dg.wait_all(sy)
                        for ic in range(NFC):
                            dq_di.issue(sy, lambda: sy.dma_start(
                                out=cq_oT[:, ic * TPC:(ic + 1) * TPC],
                                in_=d_qostage.ap()[ic * TPC:(ic + 1) * TPC, :],
                                transpose=True))

                    for tg in range(NTG):
                        for oc in range(NOC):
                            @blk.sync
                            def _(sy, tg=tg, oc=oc):
                                if tg == 0 and oc == 0:
                                    for cname_, c in ch.items():
                                        if c.n and cname_ != "cc":
                                            sy.wait_ge(c.sem, c.n)
                                    dq_di.wait_all(sy)
                                    dq_dg.wait_all(sy)
                                if oc >= 2:
                                    sy.wait_ge(ch["p1"].sem, ch["p1"].n - 1)
                                s2, ocl = oc // NOCL, oc % NOCL
                                dq_di.issue(sy, lambda: sy.dma_start(
                                    out=cwo[:, (oc % 2) * NFC * 128:(oc % 2 + 1) * NFC * 128],
                                    in_=d_qwoT.ap()[(s2 * NOCL + ocl) * 128:
                                                    (s2 * NOCL + ocl + 1) * 128, :]))

                            @blk.tensor
                            def _(t, tg=tg, oc=oc):
                                dq_di.wait_all(t)
                                if oc >= 2:
                                    t.wait_ge(ch["a"].sem, ch["a"].n - 1)
                                for c in range(NFC):
                                    t.matmul(cps_c[:, 0:WG] if oc % 2 == 0 else cps_t[:, 0:WG],
                                             cwo[:, (oc % 2) * NFC * 128 + c * 128:
                                                 (oc % 2) * NFC * 128 + (c + 1) * 128],
                                             cq_oT[:, c * TPC + tg * WG:c * TPC + tg * WG + WG],
                                             start=(c == 0), stop=(c == NFC - 1))
                                ch["t"].inc(t.drain())
                                ch["p1"].inc(t.drain())

                            @blk.scalar
                            def _(s, tg=tg, oc=oc):
                                s.wait_ge(ch["t"].sem, ch["t"].n)
                                s.activation(out=stage[:, 0:WG] if oc % 2 == 0 else stage[:, WG:2 * WG],
                                             in_=cps_c[:, 0:WG] if oc % 2 == 0 else cps_t[:, 0:WG],
                                             func=AF.Identity)
                                ch["a"].inc(s.drain())

                            for i in range(WG // 128):
                                @blk.tensor
                                def _(t, oc=oc, i=i):
                                    t.wait_ge(ch["a"].sem, ch["a"].n)
                                    t.wait_ge(ch["p2"].sem, ch["p2"].n)
                                    t.wait_ge(ch["p4"].sem, ch["p4"].n)
                                    t.transpose(cps_tb[:],
                                                (stage[:, 0:WG] if oc % 2 == 0 else
                                                 stage[:, WG:2 * WG])[:, i * 128:(i + 1) * 128],
                                                c_id_bf[:])
                                    ch["t"].inc(t.drain())

                                @blk.vector
                                def _(v, oc=oc, i=i):
                                    v.wait_ge(ch["t"].sem, ch["t"].n)
                                    v.tensor_copy(cx2g[:, i * D + oc * 128:i * D + (oc + 1) * 128],
                                                  cps_tb[:])
                                    v.drain()
                                    ch["p2"].inc(v.drain())

                        @blk.gpsimd
                        def _(g, tg=tg):
                            g.wait_ge(ch["p2"].sem, ch["p2"].n)
                            dq_dg.issue(g, lambda: g.dma_start(
                                out=d_aproj.ap()[tg * WG:(tg + 1) * WG, :].rearrange(
                                    "(i p) d -> i p d", p=128).transpose([1, 0, 2]),
                                in_=cx2g[:].rearrange("p (i d) -> p i d", d=D)))
                            dq_dg.wait_all(g)
                            ch["p4"].inc(g.drain())

                        for j in range(WG // 128):
                            @blk.sync
                            def _(sy, tg=tg, j=j):
                                sy.wait_ge(ch["p4"].sem, ch["p4"].n)
                                sy.wait_ge(ch["v"].sem, ch["v"].n)
                                sy.wait_ge(ch["t"].sem, ch["t"].n)
                                dq_di.issue(sy, lambda: sy.dma_start(
                                    out=sc_a[:], in_=x_in[tg * WG + j * 128:tg * WG + (j + 1) * 128, :]))

                            @blk.vector
                            def _(v, tg=tg, j=j):
                                dq_di.wait_all(v)
                                v.tensor_tensor(cs_b[:], sc_a[:], cx2g[:, j * D:(j + 1) * D], ALU.add)
                                v.drain()
                                v.reduce_sum(sc_st[:, 0:1], cs_b[:], AXX)
                                v.drain()
                                ch["v"].inc(v.drain())

                            @blk.scalar
                            def _(s):
                                s.wait_ge(ch["v"].sem, ch["v"].n)
                                s.activation(out=sc_a[:], in_=cs_b[:], func=AF.Square,
                                             accum_out=sc_st[:, 1:2])
                                ch["a"].inc(s.drain())

                            @blk.vector
                            def _(v):
                                v.wait_ge(ch["a"].sem, ch["a"].n)
                                v.tensor_scalar(sc_st[:, 2:3], sc_st[:, 0:1], float(1.0 / D), None, ALU.mult)
                                v.drain()
                                v.tensor_tensor(sc_st[:, 3:4], sc_st[:, 2:3], sc_st[:, 2:3], ALU.mult)
                                v.drain()
                                v.tensor_scalar(sc_st[:, 4:5], sc_st[:, 1:2], float(1.0 / D), None, ALU.mult)
                                v.drain()
                                v.tensor_tensor(sc_st[:, 4:5], sc_st[:, 4:5], sc_st[:, 3:4], ALU.subtract)
                                v.drain()
                                v.tensor_scalar(sc_st[:, 4:5], sc_st[:, 4:5], 1e-5, None, ALU.add)
                                v.drain()
                                ch["v"].inc(v.drain())

                            @blk.scalar
                            def _(s):
                                s.wait_ge(ch["v"].sem, ch["v"].n)
                                s.activation(out=sc_st[:, 5:6], in_=sc_st[:, 4:5], func=AF.Sqrt)
                                ch["a"].inc(s.drain())

                            @blk.vector
                            def _(v):
                                v.wait_ge(ch["a"].sem, ch["a"].n)
                                v.reciprocal(sc_st[:, 6:7], sc_st[:, 5:6])
                                v.drain()
                                v.tensor_tensor(sc_st[:, 7:8], sc_st[:, 2:3], sc_st[:, 6:7], ALU.mult)
                                v.drain()
                                v.tensor_scalar(sc_st[:, 7:8], sc_st[:, 7:8], -1.0, None, ALU.mult)
                                v.drain()
                                ch["v"].inc(v.drain())

                            @blk.scalar
                            def _(s):
                                s.wait_ge(ch["v"].sem, ch["v"].n)
                                s.activation(out=sc_a[:], in_=cs_b[:], func=AF.Identity,
                                             bias=sc_st[:, 7:8], scale=sc_st[:, 6:7])
                                ch["a"].inc(s.drain())

                            @blk.vector
                            def _(v):
                                v.wait_ge(ch["a"].sem, ch["a"].n)
                                v.tensor_copy(sc_bf[:], sc_a[:])
                                v.drain()
                                v.tensor_tensor(stage[:, 0:D], sc_a[:], sc_bf[:], ALU.subtract)
                                v.drain()
                                ch["v"].inc(v.drain())

                            base_p1 = ch["p1"].n
                            for c in range(2 * NFC):
                                src = sc_bf if c < NFC else stage
                                cc0 = c if c < NFC else c - NFC

                                @blk.tensor
                                def _(t, c=c, src=src, cc0=cc0, base_p1=base_p1):
                                    if c == 0:
                                        t.wait_ge(ch["v"].sem, ch["v"].n)
                                        t.wait_ge(ch["p3"].sem, ch["p3"].n)
                                    t.wait_ge(ch["p1"].sem, base_p1 + c)
                                    t.transpose(cps_tb[:], src[:, cc0 * 128:(cc0 + 1) * 128], c_id_bf[:])
                                    ch["t"].inc(t.drain())

                                @blk.vector
                                def _(v, c=c):
                                    v.wait_ge(ch["t"].sem, ch["t"].n)
                                    v.tensor_copy(cxT[:, c * 128:(c + 1) * 128], cps_tb[:])
                                    v.drain()
                                    ch["p1"].inc(v.drain())

                            @blk.gpsimd
                            def _(g, tg=tg, j=j):
                                g.wait_ge(ch["p1"].sem, ch["p1"].n)
                                for cg in range(0, NFC, 8):
                                    dq_dg.issue(g, lambda cg=cg: g.dma_start(
                                        out=d_xfT.ap().rearrange("(c f) t -> f c t", f=128)[
                                            :, cg:cg + 8, tg * WG + j * 128:tg * WG + (j + 1) * 128],
                                        in_=cxT[:, cg * 128:(cg + 8) * 128].rearrange(
                                            "f (c t) -> f c t", t=128)))
                                dq_dg.wait_all(g)
                                ch["p3"].inc(g.drain())

                            @blk.tensor
                            def _(t, tg=tg, j=j):
                                t.wait_ge(ch["p1"].sem, ch["p1"].n)
                                first = True
                                for c in range(NFC):
                                    for (la, lb) in [(0, 0), (0, 1), (1, 0)]:
                                        t.matmul(cps_r[:, j * 128:(j + 1) * 128],
                                                 c_c1t[:, (la * NFC + c) * RANK:(la * NFC + c + 1) * RANK],
                                                 cxT[:, (lb * NFC + c) * 128:(lb * NFC + c + 1) * 128],
                                                 start=first,
                                                 stop=(c == NFC - 1 and (la, lb) == (1, 0)))
                                        first = False
                                ch["t"].inc(t.drain())

                        @blk.vector
                        def _(v, tg=tg):
                            v.wait_ge(ch["t"].sem, ch["t"].n)
                            v.tensor_copy(cr1[:, tg * WG:(tg + 1) * WG], cps_r[:, 0:WG])
                            v.drain()
                            ch["v"].inc(v.drain())
                    bar(blk)

                    @blk.vector
                    def _(v):
                        v.tensor_copy(cr1.bitcast(BF16)[:, 2 * TPC:3 * TPC], cr1[:, 0:TPC])
                        v.drain()
                        v.tensor_tensor(cr1.bitcast(BF16)[:, 3 * TPC:4 * TPC], cr1[:, 0:TPC],
                                        cr1.bitcast(BF16)[:, 2 * TPC:3 * TPC], ALU.subtract)
                        v.drain()
                        ch["v"].inc(v.drain())
                    bar(blk)

                    for tt in range(NTT):
                        for rt in range(INTER // 512):
                            @blk.sync
                            def _(sy, tt=tt, rt=rt):
                                it = tt * (INTER // 512) + rt
                                if it >= 2:
                                    sy.wait_ge(ch["p1"].sem, ch["p1"].n - 1)
                                bsel = it % 2
                                dq_di.issue(sy, lambda: sy.dma_start(
                                    out=cc2[:, bsel * 1024:bsel * 1024 + 512],
                                    in_=d_c2t.ap()[0:RANK, rt * 512:(rt + 1) * 512]))
                                dq_di.issue(sy, lambda: sy.dma_start(
                                    out=cc2[:, bsel * 1024 + 512:bsel * 1024 + 1024],
                                    in_=d_c2t.ap()[RANK:2 * RANK, rt * 512:(rt + 1) * 512]))

                            @blk.tensor
                            def _(t, tt=tt, rt=rt):
                                dq_di.wait_all(t)
                                t.wait_ge(ch["a"].sem, ch["a"].n)
                                bsel = (tt * (INTER // 512) + rt) % 2
                                combos = [(2, 0), (2, 512), (3, 0)]
                                first = True
                                for (hl, off) in combos:
                                    t.matmul(cps_i[:],
                                             cr1.bitcast(BF16)[:, hl * TPC + tt * 128:
                                                               hl * TPC + (tt + 1) * 128],
                                             cc2[:, bsel * 1024 + off:bsel * 1024 + off + 512],
                                             start=first, stop=(hl, off) == (3, 0))
                                    first = False
                                ch["t"].inc(t.drain())
                                ch["p1"].inc(t.drain())

                            @blk.scalar
                            def _(s, tt=tt, rt=rt):
                                s.wait_ge(ch["t"].sem, ch["t"].n)
                                s.wait_ge(ch["v"].sem, ch["v"].n)
                                s.activation(out=sc_a[:, 0:512], in_=cps_i[:], func=AF.Identity)
                                ch["a"].inc(s.drain())

                            @blk.vector
                            def _(v, tt=tt, rt=rt):
                                v.wait_ge(ch["a"].sem, ch["a"].n)
                                for half in range(2):
                                    b = rt * 2 + half
                                    v.max(cm8[:], sc_a[:, half * 256:(half + 1) * 256])
                                    v.drain()
                                    v.max_index(cmi[:], cm8[:], sc_a[:, half * 256:(half + 1) * 256])
                                    v.drain()
                                    v.tensor_copy(cidxu[:, b:b + 1], cmi[:, 0:1])
                                    v.drain()
                                ch["v"].inc(v.drain())

                        @blk.vector
                        def _(v, tt=tt):
                            v.wait_ge(ch["v"].sem, ch["v"].n)
                            v.tensor_copy(cidxf[:, tt * NBLK:(tt + 1) * NBLK], cidxu[:])
                            v.drain()
                            v.tensor_copy(stage[:, 0:NBLK], cidxf[:, tt * NBLK:(tt + 1) * NBLK])
                            v.drain()
                            ch["v"].inc(v.drain())

                        @blk.tensor
                        def _(t, tt=tt):
                            t.wait_ge(ch["v"].sem, ch["v"].n)
                            t.wait_ge(ch["p2"].sem, ch["p2"].n)
                            t.transpose(cps_tb[0:NBLK, 0:128], stage[:, 0:NBLK], c_id_bf[:])
                            ch["t"].inc(t.drain())

                        @blk.vector
                        def _(v, tt=tt):
                            v.wait_ge(ch["t"].sem, ch["t"].n)
                            v.tensor_scalar(cnegT[:, tt * 128:(tt + 1) * 128], cps_tb[0:NBLK, 0:128],
                                            -1.0, None, ALU.mult)
                            v.drain()
                            ch["p2"].inc(v.drain())
                    bar(blk)

                    if dbg:
                        @blk.gpsimd
                        def _(g):
                            dq_dg.issue(g, lambda: g.dma_start(out=dbg_outs["o_idx"][:, :], in_=cidxf[:, :]))
                            dq_dg.issue(g, lambda: g.dma_start(out=dbg_outs["o_aproj"][:, :], in_=d_aproj[:, :]))
                            dq_dg.issue(g, lambda: g.dma_start(out=dbg_outs["o_xfT"][:, :], in_=d_xfT[:, :]))
                            dq_dg.wait_all(g)
                        bar(blk)

                if upto == "c":
                    emit_zero_y()
                    return nc, dbg_outs

                GRP = min(512, TPC)
                NG = TPC // GRP
                with ExitStack() as fstk:
                    sbF = lambda n_, s_, d_: fstk.enter_context(nc.sbuf_tensor(n_, s_, d_))
                    f_ht = sbF("f_ht", [128, RB * GRP], BF16)
                    f_xT = sbF("f_xT", [128, NFC * GRP], BF16)
                    f_w1 = sbF("f_w1", [128, 2 * NFC * 128], BF16)
                    f_w2 = sbF("f_w2", [128, 2 * 4 * 512], BF16)
                    f_er = stage[0:1, 2 * GRP:6 * GRP]
                    f_ones = stage[0:1, 6 * GRP:7 * GRP]
                    fps_p = fstk.enter_context(nc.psum_tensor("fps_p", [128, GRP], F32))
                    fps_p2 = fstk.enter_context(nc.psum_tensor("fps_p2", [128, GRP], F32))
                    fps_e = fstk.enter_context(nc.psum_tensor("fps_e", [128, GRP], F32))
                    fps_e2 = fstk.enter_context(nc.psum_tensor("fps_e2", [128, GRP], F32))
                    fps_oacc = [fstk.enter_context(nc.psum_tensor(f"fps_oa{i}", [128, 512], F32))
                                for i in range(GRP // 128)]

                    @blk.vector
                    def _(v):
                        v.memset(f_ones, 1.0)
                        v.drain()
                        ch["v"].inc(v.drain())
                    bar(blk)

                    @blk.sync
                    def _(sy):
                        sy.wait_ge(ch["cc"].sem, cc_w1w2)
                    @blk.tensor
                    def _(t):
                        t.wait_ge(ch["cc"].sem, cc_w1w2)

                    ffn_fences = []
                    for fg in range(NG):
                        glo = fg * GRP
                        @blk.sync
                        def _(sy, glo=glo):
                            for cname_, c in ch.items():
                                if c.n and cname_ != "cc":
                                    sy.wait_ge(c.sem, c.n)
                            dq_di.wait_all(sy)
                            dq_dg.wait_all(sy)
                            for cg in range(0, NFC, 8):
                                dq_di.issue(sy, lambda cg=cg: sy.dma_start(
                                    out=f_xT[:].rearrange("p (c t) -> p c t", t=GRP)[:, cg:cg + 8, :],
                                    in_=d_xfT.ap().rearrange("(c f) t -> f c t", f=128)[
                                        :, cg:cg + 8, glo:glo + GRP]))

                        er_toks = {}

                        @blk.gpsimd
                        def _(g, glo=glo):
                            for j0 in range(min(4, NBLK)):
                                er_toks[j0] = dq_dg.issue(g, lambda j0=j0: g.dma_start(
                                    out=f_er[:, (j0 % 4) * GRP:(j0 % 4 + 1) * GRP],
                                    in_=cnegT[j0:j0 + 1, glo:glo + GRP]))
                            ch["g"].inc(g.drain())

                        for rb in range(RB):
                            j = rb // 2
                            hi_half = rb % 2 == 1

                            @blk.sync
                            def _(sy, rb=rb, j=j, glo=glo):
                                if rb < 2:
                                    dq_di.wait_all(sy)
                                    sy.wait_ge(ch["g"].sem, ch["g"].n)
                                else:
                                    sy.wait_ge(ch["p1"].sem, ch["p1"].n - 1)
                                dq_di.issue(sy, lambda: sy.dma_start(
                                    out=f_w1[:, (rb % 2) * NFC * 128:(rb % 2 + 1) * NFC * 128],
                                    in_=d_w1t.ap()[rb * 128:(rb + 1) * 128, :]))

                            @blk.tensor
                            def _(t, rb=rb, j=j, hi_half=hi_half):
                                dq_di.wait_all(t)
                                if rb >= 2:
                                    t.wait_ge(ch["a"].sem, ch["a"].n - 1)
                                    t.wait_ge(ch["v"].sem, ch["v"].n - 1)
                                pp = fps_p if rb % 2 == 0 else fps_p2
                                for c in range(NFC):
                                    t.matmul(pp[:],
                                             f_w1[:, (rb % 2) * NFC * 128 + c * 128:
                                                  (rb % 2) * NFC * 128 + (c + 1) * 128],
                                             f_xT[:, c * GRP:(c + 1) * GRP],
                                             start=(c == 0), stop=(c == NFC - 1))
                                if not hi_half:
                                    t.wait_ge(*er_toks[j])
                                    pe = fps_e if j % 2 == 0 else fps_e2
                                    t.matmul(pe[:], c_iota_lo[:], f_ones, start=True, stop=False)
                                    t.matmul(pe[:], c_ones_row[:],
                                             f_er[:, (j % 4) * GRP:(j % 4 + 1) * GRP],
                                             start=False, stop=True)
                                    ch["p3"].inc(t.drain())
                                ch["t"].inc(t.drain())

                            if hi_half and j + 4 < NBLK:
                                @blk.gpsimd
                                def _(g, j=j, glo=glo):
                                    g.wait_ge(ch["p3"].sem, ch["p3"].n)
                                    er_toks[j + 4] = dq_dg.issue(g, lambda: g.dma_start(
                                        out=f_er[:, ((j + 4) % 4) * GRP:((j + 4) % 4 + 1) * GRP],
                                        in_=cnegT[j + 4:j + 5, glo:glo + GRP]))

                            @blk.scalar
                            def _(s, rb=rb):
                                s.wait_ge(ch["t"].sem, ch["t"].n)
                                pp = fps_p if rb % 2 == 0 else fps_p2
                                s.activation(out=stage[:, 0:GRP] if rb % 2 == 0 else
                                             stage[:, GRP:2 * GRP],
                                             in_=pp[:], func=AF.Relu)
                                ch["a"].inc(s.drain())

                            @blk.vector
                            def _(v, rb=rb, j=j, hi_half=hi_half):
                                v.wait_ge(ch["a"].sem, ch["a"].n)
                                pe = fps_e if j % 2 == 0 else fps_e2
                                v.scalar_tensor_tensor(
                                    f_ht[:, rb * GRP:(rb + 1) * GRP], pe[:],
                                    -128.0 if hi_half else 0.0,
                                    stage[:, 0:GRP] if rb % 2 == 0 else stage[:, GRP:2 * GRP],
                                    ALU.is_equal, ALU.mult)
                                v.drain()
                                ch["v"].inc(v.drain())
                                ch["p1"].inc(v.drain())

                        NWB = 4 if RB % 4 == 0 else 2
                        for fgrp in range(D // 512):
                            for rbp in range(RB // NWB):
                                @blk.sync
                                def _(sy, fgrp=fgrp, rbp=rbp):
                                    if fgrp == 0 and rbp == 0:
                                        sy.wait_ge(ch["v"].sem, ch["v"].n)
                                    it = fgrp * (RB // NWB) + rbp
                                    if it >= 2:
                                        sy.wait_ge(ch["p2"].sem, ch["p2"].n - 1)
                                    dq_di.issue(sy, lambda: sy.dma_start(
                                        out=f_w2[:, (rbp % 2) * NWB * 512:
                                                 (rbp % 2 + 1) * NWB * 512].rearrange(
                                            "p (b t) -> p b t", t=512),
                                        in_=d_w2.ap()[rbp * NWB * 128:(rbp + 1) * NWB * 128,
                                                      fgrp * 512:(fgrp + 1) * 512].rearrange(
                                            "(b f) t -> f b t", f=128)))

                                @blk.tensor
                                def _(t, fgrp=fgrp, rbp=rbp, glo=glo):
                                    dq_di.wait_all(t)
                                    if rbp == 0 and fgrp > 0:
                                        t.wait_ge(ch["a"].sem, ch["a"].n)
                                    for bb in range(NWB):
                                        rb = rbp * NWB + bb
                                        for i in range(GRP // 128):
                                            t.matmul(fps_oacc[i][:],
                                                     f_ht[:, rb * GRP + i * 128:rb * GRP + (i + 1) * 128],
                                                     f_w2[:, (rbp % 2) * NWB * 512 + bb * 512:
                                                          (rbp % 2) * NWB * 512 + (bb + 1) * 512],
                                                     start=(rb == 0), stop=(rb == RB - 1))
                                    ch["t"].inc(t.drain())
                                    ch["p2"].inc(t.drain())

                            for i in range(GRP // 128):
                                @blk.scalar
                                def _(s, fgrp=fgrp, i=i):
                                    if i == 0:
                                        s.wait_ge(ch["t"].sem, ch["t"].n)
                                    if len(ffn_fences) >= 4:
                                        s.wait_ge(*ffn_fences[-4])
                                    s.activation(out=sc_bf[:, (i % 4) * 512:(i % 4 + 1) * 512],
                                                 in_=fps_oacc[i][:],
                                                 func=AF.Identity)
                                    ch["a"].inc(s.drain())

                                @blk.gpsimd
                                def _(g, fgrp=fgrp, i=i, glo=glo):
                                    g.wait_ge(ch["a"].sem, ch["a"].n)
                                    tok = dq_dg.issue(g, lambda: g.dma_start(
                                        out=d_ffn.ap()[glo + i * 128:glo + (i + 1) * 128,
                                                       fgrp * 512:(fgrp + 1) * 512],
                                        in_=sc_bf[:, (i % 4) * 512:(i % 4 + 1) * 512]))
                                    ch["p4"].inc(g.drain())
                                    ffn_fences.append(tok)
                    bar(blk)

                    if dbg:
                        @blk.gpsimd
                        def _(g):
                            for cname_, c in ch.items():
                                if c.n and cname_ != "cc":
                                    g.wait_ge(c.sem, c.n)
                            dq_di.wait_all(g)
                            dq_dg.wait_all(g)
                            dq_dg.issue(g, lambda: g.dma_start(
                                out=dbg_outs["o_ffn"][:, :], in_=d_ffn.ap()[:, :]))
                            dq_dg.wait_all(g)
                            ch["g"].inc(g.drain())

                    for tt in range(NTT):
                        @blk.sync
                        def _(sy, tt=tt):
                            sy.wait_ge(ch["v"].sem, ch["v"].n)
                            dq_dg.wait_all(sy)
                            dq_di.issue(sy, lambda: sy.dma_start(out=sc_a[:], in_=x_in[tt * 128:(tt + 1) * 128, :]))
                            dq_di.issue(sy, lambda: sy.dma_start(out=sc_bf[:], in_=d_aproj.ap()[
                                tt * 128:(tt + 1) * 128, :]))
                            dq_di.issue(sy, lambda: sy.dma_start(out=stage[:, 0:D], in_=d_ffn.ap()[
                                tt * 128:(tt + 1) * 128, :]))

                        @blk.vector
                        def _(v, tt=tt):
                            dq_di.wait_all(v)
                            v.tensor_tensor(sc_a[:], sc_a[:], sc_bf[:], ALU.add)
                            v.drain()
                            v.tensor_tensor(sc_a[:], sc_a[:], stage[:, 0:D], ALU.add)
                            v.drain()
                            ch["v"].inc(v.drain())

                        @blk.gpsimd
                        def _(g, tt=tt):
                            g.wait_ge(ch["v"].sem, ch["v"].n)
                            dq_dg.issue(g, lambda: g.dma_start(out=y_out[tt * 128:(tt + 1) * 128, :], in_=sc_a[:]))

                    @blk.gpsimd
                    def _(g):
                        dq_dg.wait_all(g)
                    bar(blk)
        return nc, dbg_outs


def shard_inputs(cfg, inputs):
    S, D, H, HD, INTER, NC = cfg["S"], cfg["D"], cfg["H"], cfg["HD"], cfg["INTER"], cfg["NC"]
    TPC, HPC, OPC = S // NC, H // NC, D // NC
    RPC = INTER // NC
    x = np.ascontiguousarray(inputs["L0_attn_hidden_states"].reshape(S, D).astype(np.float32))
    maps = []
    for c in range(NC):
        m = {
            "x": np.ascontiguousarray(x[c * TPC:(c + 1) * TPC]),
            "wq": np.ascontiguousarray(inputs["L0_attn_Wq"][c * HPC * HD:(c + 1) * HPC * HD]),
            "wk": np.ascontiguousarray(inputs["L0_attn_Wk"][c * HPC * HD:(c + 1) * HPC * HD]),
            "wv": np.ascontiguousarray(inputs["L0_attn_Wv"][c * HPC * HD:(c + 1) * HPC * HD]),
            "wo": np.ascontiguousarray(inputs["L0_attn_Wo"][c * OPC:(c + 1) * OPC]),
            "c1": np.ascontiguousarray(inputs["L0_ffn_c1_w"]),
            "c2": np.ascontiguousarray(inputs["L0_ffn_c2_w"]),
            "w1": np.ascontiguousarray(inputs["L0_ffn_w1"][c * RPC:(c + 1) * RPC]),
            "w2": np.ascontiguousarray(inputs["L0_ffn_w2t"][c * RPC:(c + 1) * RPC]),
        }
        f0 = c * HPC * HD
        if f0 < cfg["P4"]:
            mq = 7.0
        elif f0 < cfg["P4"] + cfg["P6"]:
            mq = 31.0
        else:
            mq = 127.0
        qp = np.zeros((128, 2), np.float32)
        qp[:, 0] = 1.0 / mq
        qp[:, 1] = mq
        m["qparam"] = qp
        maps.append(m)
    return maps


def assemble(cfg, results):
    S, D = cfg["S"], cfg["D"]
    return np.concatenate([r["y"] for r in results], axis=0).reshape(1, S, D)


_CACHE = {}


def kernel(L0_attn_hidden_states, L0_attn_Wq, L0_attn_Wk, L0_attn_Wv, L0_attn_Wo,
           L0_ffn_c1_w, L0_ffn_c2_w, L0_ffn_w1, L0_ffn_w2t):
    cfg = dict(CFG_FULL)
    cfg["upto"] = "all"
    inputs = {
        "L0_attn_hidden_states": np.asarray(L0_attn_hidden_states),
        "L0_attn_Wq": np.asarray(L0_attn_Wq),
        "L0_attn_Wk": np.asarray(L0_attn_Wk),
        "L0_attn_Wv": np.asarray(L0_attn_Wv),
        "L0_attn_Wo": np.asarray(L0_attn_Wo),
        "L0_ffn_c1_w": np.asarray(L0_ffn_c1_w),
        "L0_ffn_c2_w": np.asarray(L0_ffn_c2_w),
        "L0_ffn_w1": np.asarray(L0_ffn_w1),
        "L0_ffn_w2t": np.asarray(L0_ffn_w2t),
    }
    if "nc" not in _CACHE:
        _CACHE["nc"] = build_nc(cfg)
    nc, _dbg = _CACHE["nc"]
    maps = shard_inputs(cfg, inputs)
    res = run_bass_kernel_spmd(nc, maps, core_ids=list(range(cfg["NC"])))
    return assemble(cfg, res.results)


# version 12
# speedup vs baseline: 1.2594x; 1.0508x over previous
import numpy as np
import ml_dtypes
from contextlib import ExitStack

import concourse.bass as bass
import concourse.mybir as mybir
from concourse.bass_utils import run_bass_kernel_spmd

F32 = mybir.dt.float32
BF16 = mybir.dt.bfloat16
I32 = mybir.dt.int32
U32 = mybir.dt.uint32
ALU = mybir.AluOpType
AF = mybir.ActivationFunctionType
AXX = mybir.AxisListType.X

CFG_SMALL = dict(S=1024, D=1024, H=8, HD=128, INTER=1024, RANK=16,
                 P4=512, P6=256, P8=256, NC=8, debug=True)
CFG_FULL = dict(S=8192, D=4096, H=32, HD=128, INTER=14336, RANK=16,
                P4=2048, P6=1024, P8=1024, NC=8, debug=False)

LN2PI_C1 = 6.28125
LN2PI_C2 = float(np.float32(2 * np.pi - 6.28125))
LN2PI_C3 = float(2 * np.pi - 6.28125 - float(np.float32(2 * np.pi - 6.28125)))
SOFTMAX_SHIFT = 28.0


class Chain:
    def __init__(self, stack, nc, name):
        self.sem = stack.enter_context(nc.semaphore(name))
        self.n = 0

    def inc(self, inst, k=1):
        inst.then_inc(self.sem, k)
        self.n += k
        return self.n

    def dma(self, inst):
        return self.inc(inst, 16)


class DQ:

    def __init__(self, stack, nc, name, k=4):
        self.sems = [stack.enter_context(nc.semaphore(f"{name}{i}")) for i in range(k)]
        self.cnt = [0] * k
        self.i = 0
        self.k = k

    def issue(self, eng, inst_fn):
        s = self.i % self.k
        if self.cnt[s]:
            eng.wait_ge(self.sems[s], 16 * self.cnt[s])
        inst = inst_fn()
        inst.then_inc(self.sems[s], 16)
        self.cnt[s] += 1
        self.i += 1
        return (self.sems[s], 16 * self.cnt[s])

    def wait_all(self, eng):
        for s in range(self.k):
            if self.cnt[s]:
                eng.wait_ge(self.sems[s], 16 * self.cnt[s])


def regions(cfg, ncols):
    out = []
    for (st, w, maxq) in [(0, cfg["P4"], 7.0),
                          (cfg["P4"], cfg["P6"], 31.0),
                          (cfg["P4"] + cfg["P6"], cfg["P8"], 127.0)]:
        if st < ncols:
            out.append((st, min(w, ncols - st), maxq))
    return out


def emit_quant(v, x_f32, out_bf, gmax, qf, cfg, ncols):
    for (st, w, maxq) in regions(cfg, ncols):
        g0, gw = st // 32, w // 32
        v.tensor_reduce(gmax[:, g0:g0 + gw].unsqueeze(2),
                        x_f32[:, st:st + w].rearrange("p (g r) -> p g r", r=32),
                        AXX, ALU.max, apply_absolute_value=True)
        v.drain()
        v.tensor_scalar(gmax[:, g0:g0 + gw], gmax[:, g0:g0 + gw],
                        float(1.0 / maxq), 1e-12, ALU.mult, ALU.max)
        v.drain()
    GW = ncols // 32
    for (st, w, maxq) in regions(cfg, ncols):
        g0, gw = st // 32, w // 32
        v.reciprocal(gmax[:, GW + g0:GW + g0 + gw], gmax[:, g0:g0 + gw])
        v.drain()
        v.tensor_tensor(qf[:, st:st + w].rearrange("p (g r) -> p g r", r=32),
                        x_f32[:, st:st + w].rearrange("p (g r) -> p g r", r=32),
                        gmax[:, GW + g0:GW + g0 + gw].unsqueeze(2).broadcast_to([128, gw, 32]),
                        ALU.mult)
        v.drain()
    v.tensor_copy(qf[:, 0:ncols].bitcast(I32), qf[:, 0:ncols])
    v.drain()
    v.tensor_copy(qf[:, 0:ncols], qf[:, 0:ncols].bitcast(I32))
    v.drain()
    for (st, w, maxq) in regions(cfg, ncols):
        g0, gw = st // 32, w // 32
        v.tensor_tensor(out_bf[:, st:st + w].rearrange("p (g r) -> p g r", r=32),
                        qf[:, st:st + w].rearrange("p (g r) -> p g r", r=32),
                        gmax[:, g0:g0 + gw].unsqueeze(2).broadcast_to([128, gw, 32]),
                        ALU.mult)
        v.drain()


def build_nc(cfg):
    S, D, H, HD = cfg["S"], cfg["D"], cfg["H"], cfg["HD"]
    INTER, RANK, NC = cfg["INTER"], cfg["RANK"], cfg["NC"]
    TPC = S // NC
    HPC = H // NC
    NFC = D // 128
    NTT = TPC // 128
    NQT = S // 512
    RB = INTER // 128
    RB_PC = RB // NC
    NBLK = INTER // 256
    NOC = NFC
    OPC = D // NC
    NOCL = OPC // 128
    dbg = cfg["debug"]
    upto = cfg.get("upto", "all")

    nc = bass.Bass("TRN2", target_bir_lowering=False, debug=False)

    x_in = nc.dram_tensor("x", [TPC, D], F32, kind="ExternalInput").ap()
    wq_in = nc.dram_tensor("wq", [HPC * HD, D], F32, kind="ExternalInput").ap()
    wk_in = nc.dram_tensor("wk", [HPC * HD, D], F32, kind="ExternalInput").ap()
    wv_in = nc.dram_tensor("wv", [HPC * HD, D], F32, kind="ExternalInput").ap()
    wo_in = nc.dram_tensor("wo", [OPC, D], F32, kind="ExternalInput").ap()
    c1_in = nc.dram_tensor("c1", [RANK, D], F32, kind="ExternalInput").ap()
    c2_in = nc.dram_tensor("c2", [INTER, RANK], F32, kind="ExternalInput").ap()
    w1_in = nc.dram_tensor("w1", [RB_PC * 128, D], F32, kind="ExternalInput").ap()
    w2_in = nc.dram_tensor("w2", [RB_PC * 128, D], F32, kind="ExternalInput").ap()
    qp_in = nc.dram_tensor("qparam", [128, 2], F32, kind="ExternalInput").ap()
    y_out = nc.dram_tensor("y", [TPC, D], F32, kind="ExternalOutput").ap()

    d_hqT_in = nc.dram_tensor("d_hqT_in", [D, TPC], BF16)
    d_hqT = nc.dram_tensor("d_hqT", [NC * D, TPC], BF16)
    d_qwoT_in = nc.dram_tensor("d_qwoT_in", [NOCL * 128, NFC * 128], BF16)
    d_qwoT = nc.dram_tensor("d_qwoT", [NC * NOCL * 128, NFC * 128], BF16)
    d_qo_in = nc.dram_tensor("d_qo_in", [HPC * S, HD], BF16)
    d_qo = nc.dram_tensor("d_qo", [H * S, HD], BF16)
    d_w1t_in = nc.dram_tensor("d_w1t_in", [RB_PC * 128, NFC * 128], BF16)
    d_w1t = nc.dram_tensor("d_w1t", [RB * 128, NFC * 128], BF16)
    d_w2_in = nc.dram_tensor("d_w2_in", [RB_PC * 128, D], BF16)
    d_w2 = nc.dram_tensor("d_w2", [RB * 128, D], BF16)
    d_aproj = nc.dram_tensor("d_aproj", [TPC, D], BF16)
    d_qostage = nc.dram_tensor("d_qostage", [(D // 128) * (S // NC), 128], BF16)
    d_xfT = nc.dram_tensor("d_xfT", [D, TPC], BF16)

    dbg_outs = {}

    def dout(name, shape, dt=F32):
        if dbg:
            dbg_outs[name] = nc.dram_tensor(name, shape, dt, kind="ExternalOutput").ap()
            return dbg_outs[name]
        return None

    o_hq = dout("o_hq", [TPC, D], BF16)
    o_hqT = dout("o_hqT", [NC * D, TPC], BF16)
    dout("o_qTr", [128, S], BF16)
    dout("o_kTr", [128, S], BF16)
    dout("o_vtm", [128, S], BF16)
    dout("o_qo", [H * S, HD], BF16)
    dout("o_idx", [128, (TPC // 128) * (INTER // 256)], F32)
    dout("o_ffn", [TPC, D], BF16)
    dout("o_ht", [128, (INTER // 128) * min(512, TPC)], BF16)
    dout("o_aproj", [TPC, D], BF16)
    dout("o_xfT", [D, TPC], BF16)

    stack = ExitStack()
    with stack:
        sb = lambda name, shape, dt: stack.enter_context(nc.sbuf_tensor(name, shape, dt))
        c_id_bf = sb("c_id_bf", [128, 128], BF16)
        c_id_f = sb("c_id_f", [128, 128], F32)
        c_ones_bf = sb("c_ones_bf", [128, 1], BF16)
        c_ones_row = sb("c_ones_row", [1, 128], BF16)
        c_iota_lo = sb("c_iota_lo", [1, 128], BF16)
        c_iota_hi = sb("c_iota_hi", [1, 128], BF16)
        c_negC = sb("c_negC", [128, 1], F32)

        stage = sb("stage", [128, D], BF16)
        sc_a = sb("sc_a", [128, D], F32)
        sc_g = sb("sc_g", [128, 2 * (D // 32)], F32)
        sc_bf = sb("sc_bf", [128, D], BF16)
        sc_st = sb("sc_st", [128, 16], F32)
        qp_inv = sb("qp_inv", [128, 2], F32)

        stk2 = ExitStack()
        sb2 = lambda name, shape, dt: stk2.enter_context(nc.sbuf_tensor(name, shape, dt))
        c_masks = sb2("c_masks", [128, 4 * 512], BF16)
        c_cos = sb2("c_cos", [128, S], BF16)
        c_ssin = sb2("c_ssin", [128, S], BF16)
        sc_b = sb2("sc_b", [128, D], F32)
        sc_bf2 = sb2("sc_bf2", [128, D], BF16)

        prestk = ExitStack()
        ps_t0h = prestk.enter_context(nc.psum_tensor("ps_t0", [128, 128], BF16))
        ps_t1h = prestk.enter_context(nc.psum_tensor("ps_t1", [128, 128], BF16))
        ps_t0 = ps_t0h[:]
        ps_t1 = ps_t1h[:]

        ch = {n: Chain(stack, nc, n) for n in
              ["v", "a", "t", "g", "cc", "p1", "p2", "p3", "p4"]}
        dq_di = DQ(stack, nc, "di", 8)
        dq_dg = DQ(stack, nc, "dg", 8)

        def bar(blk, engines=("sync", "gpsimd", "vector", "scalar", "tensor"),
                cc_upto=None):
            emap = {"sync": (blk.sync, None), "gpsimd": (blk.gpsimd, ch["g"]),
                    "vector": (blk.vector, ch["v"]), "scalar": (blk.scalar, ch["a"]),
                    "tensor": (blk.tensor, ch["t"])}
            for e in engines:
                dec, c = emap[e]
                if c is not None:
                    @dec
                    def _(eng, c=c):
                        c.inc(eng.drain())
            counts = [(ch[k].sem, ch[k].n) for k in ["v", "a", "t", "g"]]
            counts.append((ch["cc"].sem, cc_upto if cc_upto is not None else ch["cc"].n))
            counts += [(dq_di.sems[s], 16 * dq_di.cnt[s]) for s in range(dq_di.k)]
            counts += [(dq_dg.sems[s], 16 * dq_dg.cnt[s]) for s in range(dq_dg.k)]
            for e in engines:
                dec, _c = emap[e]
                @dec
                def _(eng, counts=counts):
                    for sem, n in counts:
                        if n:
                            eng.wait_ge(sem, n)

        with nc.Block() as blk:
            @blk.gpsimd
            def _(g):
                g.iota(sc_a[:, 0:128], pattern=[[1, 128]], base=0, channel_multiplier=0,
                       allow_small_or_imprecise_dtypes=True)
                g.iota(sc_a[:, 128:129], pattern=[[0, 1]], base=0, channel_multiplier=1,
                       allow_small_or_imprecise_dtypes=True)
                g.iota(sc_b[:, 0:512], pattern=[[1, 512]], base=0, channel_multiplier=-1,
                       allow_small_or_imprecise_dtypes=True)
                g.iota(sc_st[:, 0:1], pattern=[[0, 1]], base=0, channel_multiplier=1,
                       allow_small_or_imprecise_dtypes=True)
                dq_dg.issue(g, lambda: g.dma_start(out=qp_inv[:], in_=qp_in[:, :]))
                ch["g"].inc(g.drain())

            @blk.vector
            def _(v):
                v.wait_ge(ch["g"].sem, ch["g"].n)
                dq_dg.wait_all(v)
                v.tensor_scalar(c_id_f[:], sc_a[:, 0:128], sc_a[:, 128:129], None, ALU.is_equal)
                v.drain()
                v.tensor_copy(c_id_bf[:], c_id_f[:])
                v.memset(c_ones_bf[:], 1.0)
                v.memset(c_negC[:], -SOFTMAX_SHIFT)
                v.memset(c_ones_row[:], 1.0)
                for j in range(4):
                    v.tensor_scalar(c_masks[:, j * 512:(j + 1) * 512], sc_b[:, 0:512],
                                    float(j * 128), None, ALU.is_ge)
                v.drain()
                v.tensor_copy(c_iota_lo[:], sc_a[0:1, 0:128])
                v.drain()
                v.tensor_scalar(c_iota_hi[:], sc_a[0:1, 0:128], 128.0, None, ALU.add)
                v.drain()
                v.tensor_scalar(sc_st[:, 1:2], sc_st[:, 0:1], 64.0, -64.0, ALU.is_ge, ALU.mult)
                v.drain()
                v.tensor_tensor(sc_st[:, 2:3], sc_st[:, 0:1], sc_st[:, 1:2], ALU.add)
                v.drain()
                ch["v"].inc(v.drain())

            @blk.scalar
            def _(s):
                s.wait_ge(ch["v"].sem, ch["v"].n)
                s.activation(out=sc_st[:, 3:4], in_=sc_st[:, 2:3], func=AF.Exp,
                             bias=0.0, scale=float(-np.log(10000.0) / 64.0))
                ch["a"].inc(s.drain())
            bar(blk)

            w_rp = min(S, D)
            for rp in range(S // w_rp):
                lo, hi = rp * w_rp, (rp + 1) * w_rp

                @blk.gpsimd
                def _(g, lo=lo):
                    g.wait_ge(ch["v"].sem, ch["v"].n)
                    g.wait_ge(ch["a"].sem, ch["a"].n)
                    g.iota(sc_a[:, 0:w_rp], pattern=[[1, w_rp]], base=lo,
                           channel_multiplier=0, allow_small_or_imprecise_dtypes=True)
                    ch["g"].inc(g.drain())

                @blk.vector
                def _(v):
                    v.wait_ge(ch["g"].sem, ch["g"].n)
                    v.tensor_scalar(sc_b[:, 0:w_rp], sc_a[:, 0:w_rp], sc_st[:, 3:4], None, ALU.mult)
                    v.drain()
                    v.tensor_scalar(sc_a[:, 0:w_rp], sc_b[:, 0:w_rp],
                                    float(1.0 / (2 * np.pi)), None, ALU.mult)
                    v.drain()
                    v.tensor_copy(sc_a[:, 0:w_rp].bitcast(I32), sc_a[:, 0:w_rp])
                    v.drain()
                    v.tensor_copy(sc_a[:, 0:w_rp], sc_a[:, 0:w_rp].bitcast(I32))
                    v.drain()
                    for cc_ in (LN2PI_C1, LN2PI_C2, LN2PI_C3):
                        v.scalar_tensor_tensor(sc_b[:, 0:w_rp], sc_a[:, 0:w_rp],
                                               float(-cc_), sc_b[:, 0:w_rp],
                                               ALU.mult, ALU.add)
                        v.drain()
                    v.tensor_scalar(sc_a[:, 0:w_rp], sc_b[:, 0:w_rp], float(np.pi),
                                    float(-2 * np.pi), ALU.is_gt, ALU.mult)
                    v.drain()
                    v.tensor_tensor(sc_b[:, 0:w_rp], sc_b[:, 0:w_rp], sc_a[:, 0:w_rp], ALU.add)
                    v.drain()
                    v.tensor_scalar(sc_a[:, 0:w_rp], sc_b[:, 0:w_rp], float(-np.pi),
                                    float(2 * np.pi), ALU.is_lt, ALU.mult)
                    v.drain()
                    v.tensor_tensor(sc_b[:, 0:w_rp], sc_b[:, 0:w_rp], sc_a[:, 0:w_rp], ALU.add)
                    v.drain()
                    ch["v"].inc(v.drain())

                @blk.scalar
                def _(s):
                    s.wait_ge(ch["v"].sem, ch["v"].n)
                    s.activation(out=sc_a[:, 0:w_rp], in_=sc_b[:, 0:w_rp], func=AF.Sin)
                    ch["a"].inc(s.drain())

                @blk.vector
                def _(v, lo=lo, hi=hi):
                    v.wait_ge(ch["a"].sem, ch["a"].n)
                    v.tensor_scalar(c_ssin[0:64, lo:hi], sc_a[0:64, 0:w_rp], -1.0, None, ALU.mult)
                    v.drain()
                    v.tensor_copy(c_ssin[64:128, lo:hi], sc_a[64:128, 0:w_rp])
                    v.drain()
                    v.tensor_scalar(sc_b[:, 0:w_rp], sc_b[:, 0:w_rp], float(np.pi / 2),
                                    None, ALU.add)
                    v.drain()
                    v.tensor_scalar(sc_a[:, 0:w_rp], sc_b[:, 0:w_rp], float(np.pi),
                                    float(-2 * np.pi), ALU.is_gt, ALU.mult)
                    v.drain()
                    v.tensor_tensor(sc_b[:, 0:w_rp], sc_b[:, 0:w_rp], sc_a[:, 0:w_rp], ALU.add)
                    v.drain()
                    ch["v"].inc(v.drain())

                @blk.scalar
                def _(s):
                    s.wait_ge(ch["v"].sem, ch["v"].n)
                    s.activation(out=sc_a[:, 0:w_rp], in_=sc_b[:, 0:w_rp], func=AF.Sin)
                    ch["a"].inc(s.drain())

                @blk.vector
                def _(v, lo=lo, hi=hi):
                    v.wait_ge(ch["a"].sem, ch["a"].n)
                    v.tensor_copy(c_cos[:, lo:hi], sc_a[:, 0:w_rp])
                    v.drain()
                    ch["v"].inc(v.drain())
            bar(blk)

            with ExitStack() as p0stk:
                sbP = lambda n_, s_, d_: p0stk.enter_context(nc.sbuf_tensor(n_, s_, d_))
                p0_a = sbP("p0_a", [128, D], F32)
                p0_b = sbP("p0_b", [128, D], F32)
                p0_bf = sbP("p0_bf", [128, D], BF16)
                p0_g = sbP("p0_g", [128, 2 * (D // 32)], F32)
                p0_stage = sbP("p0_stage", [128, D], BF16)
                sets = [
                    dict(a=sc_a, b=sc_b, bf=sc_bf, g=sc_g, st=sc_st[:, 0:8], stg=stage),
                    dict(a=p0_a, b=p0_b, bf=p0_bf, g=p0_g, st=sc_st[:, 8:16], stg=p0_stage),
                ]
                qfence = [None, None]
                stok = [None, None]
                for tt in range(NTT):
                    S_ = sets[tt % 2]
                    A_, Bb_, BF_, G_, ST_, STG_ = (S_["a"], S_["b"], S_["bf"],
                                                   S_["g"], S_["st"], S_["stg"])

                    @blk.sync
                    def _(sy, tt=tt, A_=A_):
                        if tt < 2:
                            for cname_, c in ch.items():
                                if c.n and cname_ != "cc":
                                    sy.wait_ge(c.sem, c.n)
                            dq_dg.wait_all(sy)
                        else:
                            sy.wait_ge(ch["v"].sem, qfence[tt % 2])
                            sy.wait_ge(*stok[tt % 2])
                        dq_di.issue(sy, lambda: sy.dma_start(
                            out=A_[:], in_=x_in[tt * 128:(tt + 1) * 128, :]))

                    @blk.vector
                    def _(v, A_=A_, ST_=ST_):
                        dq_di.wait_all(v)
                        v.reduce_sum(ST_[:, 0:1], A_[:], AXX)
                        v.drain()
                        ch["v"].inc(v.drain())

                    @blk.scalar
                    def _(s, A_=A_, Bb_=Bb_, ST_=ST_):
                        dq_di.wait_all(s)
                        ch["a"].inc(s.activation(out=Bb_[:], in_=A_[:], func=AF.Square,
                                                 accum_out=ST_[:, 1:2]))

                    @blk.vector
                    def _(v, ST_=ST_):
                        v.wait_ge(ch["a"].sem, ch["a"].n)
                        v.tensor_scalar(ST_[:, 2:3], ST_[:, 0:1], float(1.0 / D), None, ALU.mult)
                        v.drain()
                        v.tensor_tensor(ST_[:, 3:4], ST_[:, 2:3], ST_[:, 2:3], ALU.mult)
                        v.drain()
                        v.tensor_scalar(ST_[:, 4:5], ST_[:, 1:2], float(1.0 / D), None, ALU.mult)
                        v.drain()
                        v.tensor_tensor(ST_[:, 4:5], ST_[:, 4:5], ST_[:, 3:4], ALU.subtract)
                        v.drain()
                        v.tensor_scalar(ST_[:, 4:5], ST_[:, 4:5], 1e-5, None, ALU.add)
                        v.drain()
                        ch["v"].inc(v.drain())

                    @blk.scalar
                    def _(s, ST_=ST_):
                        s.wait_ge(ch["v"].sem, ch["v"].n)
                        ch["a"].inc(s.activation(out=ST_[:, 5:6], in_=ST_[:, 4:5], func=AF.Sqrt))

                    @blk.vector
                    def _(v, ST_=ST_):
                        v.wait_ge(ch["a"].sem, ch["a"].n)
                        v.reciprocal(ST_[:, 6:7], ST_[:, 5:6])
                        v.drain()
                        v.tensor_tensor(ST_[:, 7:8], ST_[:, 2:3], ST_[:, 6:7], ALU.mult)
                        v.drain()
                        v.tensor_scalar(ST_[:, 7:8], ST_[:, 7:8], -1.0, None, ALU.mult)
                        v.drain()
                        ch["v"].inc(v.drain())

                    @blk.scalar
                    def _(s, A_=A_, Bb_=Bb_, ST_=ST_):
                        s.wait_ge(ch["v"].sem, ch["v"].n)
                        ch["a"].inc(s.activation(out=Bb_[:], in_=A_[:], func=AF.Identity,
                                                 bias=ST_[:, 7:8], scale=ST_[:, 6:7]))

                    @blk.vector
                    def _(v, tt=tt, A_=A_, Bb_=Bb_, BF_=BF_, G_=G_):
                        v.wait_ge(ch["a"].sem, ch["a"].n)
                        emit_quant(v, Bb_, BF_, G_, A_, cfg, D)
                        qfence[tt % 2] = ch["v"].inc(v.drain())

                    if dbg:
                        @blk.gpsimd
                        def _(g, tt=tt, BF_=BF_):
                            g.wait_ge(ch["v"].sem, ch["v"].n)
                            dq_dg.issue(g, lambda: g.dma_start(
                                out=o_hq[tt * 128:(tt + 1) * 128, :], in_=BF_[:]))

                    for c in range(NFC):
                        @blk.tensor
                        def _(t, tt=tt, c=c, BF_=BF_):
                            if c == 0:
                                t.wait_ge(ch["v"].sem, ch["v"].n)
                                if tt >= 2:
                                    t.wait_ge(*stok[tt % 2])
                            t.wait_ge(ch["p1"].sem, max(ch["p1"].n - 1, 0))
                            t.transpose(ps_t0 if c % 2 == 0 else ps_t1,
                                        BF_[:, c * 128:(c + 1) * 128], c_id_bf[:])
                            ch["t"].inc(t.drain())

                        @blk.vector
                        def _(v, c=c, STG_=STG_):
                            v.wait_ge(ch["t"].sem, ch["t"].n)
                            ch["p1"].inc(v.tensor_copy(STG_[:, c * 128:(c + 1) * 128],
                                                       ps_t0 if c % 2 == 0 else ps_t1))

                    @blk.gpsimd
                    def _(g, tt=tt, STG_=STG_):
                        g.wait_ge(ch["p1"].sem, ch["p1"].n)
                        for cg in range(0, NFC, 8):
                            tok = dq_dg.issue(g, lambda cg=cg: g.dma_start(
                                out=d_hqT_in.ap().rearrange("(c f) t -> c f t", f=128)[
                                    cg:cg + 8, :, tt * 128:(tt + 1) * 128].transpose([1, 0, 2]),
                                in_=STG_[:, cg * 128:(cg + 8) * 128].rearrange(
                                    "f (c t) -> f c t", t=128)))
                        stok[tt % 2] = tok

            @blk.gpsimd
            def _(g):
                dq_dg.wait_all(g)
                ch["cc"].inc(g.collective_compute(
                    "AllGather", ALU.bypass, replica_groups=[list(range(NC))],
                    ins=[d_hqT_in.ap().opt()], outs=[d_hqT.ap().opt()]))
                g.wait_ge(ch["cc"].sem, ch["cc"].n)
                if dbg:
                    dq_dg.issue(g, lambda: g.dma_start(out=o_hqT[:, :], in_=d_hqT[:, :]))
                dq_dg.wait_all(g)
            bar(blk)

            def emit_zero_y():
                @blk.vector
                def _(v):
                    v.memset(sc_a[:], 0.0)
                    ch["v"].inc(v.drain())

                @blk.gpsimd
                def _(g):
                    g.wait_ge(ch["v"].sem, ch["v"].n)
                    for tt in range(NTT):
                        dq_dg.issue(g, lambda: g.dma_start(out=y_out[tt * 128:(tt + 1) * 128, :], in_=sc_a[:]))
                    dq_dg.wait_all(g)

            if upto == "p0":
                emit_zero_y()
                stk2.close()
                return nc, dbg_outs

            def transpose_to_stage(n_chunks, src_bf, pst0, pst1, dst=None, dst_off=0):
                d = stage if dst is None else dst
                base = ch["p1"].n
                for c in range(n_chunks):
                    @blk.tensor
                    def _(t, c=c, base=base):
                        if c == 0:
                            t.wait_ge(ch["v"].sem, ch["v"].n)
                        t.wait_ge(ch["p1"].sem, max(base, base + c - 1))
                        t.transpose(pst0 if c % 2 == 0 else pst1,
                                    src_bf[:, c * 128:(c + 1) * 128], c_id_bf[:])
                        ch["t"].inc(t.drain())

                    @blk.vector
                    def _(v, c=c):
                        v.wait_ge(ch["t"].sem, ch["t"].n)
                        v.tensor_copy(d[:, dst_off + c * 128:dst_off + (c + 1) * 128],
                                      pst0 if c % 2 == 0 else pst1)
                        v.drain()
                        ch["p1"].inc(v.drain())

            for ocl in range(NOCL):
                @blk.sync
                def _(sy, ocl=ocl):
                    for cname_, c in ch.items():
                        if c.n and cname_ != "cc":
                            sy.wait_ge(c.sem, c.n)
                    dq_di.wait_all(sy)
                    dq_dg.wait_all(sy)
                    dq_di.issue(sy, lambda: sy.dma_start(out=sc_a[:], in_=wo_in[ocl * 128:(ocl + 1) * 128, :]))

                @blk.vector
                def _(v, ocl=ocl):
                    dq_di.wait_all(v)
                    emit_quant(v, sc_a, sc_bf, sc_g, sc_b, cfg, D)
                    ch["v"].inc(v.drain())

                transpose_to_stage(NFC, sc_bf, ps_t0, ps_t1)

                @blk.gpsimd
                def _(g, ocl=ocl):
                    g.wait_ge(ch["p1"].sem, ch["p1"].n)
                    dq_dg.issue(g, lambda: g.dma_start(
                        out=d_qwoT_in.ap()[ocl * 128:(ocl + 1) * 128, :],
                        in_=stage[:]))

            @blk.gpsimd
            def _(g):
                dq_dg.wait_all(g)
                ch["cc"].inc(g.collective_compute(
                    "AllGather", ALU.bypass, replica_groups=[list(range(NC))],
                    ins=[d_qwoT_in.ap().opt()], outs=[d_qwoT.ap().opt()]))

            w1_tok = {}
            w2_tok = {}
            w1s_tok = {}
            w2s_tok = {}
            for rbl in range(RB_PC):
                @blk.gpsimd
                def _(g, rbl=rbl):
                    if rbl == 0:
                        for cname_, c in ch.items():
                            if c.n and cname_ != "cc":
                                g.wait_ge(c.sem, c.n)
                        dq_di.wait_all(g)
                        dq_dg.wait_all(g)
                    else:
                        g.wait_ge(ch["p1"].sem, ch["p1"].n)
                    w1_tok[rbl] = dq_dg.issue(g, lambda: g.dma_start(
                        out=sc_bf[:], in_=w1_in[rbl * 128:(rbl + 1) * 128, :]))
                    if rbl > 0:
                        g.wait_ge(*w2s_tok[rbl - 1])
                    w2_tok[rbl] = dq_dg.issue(g, lambda: g.dma_start(
                        out=sc_bf2[:], in_=w2_in[rbl * 128:(rbl + 1) * 128, :]))
                    g.wait_ge(*w2_tok[rbl])
                    w2s_tok[rbl] = dq_dg.issue(g, lambda: g.dma_start(
                        out=d_w2_in.ap()[rbl * 128:(rbl + 1) * 128, :], in_=sc_bf2[:]))

                @blk.vector
                def _(v, rbl=rbl):
                    v.wait_ge(*w1_tok[rbl])
                    if rbl > 0:
                        v.wait_ge(*w1s_tok[rbl - 1])
                    ch["v"].inc(v.drain())

                transpose_to_stage(NFC, sc_bf, ps_t0, ps_t1)

                @blk.gpsimd
                def _(g, rbl=rbl):
                    g.wait_ge(ch["p1"].sem, ch["p1"].n)
                    w1s_tok[rbl] = dq_dg.issue(g, lambda: g.dma_start(
                        out=d_w1t_in.ap()[rbl * 128:(rbl + 1) * 128, :],
                        in_=stage[:]))

            @blk.gpsimd
            def _(g):
                dq_dg.wait_all(g)
                ch["cc"].inc(g.collective_compute(
                    "AllGather", ALU.bypass, replica_groups=[list(range(NC))],
                    ins=[d_w1t_in.ap().opt()], outs=[d_w1t.ap().opt()]))
                ch["cc"].inc(g.collective_compute(
                    "AllGather", ALU.bypass, replica_groups=[list(range(NC))],
                    ins=[d_w2_in.ap().opt()], outs=[d_w2.ap().opt()]))
            cc_w1w2 = ch["cc"].n
            bar(blk, cc_upto=cc_w1w2 - 3)

            if upto == "p1":
                emit_zero_y()
                stk2.close()
                return nc, dbg_outs

            prestk.close()
            TT1 = 256
            NP1 = S // TT1
            SCALE = float(1.0 / np.sqrt(HD))
            with ExitStack() as bstk:
                sbB = lambda name, shape, dt: bstk.enter_context(nc.sbuf_tensor(name, shape, dt))
                b_qw = sbB("b_qw", [128, 3 * D], BF16)
                b_qT = sbB("b_qT", [128, S], BF16)
                b_kT = sbB("b_kT", [128, S], BF16)
                b_v = sbB("b_v", [128, S], BF16)
                b_hq2 = sbB("b_hq2", [128, 2 * NFC * TT1], BF16)
                b_pt = sbB("b_pt", [128, 4 * 512], BF16)
                b_ob = sbB("b_ob", [128, 512], BF16)
                b_L = sbB("b_L", [1, 512], F32)
                b_rL = sbB("b_rL", [1, 512], F32)
                b_rLc = sbB("b_rLc", [128, 4], F32)
                p_qk = bstk.enter_context(nc.psum_tensor("p_qk", [128, 2 * TT1], F32))
                p_v = bstk.enter_context(nc.psum_tensor("p_v", [128, TT1], F32))
                p_sp = bstk.enter_context(nc.psum_tensor("p_sp", [128, 2048], F32))
                p_o = bstk.enter_context(nc.psum_tensor("p_o", [128, 512], F32))
                p_l = bstk.enter_context(nc.psum_tensor("p_l", [1, 512], F32))
                p_tb_ap = p_v[:, 128:192].bitcast(BF16)

                for h in range(HPC):
                    for pj, w_ext in enumerate([wq_in, wk_in, wv_in]):
                        @blk.sync
                        def _(sy, pj=pj, w_ext=w_ext, h=h):
                            for cname_, c in ch.items():
                                if c.n and cname_ != "cc":
                                    sy.wait_ge(c.sem, c.n)
                            dq_di.wait_all(sy)
                            dq_dg.wait_all(sy)
                            dq_di.issue(sy, lambda: sy.dma_start(
                                out=sc_a[:], in_=w_ext[h * 128:(h + 1) * 128, :]))

                        @blk.vector
                        def _(v, pj=pj):
                            dq_di.wait_all(v)
                            emit_quant(v, sc_a, sc_bf, sc_g, sc_b, cfg, D)
                            ch["v"].inc(v.drain())

                        transpose_to_stage(NFC, sc_bf, p_sp[:, 0:64].bitcast(BF16),
                                           p_sp[:, 512:576].bitcast(BF16),
                                           dst=b_qw, dst_off=pj * D)

                    for tt in range(NP1):
                        glo = tt * TT1
                        buf = tt % 2
                        boff = buf * NFC * TT1
                        @blk.sync
                        def _(sy, glo=glo, boff=boff, tt=tt):
                            if tt < 2:
                                for cname_, c in ch.items():
                                    if c.n and cname_ != "cc":
                                        sy.wait_ge(c.sem, c.n)
                                dq_di.wait_all(sy)
                                dq_dg.wait_all(sy)
                            else:
                                sy.wait_ge(ch["p2"].sem, ch["p2"].n - 1)
                            pieces = []
                            t0 = glo
                            while t0 < glo + TT1:
                                s_id = t0 // TPC
                                t1 = min(glo + TT1, (s_id + 1) * TPC)
                                pieces.append((t0, t1, s_id))
                                t0 = t1
                            for (a0, a1, s_id) in pieces:
                                for cg in range(0, NFC, 8):
                                    dq_di.issue(sy, lambda a0=a0, a1=a1, s_id=s_id, cg=cg: sy.dma_start(
                                        out=b_hq2[:, boff:boff + NFC * TT1].rearrange(
                                            "p (c t) -> p c t", t=TT1)[:, cg:cg + 8, a0 - glo:a1 - glo],
                                        in_=d_hqT.ap().rearrange("(s c f) t -> s c f t", c=NFC, f=128)[
                                            s_id, cg:cg + 8, :,
                                            a0 - s_id * TPC:a1 - s_id * TPC].transpose([1, 0, 2])))

                        @blk.tensor
                        def _(t, glo=glo, boff=boff, tt=tt):
                            dq_di.wait_all(t)
                            if tt >= 1:
                                t.wait_ge(ch["p3"].sem, ch["p3"].n)
                            for pj in range(2):
                                for c in range(NFC):
                                    t.matmul(p_qk[:, pj * TT1:(pj + 1) * TT1],
                                             b_qw[:, pj * D + c * 128:pj * D + (c + 1) * 128],
                                             b_hq2[:, boff + c * TT1:boff + (c + 1) * TT1],
                                             start=(c == 0), stop=(c == NFC - 1))
                            for sub in range(TT1 // 128):
                                for c in range(NFC):
                                    t.matmul(p_v[:, sub * 128:(sub + 1) * 128],
                                             b_hq2[:, boff + c * TT1 + sub * 128:boff + c * TT1 + (sub + 1) * 128],
                                             b_qw[:, 2 * D + c * 128:2 * D + (c + 1) * 128],
                                             start=(c == 0), stop=(c == NFC - 1))
                            ch["t"].inc(t.drain())
                            ch["p2"].inc(t.drain())

                        @blk.scalar
                        def _(s, glo=glo, tt=tt):
                            s.wait_ge(ch["t"].sem, ch["t"].n)
                            s.activation(out=b_qT[:, glo:glo + TT1], in_=p_qk[:, 0:TT1], func=AF.Identity)
                            ch["a"].inc(s.activation(out=b_v[:, glo:glo + TT1], in_=p_v[:],
                                                     func=AF.Identity))

                        @blk.vector
                        def _(v, glo=glo, tt=tt):
                            v.wait_ge(ch["t"].sem, ch["t"].n)
                            ch["v"].inc(v.tensor_copy(b_kT[:, glo:glo + TT1], p_qk[:, TT1:2 * TT1]))

                        @blk.gpsimd
                        def _(g, tt=tt):
                            g.wait_ge(ch["a"].sem, ch["a"].n)
                            g.wait_ge(ch["v"].sem, ch["v"].n)
                            ch["p3"].inc(g.drain())

                    t_sw = b_hq2[:, 0:S]
                    t_t1 = b_hq2[:, S:2 * S]
                    for tensbuf in (b_qT, b_kT):
                        @blk.gpsimd
                        def _(g, tensbuf=tensbuf):
                            for cname_, c in ch.items():
                                if c.n and cname_ != "cc":
                                    g.wait_ge(c.sem, c.n)
                            dq_di.wait_all(g)
                            dq_dg.wait_all(g)
                            dq_dg.issue(g, lambda: g.dma_start(out=t_sw[0:64, :], in_=tensbuf[64:128, :]))
                            dq_dg.issue(g, lambda: g.dma_start(out=t_sw[64:128, :], in_=tensbuf[0:64, :]))

                        @blk.vector
                        def _(v, tensbuf=tensbuf):
                            dq_dg.wait_all(v)
                            v.tensor_tensor(t_t1[:, :], t_sw[:, :], c_ssin[:], ALU.mult)
                            v.drain()
                            v.tensor_tensor(tensbuf[:], tensbuf[:], c_cos[:], ALU.mult)
                            v.drain()
                            v.tensor_tensor(tensbuf[:], tensbuf[:], t_t1[:, :], ALU.add)
                            v.drain()
                            ch["v"].inc(v.drain())

                    if dbg and h == 0:
                        @blk.gpsimd
                        def _(g):
                            g.wait_ge(ch["v"].sem, ch["v"].n)
                            dq_dg.issue(g, lambda: g.dma_start(out=dbg_outs["o_qTr"][:, :], in_=b_qT[:]))
                            dq_dg.issue(g, lambda: g.dma_start(out=dbg_outs["o_kTr"][:, :], in_=b_kT[:]))
                            dq_dg.issue(g, lambda: g.dma_start(out=dbg_outs["o_vtm"][:, :], in_=b_v[:]))

                    for qt in range(NQT):
                        nkb = 4 * qt + 4
                        qsl = b_qT[:, qt * 512:(qt + 1) * 512]
                        t_base = ch["t"].n
                        a_base = ch["a"].n
                        v_base = ch["v"].n
                        n_diag_before = [0] * (nkb + 1)
                        for kb in range(nkb):
                            n_diag_before[kb + 1] = n_diag_before[kb] + (1 if kb >= 4 * qt else 0)

                        @blk.tensor
                        def _(t, qt=qt):
                            if qt == 0:
                                t.wait_ge(ch["v"].sem, ch["v"].n)
                            t.wait_ge(ch["p4"].sem, ch["p4"].n)
                        def emit_pv_pair(p):
                            for bb in range(2):
                                kb = 2 * p + bb
                                diag = kb >= 4 * qt

                                @blk.tensor
                                def _(t, kb=kb, p=p, bb=bb, diag=diag):
                                    if diag:
                                        t.wait_ge(ch["v"].sem, v_base + n_diag_before[kb + 1])
                                    else:
                                        t.wait_ge(ch["a"].sem, a_base + p + 1)
                                    pt = b_pt[:, (p % 2) * 1024 + bb * 512:
                                              (p % 2) * 1024 + (bb + 1) * 512]
                                    ch["t"].inc(t.matmul(p_o[:], b_v[:, kb * 128:(kb + 1) * 128], pt,
                                                         start=(kb == 0), stop=(kb == nkb - 1)))
                                    ch["t"].inc(t.matmul(p_l[:], c_ones_bf[:], pt,
                                                         start=(kb == 0), stop=(kb == nkb - 1)))

                        for p in range(nkb // 2):
                            for bb in range(2):
                                kb = 2 * p + bb

                                @blk.tensor
                                def _(t, kb=kb, p=p, bb=bb, a_base=a_base):
                                    if bb == 0 and p >= 2:
                                        t.wait_ge(ch["a"].sem, a_base + p - 1)
                                    ch["t"].inc(t.matmul(
                                        p_sp[:, (p % 2) * 1024 + bb * 512:(p % 2) * 1024 + (bb + 1) * 512],
                                        b_kT[:, kb * 128:(kb + 1) * 128], qsl,
                                        start=True, stop=True))

                            @blk.scalar
                            def _(s, p=p, t_base=t_base):
                                s.wait_ge(ch["t"].sem,
                                          t_base + (2 if p == 0 else 6 * p - 2))
                                ch["a"].inc(s.activation(
                                    out=b_pt[:, (p % 2) * 1024:(p % 2 + 1) * 1024],
                                    in_=p_sp[:, (p % 2) * 1024:(p % 2 + 1) * 1024],
                                    func=AF.Exp, bias=c_negC[:, 0:1], scale=SCALE))

                            for bb in range(2):
                                kb = 2 * p + bb
                                if kb >= 4 * qt:
                                    @blk.vector
                                    def _(v, kb=kb, p=p, bb=bb, qt=qt, a_base=a_base):
                                        v.wait_ge(ch["a"].sem, a_base + p + 1)
                                        d = kb - 4 * qt
                                        sl = b_pt[:, (p % 2) * 1024 + bb * 512:
                                                  (p % 2) * 1024 + (bb + 1) * 512]
                                        ch["v"].inc(v.tensor_tensor(
                                            sl, sl, c_masks[:, d * 512:(d + 1) * 512], ALU.mult))

                            if p >= 1:
                                emit_pv_pair(p - 1)
                        emit_pv_pair(nkb // 2 - 1)

                        @blk.scalar
                        def _(s, qt=qt):
                            s.wait_ge(ch["t"].sem, ch["t"].n)
                            s.activation(out=b_ob[:], in_=p_o[:], func=AF.Identity)
                            ch["a"].inc(s.drain())

                        @blk.vector
                        def _(v, qt=qt):
                            v.wait_ge(ch["t"].sem, ch["t"].n)
                            v.tensor_copy(b_L[:], p_l[:])
                            v.drain()
                            v.reciprocal(b_rL[:], b_L[:])
                            v.drain()
                            ch["v"].inc(v.drain())

                        @blk.tensor
                        def _(t, qt=qt):
                            t.wait_ge(ch["v"].sem, ch["v"].n)
                            for i in range(4):
                                t.transpose(p_qk[:, i:i + 1],
                                            b_rL[:, i * 128:(i + 1) * 128], c_id_f[0:1, 0:1])
                            ch["t"].inc(t.drain())

                        @blk.vector
                        def _(v, qt=qt):
                            v.wait_ge(ch["t"].sem, ch["t"].n)
                            v.tensor_copy(b_rLc[:], p_qk[:, 0:4])
                            v.drain()
                            ch["v"].inc(v.drain())

                        for i in range(4):
                            @blk.tensor
                            def _(t, i=i, qt=qt):
                                t.wait_ge(ch["a"].sem, ch["a"].n)
                                if i > 0:
                                    t.wait_ge(ch["p1"].sem, ch["p1"].n)
                                t.transpose(p_tb_ap, b_ob[:, i * 128:(i + 1) * 128], c_id_bf[:])
                                ch["t"].inc(t.drain())

                            @blk.vector
                            def _(v, i=i, qt=qt, h=h):
                                v.wait_ge(ch["t"].sem, ch["t"].n)
                                v.wait_ge(ch["v"].sem, ch["v"].n)
                                v.tensor_scalar(sc_a[:, 0:128], p_tb_ap, b_rLc[:, i:i + 1],
                                                None, ALU.mult)
                                v.drain()
                                v.tensor_reduce(sc_g[:, 0:4].unsqueeze(2),
                                                sc_a[:, 0:128].rearrange("p (g r) -> p g r", r=32),
                                                AXX, ALU.max, apply_absolute_value=True)
                                v.drain()
                                v.tensor_scalar(sc_g[:, 0:4], sc_g[:, 0:4], qp_inv[:, 0:1],
                                                1e-12, ALU.mult, ALU.max)
                                v.drain()
                                v.reciprocal(sc_g[:, 4:8], sc_g[:, 0:4])
                                v.drain()
                                v.tensor_tensor(sc_b[:, 0:128].rearrange("p (g r) -> p g r", r=32),
                                                sc_a[:, 0:128].rearrange("p (g r) -> p g r", r=32),
                                                sc_g[:, 4:8].unsqueeze(2).broadcast_to([128, 4, 32]),
                                                ALU.mult)
                                v.drain()
                                v.tensor_copy(sc_b[:, 0:128].bitcast(I32), sc_b[:, 0:128])
                                v.drain()
                                v.tensor_copy(sc_b[:, 0:128], sc_b[:, 0:128].bitcast(I32))
                                v.drain()
                                v.tensor_tensor(stage[:, i * 128:(i + 1) * 128].rearrange(
                                                    "p (g r) -> p g r", r=32),
                                                sc_b[:, 0:128].rearrange("p (g r) -> p g r", r=32),
                                                sc_g[:, 0:4].unsqueeze(2).broadcast_to([128, 4, 32]),
                                                ALU.mult)
                                v.drain()
                                ch["p1"].inc(v.drain())

                        @blk.gpsimd
                        def _(g, qt=qt, h=h):
                            g.wait_ge(ch["p1"].sem, ch["p1"].n)
                            dq_dg.issue(g, lambda: g.dma_start(
                                out=d_qo_in.ap()[h * S + qt * 512:h * S + (qt + 1) * 512, :].rearrange(
                                    "(i q) d -> i q d", q=128).transpose([1, 0, 2]),
                                in_=stage[:, 0:512].rearrange("q (i d) -> q i d", d=128)))
                            dq_dg.wait_all(g)
                            ch["p4"].inc(g.drain())

                @blk.gpsimd
                def _(g):
                    for cname_, c in ch.items():
                        if c.n and cname_ != "cc":
                            g.wait_ge(c.sem, c.n)
                    dq_di.wait_all(g)
                    dq_dg.wait_all(g)
                    ch["cc"].inc(g.collective_compute(
                        "AllGather", ALU.bypass, replica_groups=[list(range(NC))],
                        ins=[d_qo_in.ap().opt()], outs=[d_qo.ap().opt()]))
                    if dbg:
                        g.wait_ge(ch["cc"].sem, ch["cc"].n)
                        dq_dg.issue(g, lambda: g.dma_start(out=dbg_outs["o_qo"][:, :], in_=d_qo[:, :]))
                cc_ag4 = ch["cc"].n
                bar(blk, cc_upto=cc_ag4 - 1)

            stk2.close()
            if upto == "b":
                emit_zero_y()
                return nc, dbg_outs
            WG = min(TPC, 512)
            NTG = TPC // WG
            d_c2t = nc.dram_tensor("d_c2t", [2 * RANK, INTER], BF16)
            d_ffn = nc.dram_tensor("d_ffn", [TPC, D], BF16)
            with ExitStack() as cfstk:
                sbCF = lambda n_, s_, d_: cfstk.enter_context(nc.sbuf_tensor(n_, s_, d_))
                cnegT = sbCF("cnegT", [NBLK, TPC], BF16)
                with ExitStack() as cstk:
                    sbC = lambda n_, s_, d_: cstk.enter_context(nc.sbuf_tensor(n_, s_, d_))
                    c_c1t = sbC("c_c1t", [128, 2 * NFC * RANK], BF16)
                    cq_oT = sbC("cq_oT", [128, NFC * TPC], BF16)
                    cs_b = sbC("cs_b", [128, D], F32)
                    cwo = sbC("cwo", [128, 2 * NFC * 128], BF16)
                    cx2g = sbC("cx2g", [128, (WG // 128) * D], BF16)
                    cxT = sbC("cxT", [128, 2 * NFC * 128], BF16)
                    cr1 = sbC("cr1", [16, 3 * TPC], F32)
                    cc2 = sbC("cc2", [16, 2 * 1024], BF16)
                    cidxf = sbC("cidxf", [128, NTT * NBLK], F32)
                    cidxu = sbC("cidxu", [128, NBLK], U32)
                    cm8 = sbC("cm8", [128, 8], F32)
                    cmi = sbC("cmi", [128, 8], U32)
                    cps_tb = cstk.enter_context(nc.psum_tensor("cps_tb", [128, 128], BF16))
                    cps_t = cstk.enter_context(nc.psum_tensor("cps_t", [128, 512], F32))
                    cps_c = cstk.enter_context(nc.psum_tensor("cps_c", [128, 512], F32))
                    cps_r = cstk.enter_context(nc.psum_tensor("cps_r", [16, 512], F32))
                    cps_i = cstk.enter_context(nc.psum_tensor("cps_i", [128, 512], F32))
                    cps_i2 = cstk.enter_context(nc.psum_tensor("cps_i2", [128, 512], F32))

                    @blk.sync
                    def _(sy):
                        for cname_, c in ch.items():
                            if c.n and cname_ != "cc":
                                sy.wait_ge(c.sem, c.n)
                        dq_di.wait_all(sy)
                        dq_dg.wait_all(sy)
                        dq_di.issue(sy, lambda: sy.dma_start(out=cs_b[0:RANK, :], in_=c1_in[:, :]))

                    for c in range(NFC):
                        @blk.tensor
                        def _(t, c=c):
                            dq_di.wait_all(t)
                            if c >= 1:
                                t.wait_ge(ch["p1"].sem, ch["p1"].n)
                            t.transpose(cps_t[:, 0:RANK], cs_b[0:RANK, c * 128:(c + 1) * 128],
                                        c_id_f[0:RANK, 0:RANK])
                            ch["t"].inc(t.drain())

                        @blk.vector
                        def _(v, c=c):
                            v.wait_ge(ch["t"].sem, ch["t"].n)
                            v.tensor_copy(sc_a[:, c * RANK:(c + 1) * RANK], cps_t[:, 0:RANK])
                            v.drain()
                            ch["p1"].inc(v.drain())

                    @blk.vector
                    def _(v):
                        v.wait_ge(ch["p1"].sem, ch["p1"].n)
                        v.tensor_copy(c_c1t[:, 0:NFC * RANK], sc_a[:, 0:NFC * RANK])
                        v.drain()
                        v.tensor_tensor(c_c1t[:, NFC * RANK:2 * NFC * RANK], sc_a[:, 0:NFC * RANK],
                                        c_c1t[:, 0:NFC * RANK], ALU.subtract)
                        v.drain()
                        ch["v"].inc(v.drain())

                    for rt in range(INTER // 128):
                        @blk.sync
                        def _(sy, rt=rt):
                            sy.wait_ge(ch["v"].sem, ch["v"].n)
                            if rt > 0:
                                sy.wait_ge(ch["p2"].sem, ch["p2"].n)
                            dq_di.issue(sy, lambda: sy.dma_start(out=cs_b[:, 0:RANK],
                                                      in_=c2_in[rt * 128:(rt + 1) * 128, :]))

                        @blk.tensor
                        def _(t, rt=rt):
                            dq_di.wait_all(t)
                            if rt > 0:
                                t.wait_ge(ch["p3"].sem, ch["p3"].n)
                            t.transpose(cps_t[0:RANK, 0:128], cs_b[:, 0:RANK], c_id_f[:])
                            ch["t"].inc(t.drain())

                        @blk.vector
                        def _(v, rt=rt):
                            v.wait_ge(ch["t"].sem, ch["t"].n)
                            v.tensor_copy(sc_a[0:RANK, 0:128], cps_t[0:RANK, 0:128])
                            v.drain()
                            ch["p3"].inc(v.drain())
                            v.tensor_copy(sc_bf[0:RANK, 0:128], sc_a[0:RANK, 0:128])
                            v.drain()
                            v.tensor_tensor(sc_bf[0:RANK, 128:256], sc_a[0:RANK, 0:128],
                                            sc_bf[0:RANK, 0:128], ALU.subtract)
                            v.drain()
                            ch["v"].inc(v.drain())

                        @blk.gpsimd
                        def _(g, rt=rt):
                            g.wait_ge(ch["v"].sem, ch["v"].n)
                            dq_dg.issue(g, lambda: g.dma_start(out=d_c2t.ap()[0:RANK, rt * 128:(rt + 1) * 128],
                                                     in_=sc_bf[0:RANK, 0:128]))
                            dq_dg.issue(g, lambda: g.dma_start(out=d_c2t.ap()[RANK:2 * RANK, rt * 128:(rt + 1) * 128],
                                                     in_=sc_bf[0:RANK, 128:256]))
                            dq_dg.wait_all(g)
                            ch["p2"].inc(g.drain())
                    bar(blk)

                    @blk.gpsimd
                    def _(g):
                        g.wait_ge(ch["cc"].sem, cc_ag4)
                        pid = g.partition_id()
                        dq_dg.issue(g, lambda: g.dma_start(
                            out=d_qostage.ap().rearrange("(ic t) d -> ic t d", t=TPC),
                            in_=d_qo.ap().rearrange("(ic s) d -> ic s d", s=S)[
                                :, bass.ds(pid * TPC, TPC), :]))

                    @blk.sync
                    def _(sy):
                        dq_dg.wait_all(sy)
                        for ic in range(NFC):
                            dq_di.issue(sy, lambda: sy.dma_start(
                                out=cq_oT[:, ic * TPC:(ic + 1) * TPC],
                                in_=d_qostage.ap()[ic * TPC:(ic + 1) * TPC, :],
                                transpose=True))

                    for tg in range(NTG):
                        for oc in range(NOC):
                            @blk.sync
                            def _(sy, tg=tg, oc=oc):
                                if tg == 0 and oc == 0:
                                    for cname_, c in ch.items():
                                        if c.n and cname_ != "cc":
                                            sy.wait_ge(c.sem, c.n)
                                    dq_di.wait_all(sy)
                                    dq_dg.wait_all(sy)
                                if oc >= 2:
                                    sy.wait_ge(ch["p1"].sem, ch["p1"].n - 1)
                                s2, ocl = oc // NOCL, oc % NOCL
                                dq_di.issue(sy, lambda: sy.dma_start(
                                    out=cwo[:, (oc % 2) * NFC * 128:(oc % 2 + 1) * NFC * 128],
                                    in_=d_qwoT.ap()[(s2 * NOCL + ocl) * 128:
                                                    (s2 * NOCL + ocl + 1) * 128, :]))

                            @blk.tensor
                            def _(t, tg=tg, oc=oc):
                                dq_di.wait_all(t)
                                if oc >= 2:
                                    t.wait_ge(ch["a"].sem, ch["a"].n - 1)
                                for c in range(NFC):
                                    t.matmul(cps_c[:, 0:WG] if oc % 2 == 0 else cps_t[:, 0:WG],
                                             cwo[:, (oc % 2) * NFC * 128 + c * 128:
                                                 (oc % 2) * NFC * 128 + (c + 1) * 128],
                                             cq_oT[:, c * TPC + tg * WG:c * TPC + tg * WG + WG],
                                             start=(c == 0), stop=(c == NFC - 1))
                                ch["t"].inc(t.drain())
                                ch["p1"].inc(t.drain())

                            @blk.scalar
                            def _(s, tg=tg, oc=oc):
                                s.wait_ge(ch["t"].sem, ch["t"].n)
                                s.activation(out=stage[:, 0:WG] if oc % 2 == 0 else stage[:, WG:2 * WG],
                                             in_=cps_c[:, 0:WG] if oc % 2 == 0 else cps_t[:, 0:WG],
                                             func=AF.Identity)
                                ch["a"].inc(s.drain())

                            for i in range(WG // 128):
                                @blk.tensor
                                def _(t, oc=oc, i=i):
                                    t.wait_ge(ch["a"].sem, ch["a"].n)
                                    t.wait_ge(ch["p2"].sem, ch["p2"].n)
                                    t.wait_ge(ch["p4"].sem, ch["p4"].n)
                                    t.transpose(cps_tb[:],
                                                (stage[:, 0:WG] if oc % 2 == 0 else
                                                 stage[:, WG:2 * WG])[:, i * 128:(i + 1) * 128],
                                                c_id_bf[:])
                                    ch["t"].inc(t.drain())

                                @blk.vector
                                def _(v, oc=oc, i=i):
                                    v.wait_ge(ch["t"].sem, ch["t"].n)
                                    v.tensor_copy(cx2g[:, i * D + oc * 128:i * D + (oc + 1) * 128],
                                                  cps_tb[:])
                                    v.drain()
                                    ch["p2"].inc(v.drain())

                        @blk.gpsimd
                        def _(g, tg=tg):
                            g.wait_ge(ch["p2"].sem, ch["p2"].n)
                            dq_dg.issue(g, lambda: g.dma_start(
                                out=d_aproj.ap()[tg * WG:(tg + 1) * WG, :].rearrange(
                                    "(i p) d -> i p d", p=128).transpose([1, 0, 2]),
                                in_=cx2g[:].rearrange("p (i d) -> p i d", d=D)))
                            dq_dg.wait_all(g)
                            ch["p4"].inc(g.drain())

                        for j in range(WG // 128):
                            @blk.sync
                            def _(sy, tg=tg, j=j):
                                sy.wait_ge(ch["p4"].sem, ch["p4"].n)
                                sy.wait_ge(ch["v"].sem, ch["v"].n)
                                sy.wait_ge(ch["t"].sem, ch["t"].n)
                                dq_di.issue(sy, lambda: sy.dma_start(
                                    out=sc_a[:], in_=x_in[tg * WG + j * 128:tg * WG + (j + 1) * 128, :]))

                            @blk.vector
                            def _(v, tg=tg, j=j):
                                dq_di.wait_all(v)
                                v.tensor_tensor(cs_b[:], sc_a[:], cx2g[:, j * D:(j + 1) * D], ALU.add)
                                v.drain()
                                v.reduce_sum(sc_st[:, 0:1], cs_b[:], AXX)
                                v.drain()
                                ch["v"].inc(v.drain())

                            @blk.scalar
                            def _(s):
                                s.wait_ge(ch["v"].sem, ch["v"].n)
                                s.activation(out=sc_a[:], in_=cs_b[:], func=AF.Square,
                                             accum_out=sc_st[:, 1:2])
                                ch["a"].inc(s.drain())

                            @blk.vector
                            def _(v):
                                v.wait_ge(ch["a"].sem, ch["a"].n)
                                v.tensor_scalar(sc_st[:, 2:3], sc_st[:, 0:1], float(1.0 / D), None, ALU.mult)
                                v.drain()
                                v.tensor_tensor(sc_st[:, 3:4], sc_st[:, 2:3], sc_st[:, 2:3], ALU.mult)
                                v.drain()
                                v.tensor_scalar(sc_st[:, 4:5], sc_st[:, 1:2], float(1.0 / D), None, ALU.mult)
                                v.drain()
                                v.tensor_tensor(sc_st[:, 4:5], sc_st[:, 4:5], sc_st[:, 3:4], ALU.subtract)
                                v.drain()
                                v.tensor_scalar(sc_st[:, 4:5], sc_st[:, 4:5], 1e-5, None, ALU.add)
                                v.drain()
                                ch["v"].inc(v.drain())

                            @blk.scalar
                            def _(s):
                                s.wait_ge(ch["v"].sem, ch["v"].n)
                                s.activation(out=sc_st[:, 5:6], in_=sc_st[:, 4:5], func=AF.Sqrt)
                                ch["a"].inc(s.drain())

                            @blk.vector
                            def _(v):
                                v.wait_ge(ch["a"].sem, ch["a"].n)
                                v.reciprocal(sc_st[:, 6:7], sc_st[:, 5:6])
                                v.drain()
                                v.tensor_tensor(sc_st[:, 7:8], sc_st[:, 2:3], sc_st[:, 6:7], ALU.mult)
                                v.drain()
                                v.tensor_scalar(sc_st[:, 7:8], sc_st[:, 7:8], -1.0, None, ALU.mult)
                                v.drain()
                                ch["v"].inc(v.drain())

                            @blk.scalar
                            def _(s):
                                s.wait_ge(ch["v"].sem, ch["v"].n)
                                s.activation(out=sc_a[:], in_=cs_b[:], func=AF.Identity,
                                             bias=sc_st[:, 7:8], scale=sc_st[:, 6:7])
                                ch["a"].inc(s.drain())

                            @blk.vector
                            def _(v):
                                v.wait_ge(ch["a"].sem, ch["a"].n)
                                v.tensor_copy(sc_bf[:], sc_a[:])
                                v.drain()
                                v.tensor_tensor(stage[:, 0:D], sc_a[:], sc_bf[:], ALU.subtract)
                                v.drain()
                                ch["v"].inc(v.drain())

                            base_p1 = ch["p1"].n
                            for c in range(2 * NFC):
                                src = sc_bf if c < NFC else stage
                                cc0 = c if c < NFC else c - NFC

                                @blk.tensor
                                def _(t, c=c, src=src, cc0=cc0, base_p1=base_p1):
                                    if c == 0:
                                        t.wait_ge(ch["v"].sem, ch["v"].n)
                                        t.wait_ge(ch["p3"].sem, ch["p3"].n)
                                    t.wait_ge(ch["p1"].sem, base_p1 + c)
                                    t.transpose(cps_tb[:], src[:, cc0 * 128:(cc0 + 1) * 128], c_id_bf[:])
                                    ch["t"].inc(t.drain())

                                @blk.vector
                                def _(v, c=c):
                                    v.wait_ge(ch["t"].sem, ch["t"].n)
                                    v.tensor_copy(cxT[:, c * 128:(c + 1) * 128], cps_tb[:])
                                    v.drain()
                                    ch["p1"].inc(v.drain())

                            @blk.gpsimd
                            def _(g, tg=tg, j=j):
                                g.wait_ge(ch["p1"].sem, ch["p1"].n)
                                for cg in range(0, NFC, 8):
                                    dq_dg.issue(g, lambda cg=cg: g.dma_start(
                                        out=d_xfT.ap().rearrange("(c f) t -> f c t", f=128)[
                                            :, cg:cg + 8, tg * WG + j * 128:tg * WG + (j + 1) * 128],
                                        in_=cxT[:, cg * 128:(cg + 8) * 128].rearrange(
                                            "f (c t) -> f c t", t=128)))
                                dq_dg.wait_all(g)
                                ch["p3"].inc(g.drain())

                            @blk.tensor
                            def _(t, tg=tg, j=j):
                                t.wait_ge(ch["p1"].sem, ch["p1"].n)
                                first = True
                                for c in range(NFC):
                                    for (la, lb) in [(0, 0), (0, 1), (1, 0)]:
                                        t.matmul(cps_r[:, j * 128:(j + 1) * 128],
                                                 c_c1t[:, (la * NFC + c) * RANK:(la * NFC + c + 1) * RANK],
                                                 cxT[:, (lb * NFC + c) * 128:(lb * NFC + c + 1) * 128],
                                                 start=first,
                                                 stop=(c == NFC - 1 and (la, lb) == (1, 0)))
                                        first = False
                                ch["t"].inc(t.drain())

                        @blk.vector
                        def _(v, tg=tg):
                            v.wait_ge(ch["t"].sem, ch["t"].n)
                            v.tensor_copy(cr1[:, tg * WG:(tg + 1) * WG], cps_r[:, 0:WG])
                            v.drain()
                            ch["v"].inc(v.drain())
                    bar(blk)

                    @blk.vector
                    def _(v):
                        v.tensor_copy(cr1.bitcast(BF16)[:, 2 * TPC:3 * TPC], cr1[:, 0:TPC])
                        v.drain()
                        v.tensor_tensor(cr1.bitcast(BF16)[:, 3 * TPC:4 * TPC], cr1[:, 0:TPC],
                                        cr1.bitcast(BF16)[:, 2 * TPC:3 * TPC], ALU.subtract)
                        v.drain()
                        ch["v"].inc(v.drain())
                    bar(blk)

                    for tt in range(NTT):
                        for rt in range(INTER // 512):
                            @blk.sync
                            def _(sy, tt=tt, rt=rt):
                                it = tt * (INTER // 512) + rt
                                if it >= 2:
                                    sy.wait_ge(ch["p1"].sem, ch["p1"].n - 1)
                                bsel = it % 2
                                dq_di.issue(sy, lambda: sy.dma_start(
                                    out=cc2[:, bsel * 1024:bsel * 1024 + 512],
                                    in_=d_c2t.ap()[0:RANK, rt * 512:(rt + 1) * 512]))
                                dq_di.issue(sy, lambda: sy.dma_start(
                                    out=cc2[:, bsel * 1024 + 512:bsel * 1024 + 1024],
                                    in_=d_c2t.ap()[RANK:2 * RANK, rt * 512:(rt + 1) * 512]))

                            @blk.tensor
                            def _(t, tt=tt, rt=rt):
                                dq_di.wait_all(t)
                                it = tt * (INTER // 512) + rt
                                if it >= 2:
                                    t.wait_ge(ch["a"].sem, ch["a"].n - 1)
                                bsel = it % 2
                                combos = [(2, 0), (2, 512), (3, 0)]
                                first = True
                                for (hl, off) in combos:
                                    ins_ = t.matmul(cps_i[:] if bsel == 0 else cps_i2[:],
                                             cr1.bitcast(BF16)[:, hl * TPC + tt * 128:
                                                               hl * TPC + (tt + 1) * 128],
                                             cc2[:, bsel * 1024 + off:bsel * 1024 + off + 512],
                                             start=first, stop=(hl, off) == (3, 0))
                                    first = False
                                ch["t"].inc(ins_)
                                ch["p1"].inc(t.drain())

                            @blk.scalar
                            def _(s, tt=tt, rt=rt):
                                s.wait_ge(ch["t"].sem, ch["t"].n)
                                it = tt * (INTER // 512) + rt
                                if it >= 2:
                                    s.wait_ge(ch["v"].sem, ch["v"].n - 1)
                                bsel = it % 2
                                ch["a"].inc(s.activation(
                                    out=sc_a[:, bsel * 512:(bsel + 1) * 512],
                                    in_=cps_i[:] if bsel == 0 else cps_i2[:], func=AF.Identity))

                            @blk.vector
                            def _(v, tt=tt, rt=rt):
                                v.wait_ge(ch["a"].sem, ch["a"].n)
                                bsel = (tt * (INTER // 512) + rt) % 2
                                for half in range(2):
                                    b = rt * 2 + half
                                    v.max(cm8[:], sc_a[:, bsel * 512 + half * 256:
                                                        bsel * 512 + (half + 1) * 256])
                                    v.drain()
                                    v.max_index(cmi[:], cm8[:],
                                                sc_a[:, bsel * 512 + half * 256:
                                                     bsel * 512 + (half + 1) * 256])
                                    v.drain()
                                    v.tensor_copy(cidxu[:, b:b + 1], cmi[:, 0:1])
                                    v.drain()
                                ch["v"].inc(v.drain())

                        @blk.vector
                        def _(v, tt=tt):
                            v.wait_ge(ch["v"].sem, ch["v"].n)
                            v.tensor_copy(cidxf[:, tt * NBLK:(tt + 1) * NBLK], cidxu[:])
                            v.drain()
                            v.tensor_copy(stage[:, 0:NBLK], cidxf[:, tt * NBLK:(tt + 1) * NBLK])
                            v.drain()
                            ch["v"].inc(v.drain())

                        @blk.tensor
                        def _(t, tt=tt):
                            t.wait_ge(ch["v"].sem, ch["v"].n)
                            t.wait_ge(ch["p2"].sem, ch["p2"].n)
                            t.transpose(cps_tb[0:NBLK, 0:128], stage[:, 0:NBLK], c_id_bf[:])
                            ch["t"].inc(t.drain())

                        @blk.vector
                        def _(v, tt=tt):
                            v.wait_ge(ch["t"].sem, ch["t"].n)
                            v.tensor_scalar(cnegT[:, tt * 128:(tt + 1) * 128], cps_tb[0:NBLK, 0:128],
                                            -1.0, None, ALU.mult)
                            v.drain()
                            ch["p2"].inc(v.drain())
                    bar(blk)

                    if dbg:
                        @blk.gpsimd
                        def _(g):
                            dq_dg.issue(g, lambda: g.dma_start(out=dbg_outs["o_idx"][:, :], in_=cidxf[:, :]))
                            dq_dg.issue(g, lambda: g.dma_start(out=dbg_outs["o_aproj"][:, :], in_=d_aproj[:, :]))
                            dq_dg.issue(g, lambda: g.dma_start(out=dbg_outs["o_xfT"][:, :], in_=d_xfT[:, :]))
                            dq_dg.wait_all(g)
                        bar(blk)

                if upto == "c":
                    emit_zero_y()
                    return nc, dbg_outs

                GRP = min(512, TPC)
                NG = TPC // GRP
                with ExitStack() as fstk:
                    sbF = lambda n_, s_, d_: fstk.enter_context(nc.sbuf_tensor(n_, s_, d_))
                    f_ht = sbF("f_ht", [128, RB * GRP], BF16)
                    f_xT = sbF("f_xT", [128, NFC * GRP], BF16)
                    f_w1 = sbF("f_w1", [128, 2 * NFC * 128], BF16)
                    f_w2 = sbF("f_w2", [128, 2 * 4 * 512], BF16)
                    f_er = stage[0:1, 2 * GRP:6 * GRP]
                    f_ones = stage[0:1, 6 * GRP:7 * GRP]
                    fps_p = fstk.enter_context(nc.psum_tensor("fps_p", [128, GRP], F32))
                    fps_p2 = fstk.enter_context(nc.psum_tensor("fps_p2", [128, GRP], F32))
                    fps_e = fstk.enter_context(nc.psum_tensor("fps_e", [128, GRP], F32))
                    fps_e2 = fstk.enter_context(nc.psum_tensor("fps_e2", [128, GRP], F32))
                    fps_oacc = [fstk.enter_context(nc.psum_tensor(f"fps_oa{i}", [128, 512], F32))
                                for i in range(GRP // 128)]

                    @blk.vector
                    def _(v):
                        v.memset(f_ones, 1.0)
                        v.drain()
                        ch["v"].inc(v.drain())
                    bar(blk)

                    @blk.sync
                    def _(sy):
                        sy.wait_ge(ch["cc"].sem, cc_w1w2)
                    @blk.tensor
                    def _(t):
                        t.wait_ge(ch["cc"].sem, cc_w1w2)

                    ffn_fences = []
                    for fg in range(NG):
                        glo = fg * GRP
                        @blk.sync
                        def _(sy, glo=glo):
                            for cname_, c in ch.items():
                                if c.n and cname_ != "cc":
                                    sy.wait_ge(c.sem, c.n)
                            dq_di.wait_all(sy)
                            dq_dg.wait_all(sy)
                            for cg in range(0, NFC, 8):
                                dq_di.issue(sy, lambda cg=cg: sy.dma_start(
                                    out=f_xT[:].rearrange("p (c t) -> p c t", t=GRP)[:, cg:cg + 8, :],
                                    in_=d_xfT.ap().rearrange("(c f) t -> f c t", f=128)[
                                        :, cg:cg + 8, glo:glo + GRP]))

                        er_toks = {}

                        @blk.gpsimd
                        def _(g, glo=glo):
                            for j0 in range(min(4, NBLK)):
                                er_toks[j0] = dq_dg.issue(g, lambda j0=j0: g.dma_start(
                                    out=f_er[:, (j0 % 4) * GRP:(j0 % 4 + 1) * GRP],
                                    in_=cnegT[j0:j0 + 1, glo:glo + GRP]))
                            ch["g"].inc(g.drain())

                        for rb in range(RB):
                            j = rb // 2
                            hi_half = rb % 2 == 1

                            @blk.sync
                            def _(sy, rb=rb, j=j, glo=glo):
                                if rb < 2:
                                    dq_di.wait_all(sy)
                                    sy.wait_ge(ch["g"].sem, ch["g"].n)
                                else:
                                    sy.wait_ge(ch["p1"].sem, ch["p1"].n - 1)
                                dq_di.issue(sy, lambda: sy.dma_start(
                                    out=f_w1[:, (rb % 2) * NFC * 128:(rb % 2 + 1) * NFC * 128],
                                    in_=d_w1t.ap()[rb * 128:(rb + 1) * 128, :]))

                            @blk.tensor
                            def _(t, rb=rb, j=j, hi_half=hi_half):
                                dq_di.wait_all(t)
                                if rb >= 2:
                                    t.wait_ge(ch["a"].sem, ch["a"].n - 1)
                                    t.wait_ge(ch["v"].sem, ch["v"].n - 1)
                                pp = fps_p if rb % 2 == 0 else fps_p2
                                for c in range(NFC):
                                    t.matmul(pp[:],
                                             f_w1[:, (rb % 2) * NFC * 128 + c * 128:
                                                  (rb % 2) * NFC * 128 + (c + 1) * 128],
                                             f_xT[:, c * GRP:(c + 1) * GRP],
                                             start=(c == 0), stop=(c == NFC - 1))
                                if not hi_half:
                                    t.wait_ge(*er_toks[j])
                                    pe = fps_e if j % 2 == 0 else fps_e2
                                    t.matmul(pe[:], c_iota_lo[:], f_ones, start=True, stop=False)
                                    t.matmul(pe[:], c_ones_row[:],
                                             f_er[:, (j % 4) * GRP:(j % 4 + 1) * GRP],
                                             start=False, stop=True)
                                    ch["p3"].inc(t.drain())
                                ch["t"].inc(t.drain())

                            if hi_half and j + 4 < NBLK:
                                @blk.gpsimd
                                def _(g, j=j, glo=glo):
                                    g.wait_ge(ch["p3"].sem, ch["p3"].n)
                                    er_toks[j + 4] = dq_dg.issue(g, lambda: g.dma_start(
                                        out=f_er[:, ((j + 4) % 4) * GRP:((j + 4) % 4 + 1) * GRP],
                                        in_=cnegT[j + 4:j + 5, glo:glo + GRP]))

                            @blk.scalar
                            def _(s, rb=rb):
                                s.wait_ge(ch["t"].sem, ch["t"].n)
                                pp = fps_p if rb % 2 == 0 else fps_p2
                                ch["a"].inc(s.activation(
                                    out=stage[:, 0:GRP] if rb % 2 == 0 else stage[:, GRP:2 * GRP],
                                    in_=pp[:], func=AF.Relu))

                            @blk.vector
                            def _(v, rb=rb, j=j, hi_half=hi_half):
                                v.wait_ge(ch["a"].sem, ch["a"].n)
                                pe = fps_e if j % 2 == 0 else fps_e2
                                ins_ = v.scalar_tensor_tensor(
                                    f_ht[:, rb * GRP:(rb + 1) * GRP], pe[:],
                                    -128.0 if hi_half else 0.0,
                                    stage[:, 0:GRP] if rb % 2 == 0 else stage[:, GRP:2 * GRP],
                                    ALU.is_equal, ALU.mult)
                                ch["v"].inc(ins_)
                                ch["p1"].inc(v.drain())

                        NWB = 4 if RB % 4 == 0 else 2
                        for fgrp in range(D // 512):
                            for rbp in range(RB // NWB):
                                @blk.sync
                                def _(sy, fgrp=fgrp, rbp=rbp):
                                    if fgrp == 0 and rbp == 0:
                                        sy.wait_ge(ch["v"].sem, ch["v"].n)
                                    it = fgrp * (RB // NWB) + rbp
                                    if it >= 2:
                                        sy.wait_ge(ch["p2"].sem, ch["p2"].n - 1)
                                    dq_di.issue(sy, lambda: sy.dma_start(
                                        out=f_w2[:, (rbp % 2) * NWB * 512:
                                                 (rbp % 2 + 1) * NWB * 512].rearrange(
                                            "p (b t) -> p b t", t=512),
                                        in_=d_w2.ap()[rbp * NWB * 128:(rbp + 1) * NWB * 128,
                                                      fgrp * 512:(fgrp + 1) * 512].rearrange(
                                            "(b f) t -> f b t", f=128)))

                                @blk.tensor
                                def _(t, fgrp=fgrp, rbp=rbp, glo=glo):
                                    dq_di.wait_all(t)
                                    if rbp == 0 and fgrp > 0:
                                        t.wait_ge(ch["a"].sem, ch["a"].n)
                                    for bb in range(NWB):
                                        rb = rbp * NWB + bb
                                        for i in range(GRP // 128):
                                            t.matmul(fps_oacc[i][:],
                                                     f_ht[:, rb * GRP + i * 128:rb * GRP + (i + 1) * 128],
                                                     f_w2[:, (rbp % 2) * NWB * 512 + bb * 512:
                                                          (rbp % 2) * NWB * 512 + (bb + 1) * 512],
                                                     start=(rb == 0), stop=(rb == RB - 1))
                                    ch["t"].inc(t.drain())
                                    ch["p2"].inc(t.drain())

                            for i in range(GRP // 128):
                                @blk.scalar
                                def _(s, fgrp=fgrp, i=i):
                                    if i == 0:
                                        s.wait_ge(ch["t"].sem, ch["t"].n)
                                    if len(ffn_fences) >= 4:
                                        s.wait_ge(*ffn_fences[-4])
                                    ch["a"].inc(s.activation(
                                        out=sc_bf[:, (i % 4) * 512:(i % 4 + 1) * 512],
                                        in_=fps_oacc[i][:], func=AF.Identity))

                                @blk.gpsimd
                                def _(g, fgrp=fgrp, i=i, glo=glo):
                                    g.wait_ge(ch["a"].sem, ch["a"].n)
                                    tok = dq_dg.issue(g, lambda: g.dma_start(
                                        out=d_ffn.ap()[glo + i * 128:glo + (i + 1) * 128,
                                                       fgrp * 512:(fgrp + 1) * 512],
                                        in_=sc_bf[:, (i % 4) * 512:(i % 4 + 1) * 512]))
                                    ch["p4"].inc(g.drain())
                                    ffn_fences.append(tok)
                    bar(blk)

                    if dbg:
                        @blk.gpsimd
                        def _(g):
                            for cname_, c in ch.items():
                                if c.n and cname_ != "cc":
                                    g.wait_ge(c.sem, c.n)
                            dq_di.wait_all(g)
                            dq_dg.wait_all(g)
                            dq_dg.issue(g, lambda: g.dma_start(
                                out=dbg_outs["o_ffn"][:, :], in_=d_ffn.ap()[:, :]))
                            dq_dg.wait_all(g)
                            ch["g"].inc(g.drain())

                    for tt in range(NTT):
                        @blk.sync
                        def _(sy, tt=tt):
                            sy.wait_ge(ch["v"].sem, ch["v"].n)
                            dq_dg.wait_all(sy)
                            dq_di.issue(sy, lambda: sy.dma_start(out=sc_a[:], in_=x_in[tt * 128:(tt + 1) * 128, :]))
                            dq_di.issue(sy, lambda: sy.dma_start(out=sc_bf[:], in_=d_aproj.ap()[
                                tt * 128:(tt + 1) * 128, :]))
                            dq_di.issue(sy, lambda: sy.dma_start(out=stage[:, 0:D], in_=d_ffn.ap()[
                                tt * 128:(tt + 1) * 128, :]))

                        @blk.vector
                        def _(v, tt=tt):
                            dq_di.wait_all(v)
                            v.tensor_tensor(sc_a[:], sc_a[:], sc_bf[:], ALU.add)
                            v.drain()
                            v.tensor_tensor(sc_a[:], sc_a[:], stage[:, 0:D], ALU.add)
                            v.drain()
                            ch["v"].inc(v.drain())

                        @blk.gpsimd
                        def _(g, tt=tt):
                            g.wait_ge(ch["v"].sem, ch["v"].n)
                            dq_dg.issue(g, lambda: g.dma_start(out=y_out[tt * 128:(tt + 1) * 128, :], in_=sc_a[:]))

                    @blk.gpsimd
                    def _(g):
                        dq_dg.wait_all(g)
                    bar(blk)
        return nc, dbg_outs


def shard_inputs(cfg, inputs):
    S, D, H, HD, INTER, NC = cfg["S"], cfg["D"], cfg["H"], cfg["HD"], cfg["INTER"], cfg["NC"]
    TPC, HPC, OPC = S // NC, H // NC, D // NC
    RPC = INTER // NC
    x = np.ascontiguousarray(inputs["L0_attn_hidden_states"].reshape(S, D).astype(np.float32))
    maps = []
    for c in range(NC):
        m = {
            "x": np.ascontiguousarray(x[c * TPC:(c + 1) * TPC]),
            "wq": np.ascontiguousarray(inputs["L0_attn_Wq"][c * HPC * HD:(c + 1) * HPC * HD]),
            "wk": np.ascontiguousarray(inputs["L0_attn_Wk"][c * HPC * HD:(c + 1) * HPC * HD]),
            "wv": np.ascontiguousarray(inputs["L0_attn_Wv"][c * HPC * HD:(c + 1) * HPC * HD]),
            "wo": np.ascontiguousarray(inputs["L0_attn_Wo"][c * OPC:(c + 1) * OPC]),
            "c1": np.ascontiguousarray(inputs["L0_ffn_c1_w"]),
            "c2": np.ascontiguousarray(inputs["L0_ffn_c2_w"]),
            "w1": np.ascontiguousarray(inputs["L0_ffn_w1"][c * RPC:(c + 1) * RPC]),
            "w2": np.ascontiguousarray(inputs["L0_ffn_w2t"][c * RPC:(c + 1) * RPC]),
        }
        f0 = c * HPC * HD
        if f0 < cfg["P4"]:
            mq = 7.0
        elif f0 < cfg["P4"] + cfg["P6"]:
            mq = 31.0
        else:
            mq = 127.0
        qp = np.zeros((128, 2), np.float32)
        qp[:, 0] = 1.0 / mq
        qp[:, 1] = mq
        m["qparam"] = qp
        maps.append(m)
    return maps


def assemble(cfg, results):
    S, D = cfg["S"], cfg["D"]
    return np.concatenate([r["y"] for r in results], axis=0).reshape(1, S, D)


_CACHE = {}


def kernel(L0_attn_hidden_states, L0_attn_Wq, L0_attn_Wk, L0_attn_Wv, L0_attn_Wo,
           L0_ffn_c1_w, L0_ffn_c2_w, L0_ffn_w1, L0_ffn_w2t):
    cfg = dict(CFG_FULL)
    cfg["upto"] = "all"
    inputs = {
        "L0_attn_hidden_states": np.asarray(L0_attn_hidden_states),
        "L0_attn_Wq": np.asarray(L0_attn_Wq),
        "L0_attn_Wk": np.asarray(L0_attn_Wk),
        "L0_attn_Wv": np.asarray(L0_attn_Wv),
        "L0_attn_Wo": np.asarray(L0_attn_Wo),
        "L0_ffn_c1_w": np.asarray(L0_ffn_c1_w),
        "L0_ffn_c2_w": np.asarray(L0_ffn_c2_w),
        "L0_ffn_w1": np.asarray(L0_ffn_w1),
        "L0_ffn_w2t": np.asarray(L0_ffn_w2t),
    }
    if "nc" not in _CACHE:
        _CACHE["nc"] = build_nc(cfg)
    nc, _dbg = _CACHE["nc"]
    maps = shard_inputs(cfg, inputs)
    res = run_bass_kernel_spmd(nc, maps, core_ids=list(range(cfg["NC"])))
    return assemble(cfg, res.results)
